# version 12
# baseline (speedup 1.0000x reference)
import dataclasses
import math
import numpy as np
from contextlib import ExitStack

import concourse.bass as bass
import concourse.mybir as mybir
from concourse.bass_utils import run_bass_kernel_spmd

F32 = mybir.dt.float32
BF16 = mybir.dt.bfloat16
I32 = mybir.dt.int32
AF = mybir.ActivationFunctionType
ALU = mybir.AluOpType
AX = mybir.AxisListType.X

T = 2048
C = 1024
NH = 16
HD = 32
VD = 64
HID = 3072
SCL = 1.0 / 127.0
LAM0 = 0.2
EPS = 1e-5
ISQ = 1.0 / math.sqrt(32.0)
NT = T // 128
NM = C // 128
NHT = HID // 128


def build_nc(debug=(), stop=None):
    debug = set(debug)
    nc = bass.Bass("TRN2", target_bir_lowering=False)

    x_d = nc.dram_tensor("x", [T, C], F32, kind="ExternalInput").ap()
    wattn_d = nc.dram_tensor("wattn", [C, 3 * C], F32, kind="ExternalInput").ap()
    wproj_d = nc.dram_tensor("wproj", [C, C], F32, kind="ExternalInput").ap()
    lam_d = nc.dram_tensor("lambdas", [1, 4 * HD], F32, kind="ExternalInput").ap()
    wg_d = nc.dram_tensor("wg", [HID, C], I32, kind="ExternalInput").ap()
    wu_d = nc.dram_tensor("wu", [HID, C], I32, kind="ExternalInput").ap()
    wd_d = nc.dram_tensor("wd", [C, HID], I32, kind="ExternalInput").ap()
    ident_d = nc.dram_tensor("ident", [128, 128], F32, kind="ExternalInput").ap()
    swap_d = nc.dram_tensor("swapm", [128, 128], F32, kind="ExternalInput").ap()
    tri_d = nc.dram_tensor("tri", [128, 128], F32, kind="ExternalInput").ap()
    mkb_d = nc.dram_tensor("maskbig", [128, 128], F32, kind="ExternalInput").ap()
    cos_d = nc.dram_tensor("cosr", [128, T], F32, kind="ExternalInput").ap()
    sinm_d = nc.dram_tensor("sinr", [128, T], F32, kind="ExternalInput").ap()
    onesr_d = nc.dram_tensor("onesr", [1, 128], F32, kind="ExternalInput").ap()
    ccols_d = nc.dram_tensor("ccols", [128, 4], F32, kind="ExternalInput").ap()
    out_d = nc.dram_tensor("out", [T, C], F32, kind="ExternalOutput").ap()
    spill_d = nc.dram_tensor("spill", [T, C], BF16, kind="Internal").ap()
    hact_d = nc.dram_tensor("hacts", [HID, T], BF16, kind="Internal").ap()
    wdt_d = nc.dram_tensor("wdts", [HID, C], BF16, kind="Internal").ap()
    dnt_d = nc.dram_tensor("dnts", [C, T], BF16, kind="Internal").ap()
    dbg_d = {}
    for name, shape in (("xnT", [128, NM * T]), ("qT", [128, NM * T]), ("kT", [128, NM * T]),
                        ("v", [T, NH * 65]), ("aev0", [2 * 65, T]), ("dnT", [128, NM * T]),
                        ("hact", [128, NHT * 1024])):
        if name in debug:
            dbg_d[name] = nc.dram_tensor("dbg_" + name, shape, F32, kind="ExternalOutput").ap()

    Q = {k: [] for k in ("tensor", "vector", "scalar", "gpsimd", "sync")}
    CNT = {}
    MARKS = {}

    def mark(name):
        MARKS[name] = {k: len(v) for k, v in Q.items()}

    with ExitStack() as top:
        et = top.enter_context
        sm = {n: et(nc.semaphore(n)) for n in ("qx", "qw", "qs", "t", "v", "s", "g", "o")}

        def bump(sem, by=1):
            CNT[sem] = CNT.get(sem, 0) + by
            return CNT[sem]

        def now(sem):
            return CNT.get(sem, 0)

        def WVAL(eng, sem, val):
            if val and val > 0:
                Q[eng].append(lambda e, h=sm[sem], v=val: e.wait_ge(h, v))

        def dma(eng, out, in_, sem):
            tgt = bump(sem, 16)
            Q[eng].append(lambda e, o=out, i=in_, h=sm[sem]: e.dma_start(out=o, in_=i).then_inc(h, 16))
            return tgt

        _ldw_cache = {}

        def _swap_ldw(b):
            ins = b.ins
            blk = _ldw_cache.get("blk")
            if blk is not None:
                insts = blk.instructions
                k = len(insts) - 1
                if insts[k].name == ins.name:
                    insts[k] = dataclasses.replace(ins, ldweights=False)
                    return
            for fn_ in nc.m.functions:
                for bb in fn_.blocks:
                    insts = bb.instructions
                    k = len(insts) - 1
                    if k >= 0 and insts[k].name == ins.name:
                        _ldw_cache["blk"] = bb
                        insts[k] = dataclasses.replace(ins, ldweights=False)
                        return
            raise RuntimeError("ldw swap failed")

        def op(eng, fn, inc=None, reuse=False):
            if not reuse:
                if inc is None:
                    Q[eng].append(fn)
                    return None
                tgt = bump(inc)
                Q[eng].append(lambda e, f=fn, h=sm[inc]: f(e).then_inc(h, 1))
                return tgt
            tgt = bump(inc) if inc else None
            h = sm[inc] if inc else None

            def wrapped(e, f=fn, h=h):
                b = f(e)
                if h is not None:
                    b.then_inc(h, 1)
                _swap_ldw(b)
            Q[eng].append(wrapped)
            return tgt

        def vtiny(fn):
            t = op("vector", fn, inc="v")
            WVAL("vector", "v", t)
            return t

        def stiny(fn):
            t = op("scalar", fn, inc="s")
            WVAL("scalar", "s", t)
            return t

        ident_bf = et(nc.sbuf_tensor("ident_bf", [128, 128], BF16))
        swap_bf = et(nc.sbuf_tensor("swap_bf", [128, 128], BF16))
        tri_bf = et(nc.sbuf_tensor("tri_bf", [128, 128], BF16))
        mkb = et(nc.sbuf_tensor("mkb", [128, 128], F32))
        onesr_f = et(nc.sbuf_tensor("onesr_f", [1, 128], F32))
        ccols = et(nc.sbuf_tensor("ccols_sb", [128, 4], F32))
        lam_col = et(nc.sbuf_tensor("lam_col", [128, 1], F32))
        negC = et(nc.sbuf_tensor("negC", [128, 1], F32))
        small = et(nc.sbuf_tensor("small", [128, 8], F32))
        lamv = et(nc.sbuf_tensor("lamv", [1, 8], F32))
        lam_in = et(nc.sbuf_tensor("lam_in", [1, 4 * HD], F32))
        lprod = et(nc.sbuf_tensor("lprod", [1, 2 * HD], F32))
        eps_col = ccols[:, 0:1]
        ln08_col = ccols[:, 1:2]

        def tap(pairs, ncols, stage):
            for src_ap, dram_ap in pairs:
                np_ = src_ap.shape[0]
                vv = op("vector", lambda e, sp=src_ap, st=stage, n=ncols, np_=np_: e.tensor_copy(st[0:np_, 0:n], sp), inc="v")
                WVAL("sync", "v", vv)
                dq = dma("sync", dram_ap, stage[0:np_, 0:ncols], "qs")
                WVAL("vector", "qs", dq)

        dma("sync", onesr_f[:], onesr_d, "qx")
        dma("sync", ccols[:], ccols_d, "qx")
        qx_l = dma("sync", lam_in[:], lam_d, "qx")
        dma("gpsimd", ident_bf[:], ident_d, "qw")
        dma("gpsimd", swap_bf[:], swap_d, "qw")
        dma("gpsimd", tri_bf[:], tri_d, "qw")
        qw_consts = dma("gpsimd", mkb[:], mkb_d, "qw")

        WVAL("vector", "qx", qx_l)
        vtiny(lambda e: e.tensor_mul(lprod[:, 0:HD], lam_in[:, 0:HD], lam_in[:, HD:2 * HD]))
        vtiny(lambda e: e.tensor_mul(lprod[:, HD:2 * HD], lam_in[:, 2 * HD:3 * HD], lam_in[:, 3 * HD:4 * HD]))
        v1 = op("vector", lambda e: e.reduce_sum(lamv[0:1, 0:2], lprod[:].rearrange("p (a b) -> p a b", a=2), axis=AX), inc="v")
        WVAL("scalar", "v", v1)
        s1 = op("scalar", lambda e: e.activation(lamv[0:1, 2:4], lamv[0:1, 0:2], AF.Exp), inc="s")
        WVAL("vector", "s", s1)
        vtiny(lambda e: e.tensor_sub(lamv[0:1, 0:1], lamv[0:1, 2:3], lamv[0:1, 3:4]))
        v_lam = op("vector", lambda e: e.tensor_scalar_add(lamv[0:1, 0:1], lamv[0:1, 0:1], LAM0), inc="v")

        scope_x = ExitStack()
        xnT = scope_x.enter_context(nc.sbuf_tensor("xnT", [128, NM * T], BF16))
        with ExitStack() as p0:
            ep = p0.enter_context
            xst = [ep(nc.sbuf_tensor(f"xst{i}", [128, C], F32)) for i in range(2)]
            sqj = ep(nc.sbuf_tensor("sqj", [128, C], F32))
            xnbp = [ep(nc.sbuf_tensor(f"xnbp{i}", [128, C], BF16)) for i in range(2)]
            ps_tr = [ep(nc.psum_tensor(f"ps_tr{i}", [128, NM * 128], F32)) for i in range(2)]
            ps_misc = ep(nc.psum_tensor("ps_misc", [128, 1], F32))

            WVAL("tensor", "qx", qx_l)
            WVAL("tensor", "v", v_lam)
            t_lam = op("tensor", lambda e: e.matmul(ps_misc[:], onesr_f[:], lamv[0:1, 0:1], start=True, stop=True), inc="t")
            WVAL("vector", "t", t_lam)
            op("vector", lambda e: e.tensor_copy(lam_col[:], ps_misc[:]), inc="v")

            xn_done = [0] * NT
            ev_done = [0] * NT
            rs_done = [0] * NT

            def ln_stats(st):
                par = st % 2
                c0 = 4 * par
                if st >= 2:
                    WVAL("sync", "v", xn_done[st - 2])
                ld = dma("sync", xst[par][:], x_d[st * 128:(st + 1) * 128, :], "qx")
                WVAL("vector", "qx", ld)
                vtiny(lambda e, par=par, c0=c0: e.reduce_sum(small[:, c0:c0 + 1], xst[par][:], axis=AX))
                v_mn = op("vector", lambda e, c0=c0: e.tensor_scalar_mul(small[:, c0 + 1:c0 + 2], small[:, c0:c0 + 1], -1.0 / C), inc="v")
                WVAL("scalar", "v", v_mn)
                op("scalar", lambda e, par=par, c0=c0: e.activation(
                    sqj[:], xst[par][:], AF.Square, bias=small[:, c0 + 1:c0 + 2], accum_out=small[:, c0 + 2:c0 + 3]))
                stiny(lambda e, c0=c0: e.activation(small[:, c0 + 3:c0 + 4], small[:, c0 + 2:c0 + 3], AF.Ln, bias=eps_col, scale=1.0 / C))
                rs_done[st] = op("scalar", lambda e, c0=c0: e.activation(small[:, c0 + 3:c0 + 4], small[:, c0 + 3:c0 + 4], AF.Exp, scale=-0.5), inc="s")

            def ln_finish(st):
                par = st % 2
                c0 = 4 * par
                WVAL("vector", "s", rs_done[st])
                xn_done[st] = op("vector", lambda e, par=par, c0=c0: e.tensor_scalar(
                    xnbp[par][:], xst[par][:], small[:, c0 + 1:c0 + 2], small[:, c0 + 3:c0 + 4],
                    ALU.add, ALU.mult), inc="v")
                WVAL("tensor", "v", xn_done[st])
                WVAL("tensor", "qw", qw_consts)
                if st >= 2:
                    WVAL("tensor", "v", ev_done[st - 2])
                t_tr = None
                for m in range(NM):
                    t_tr = op("tensor", lambda e, m=m, par=par: e.matmul(
                        ps_tr[par][:, m * 128:(m + 1) * 128], xnbp[par][:, m * 128:(m + 1) * 128], ident_bf[:],
                        start=True, stop=True),
                        inc="t" if m == NM - 1 else None)
                WVAL("vector", "t", t_tr)
                ev_done[st] = op("vector", lambda e, st=st, par=par: e.tensor_copy(
                    xnT[:].rearrange("p (m t) -> p m t", m=NM)[:, :, st * 128:(st + 1) * 128],
                    ps_tr[par][:].rearrange("p (m t) -> p m t", m=NM)), inc="v")

            ln_stats(0)
            for st in range(1, NT):
                ln_stats(st)
                ln_finish(st - 1)
            ln_finish(NT - 1)
            v_xnT = ev_done[NT - 1]
            p0_t, p0_s = now("t"), now("s")
            mark("p0")

        with ExitStack() as pA:
            ea = pA.enter_context
            qT = ea(nc.sbuf_tensor("qT", [128, NM * T], BF16))
            kT = ea(nc.sbuf_tensor("kT", [128, NM * T], BF16))
            vaug = [ea(nc.sbuf_tensor(f"vaug{i}", [128, NH * 65], BF16)) for i in range(NT)]

            g_ones = 0
            for st in range(NT):
                g_ones = op("gpsimd", lambda e, st=st: e.memset(
                    vaug[st][:].rearrange("p (h c) -> p h c", h=NH)[:, :, 64:65], 1.0),
                    inc="g" if st == NT - 1 else None)

            with ExitStack() as p1:
                e1 = p1.enter_context
                cosr = e1(nc.sbuf_tensor("cosr_sb", [128, T], BF16))
                sinr = e1(nc.sbuf_tensor("sinr_sb", [128, T], BF16))
                wab = [e1(nc.sbuf_tensor(f"wab{i}", [128, C], BF16)) for i in range(NM)]
                wab2 = [e1(nc.sbuf_tensor(f"wab2{i}", [128, C], BF16)) for i in range(NM)]
                tmp1 = e1(nc.sbuf_tensor("tmp1", [128, 512], F32))
                tmp2 = e1(nc.sbuf_tensor("tmp2", [128, 512], F32))
                macc = e1(nc.sbuf_tensor("macc", [128, 2], F32))
                mbf = e1(nc.sbuf_tensor("mbf", [128, 2], BF16))

                WVAL("gpsimd", "v", v_xnT)
                WVAL("gpsimd", "t", p0_t)
                WVAL("gpsimd", "s", p0_s)
                qw_cs = dma("gpsimd", cosr[:], cos_d, "qw")
                qw_cs = dma("gpsimd", sinr[:], sinm_d, "qw")
                ps_w = [e1(nc.psum_tensor(f"psw{i}", [128, 512], F32)) for i in range(2)]

                class Rope:

                    def __init__(self):
                        self.items = []
                        for w in (0, 1):
                            for i in range(4 * NM):
                                self.items.append((w, i))
                                if i % 4 == 3:
                                    self.items.append((2 + w, i // 4))
                        self.i = 0
                        self.rev = {}
                        self.ready = [0, 0]
                        self.first = True

                    def step(self):
                        if self.i >= len(self.items):
                            return
                        which, idx = self.items[self.i]
                        if self.ready[which % 2] == 0:
                            return
                        if which >= 2:
                            ft, j = idx, which - 2
                            srcq = qT if j == 0 else kT
                            vtiny(lambda e, ft=ft, srcq=srcq: e.reduce_max(
                                small[:, 4:5], srcq[:, ft * T:(ft + 1) * T], axis=AX, apply_absolute_value=True))
                            if ft == 0:
                                vtiny(lambda e, j=j: e.tensor_copy(macc[:, j:j + 1], small[:, 4:5]))
                            else:
                                vtiny(lambda e, j=j: e.tensor_max(macc[:, j:j + 1], macc[:, j:j + 1], small[:, 4:5]))
                            self.i += 1
                            return
                        dst = qT if which == 0 else kT
                        gi = sum(1 for it in self.items[:self.i] if it[0] < 2)
                        self.i += 1
                        ft, c4, par = idx // 4, idx % 4, gi % 2
                        if self.first:
                            WVAL("tensor", "v", v_xnT)
                            self.first = False
                        if idx == 0:
                            WVAL("tensor", "v", self.ready[which])
                        if gi >= 2:
                            WVAL("tensor", "v", self.rev[gi - 2])
                        tsw = op("tensor", lambda e, ft=ft, c4=c4, p=par, dst=dst: e.matmul(
                            ps_w[p][:], swap_bf[:], dst[:, ft * T + c4 * 512: ft * T + (c4 + 1) * 512],
                            start=True, stop=True), inc="t")
                        WVAL("vector", "t", tsw)
                        if gi == 0:
                            WVAL("vector", "qw", qw_cs)
                        op("vector", lambda e, ft=ft, c4=c4, dst=dst: e.tensor_mul(
                            tmp1[:], dst[:, ft * T + c4 * 512: ft * T + (c4 + 1) * 512],
                            cosr[:, c4 * 512:(c4 + 1) * 512]))
                        op("vector", lambda e, c4=c4, p=par: e.tensor_mul(
                            tmp2[:], ps_w[p][:], sinr[:, c4 * 512:(c4 + 1) * 512]))
                        self.rev[gi] = op("vector", lambda e, ft=ft, c4=c4, dst=dst: e.tensor_add(
                            dst[:, ft * T + c4 * 512: ft * T + (c4 + 1) * 512], tmp1[:], tmp2[:]), inc="v")

                    def drain(self):
                        while self.i < len(self.items):
                            before = self.i
                            self.step()
                            if self.i == before:
                                break
                        return self.rev.get(8 * NM - 1, 0)

                rope = Rope()

                def qk_phase(base, dst, wab_free_t, prev_ev_v, which):
                    wb = wab if which % 2 == 0 else wab2
                    qw_t = None
                    for mt in range(NM):
                        WVAL("gpsimd", "t", wab_free_t)
                        qw_t = dma("gpsimd", wb[mt][:], wattn_d[mt * 128:(mt + 1) * 128, base:base + C], "qw")
                    WVAL("tensor", "qw", qw_t)
                    WVAL("tensor", "v", v_xnT)
                    WVAL("tensor", "v", prev_ev_v)
                    last_t = None
                    ev = {}
                    with ExitStack() as pq_s:
                        ps_q = [pq_s.enter_context(nc.psum_tensor(f"psq{base}{i}", [128, 1024], F32)) for i in range(3)]
                        for g in range(2 * NM):
                            ft, half, par = g // 2, g % 2, g % 3
                            if g >= 3:
                                WVAL("tensor", "v", ev[g - 3])
                            for mt in range(NM):
                                for c2 in range(2):
                                    last_t = op("tensor", lambda e, mt=mt, c2=c2, ft=ft, half=half, p=par, wb=wb: e.matmul(
                                        ps_q[p][:, c2 * 512:(c2 + 1) * 512],
                                        wb[mt][:, ft * 128:(ft + 1) * 128],
                                        xnT[:, mt * T + half * 1024 + c2 * 512: mt * T + half * 1024 + (c2 + 1) * 512],
                                        start=(mt == 0), stop=(mt == NM - 1)),
                                        inc="t" if (mt == NM - 1 and c2 == 1) else None, reuse=(c2 == 1))
                            WVAL("vector", "t", last_t)
                            ev[g] = op("vector", lambda e, ft=ft, half=half, p=par: e.tensor_copy(
                                dst[:, ft * T + half * 1024: ft * T + (half + 1) * 1024], ps_q[p][:]), inc="v")
                            if which >= 1:
                                rope.step()
                    return last_t, ev[2 * NM - 1]

                wq_t, q_ev = qk_phase(0, qT, 0, v_xnT, 0)
                rope.ready[0] = q_ev
                mark("q")
                wk_t, k_ev = qk_phase(C, kT, 0, q_ev, 1)
                rope.ready[1] = k_ev
                mark("k")

                def v_phase():
                    wv_t = None
                    for mt in range(NM):
                        WVAL("gpsimd", "t", wq_t)
                        wv_t = dma("gpsimd", wab[mt][:], wattn_d[mt * 128:(mt + 1) * 128, 2 * C:3 * C], "qw")
                    WVAL("tensor", "qw", wv_t)
                    WVAL("tensor", "v", k_ev)
                    vev = {}
                    with ExitStack() as pv_s:
                        ps_v = [pv_s.enter_context(nc.psum_tensor(f"psv{i}", [128, 1024], F32)) for i in range(3)]
                        for st in range(NT):
                            par = st % 3
                            if st >= 3:
                                WVAL("tensor", "v", vev[st - 3])
                            lt = None
                            for mt in range(NM):
                                for c2 in range(2):
                                    lt = op("tensor", lambda e, mt=mt, c2=c2, st=st, p=par: e.matmul(
                                        ps_v[p][:, c2 * 512:(c2 + 1) * 512],
                                        xnT[:, mt * T + st * 128:mt * T + (st + 1) * 128],
                                        wab[mt][:, c2 * 512:(c2 + 1) * 512],
                                        start=(mt == 0), stop=(mt == NM - 1)),
                                        inc="t" if (mt == NM - 1 and c2 == 1) else None, reuse=(c2 == 1))
                            WVAL("vector", "t", lt)
                            vev[st] = op("vector", lambda e, st=st, p=par: e.tensor_copy(
                                vaug[st][:].rearrange("p (h c) -> p h c", h=NH)[:, :, 0:64],
                                ps_v[p][:].rearrange("p (h c) -> p h c", h=NH)), inc="v")
                            rope.step()
                            rope.step()
                    return vev[NT - 1]

                v_v_done = v_phase()
                mark("v")
                q_rope = k_rope = rope.drain()
                mark("rope")

                def bound_phase():
                    v_mx = op("vector", lambda e: e.tensor_copy(mbf[:], macc[:]), inc="v")
                    if True:
                        ps_m = ps_w[0][0:1, 0:256]
                        ps_c = ps_w[1][:, 0:1]
                        WVAL("tensor", "v", v_mx)
                        op("tensor", lambda e: e.matmul(ps_m[0:1, 0:128], mbf[:, 0:1], ident_bf[:], start=True, stop=True))
                        t_mt = op("tensor", lambda e: e.matmul(ps_m[0:1, 128:256], mbf[:, 1:2], ident_bf[:], start=True, stop=True), inc="t")
                        bound_done = {}
                        WVAL("vector", "t", t_mt)
                        vtiny(lambda e: e.reduce_max(lamv[0:1, 2:4], ps_m[0:1, :].rearrange("p (a b) -> p a b", a=2), axis=AX))
                        vtiny(lambda e: e.tensor_mul(lamv[0:1, 1:2], lamv[0:1, 2:3], lamv[0:1, 3:4]))
                        v_cb = op("vector", lambda e: e.tensor_scalar_mul(lamv[0:1, 1:2], lamv[0:1, 1:2], 32.0 * ISQ), inc="v")
                        WVAL("tensor", "v", v_cb)
                        t_cb = op("tensor", lambda e: e.matmul(ps_c[:], onesr_f[:], lamv[0:1, 1:2], start=True, stop=True), inc="t")
                        WVAL("scalar", "t", t_cb)
                        return stiny(lambda e: e.mul(negC[:], ps_c[:], -1.0))

                s_negc = bound_phase()
                mark("bound")

                if dbg_d:
                    dstage = e1(nc.sbuf_tensor("dstage", [128, T], F32))
                    if "xnT" in dbg_d:
                        WVAL("vector", "v", v_xnT)
                        tap([(xnT[:, f * T:(f + 1) * T], dbg_d["xnT"][:, f * T:(f + 1) * T]) for f in range(NM)], T, dstage)
                    if "qT" in dbg_d:
                        WVAL("vector", "v", q_rope)
                        tap([(qT[:, f * T:(f + 1) * T], dbg_d["qT"][:, f * T:(f + 1) * T]) for f in range(NM)], T, dstage)
                    if "kT" in dbg_d:
                        WVAL("vector", "v", k_rope)
                        tap([(kT[:, f * T:(f + 1) * T], dbg_d["kT"][:, f * T:(f + 1) * T]) for f in range(NM)], T, dstage)
                    if "v" in dbg_d:
                        WVAL("vector", "v", v_v_done)
                        WVAL("vector", "g", g_ones)
                        tap([(vaug[st][:], dbg_d["v"][st * 128:(st + 1) * 128, :]) for st in range(NT)], NH * 65, dstage)

            p1_t, p1_v = now("t"), now("v")
            with ExitStack() as p2:
                e2 = p2.enter_context
                Pbuf = [e2(nc.sbuf_tensor(f"Pbuf{i}", [128, T], BF16)) for i in range(3)]
                a_ev = [e2(nc.sbuf_tensor(f"aev{i}", [65, T], BF16)) for i in range(2)]
                t1f = e2(nc.sbuf_tensor("t1f", [128, 512], F32))
                t2f = e2(nc.sbuf_tensor("t2f", [128, 512], F32))
                dff = e2(nc.sbuf_tensor("dff", [128, 512], F32))
                sqf = e2(nc.sbuf_tensor("sqf", [128, 512], F32))
                dn2 = e2(nc.sbuf_tensor("dn2", [128, T], BF16))
                rab = e2(nc.sbuf_tensor("rab", [128, 32], F32))
                dnstg = e2(nc.sbuf_tensor("dnstg", [128, T], BF16))
                wgn = [e2(nc.sbuf_tensor(f"wgn{i}", [128, C], BF16)) for i in range(2)]
                wun = [e2(nc.sbuf_tensor(f"wun{i}", [128, C], BF16)) for i in range(2)]
                wdn = [e2(nc.sbuf_tensor(f"wdn{i}", [128, C], BF16)) for i in range(2)]
                wgT = [e2(nc.sbuf_tensor(f"wgT{i}", [128, C], BF16)) for i in range(2)]
                wuT = [e2(nc.sbuf_tensor(f"wuT{i}", [128, C], BF16)) for i in range(2)]
                tsil = e2(nc.sbuf_tensor("tsil", [128, C], BF16))
                hstg = [e2(nc.sbuf_tensor(f"hstg{i}", [128, C], BF16)) for i in range(2)]
                wdstg = [e2(nc.sbuf_tensor(f"wdstg{i}", [128, C], BF16)) for i in range(2)]
                psF = e2(nc.psum_tensor("psF", [128, 1024], F32))
                dstage2 = e2(nc.sbuf_tensor("dstage2", [128, T], F32)) if dbg_d else None

                WVAL("tensor", "v", v_v_done)
                WVAL("tensor", "g", g_ones)
                WVAL("tensor", "s", s_negc)

                st8 = {"exp_par": [0, 0], "piece": 0, "dn_t": 0, "dn_ev_s": 0, "aev2": 0,
                       "post_v": 0, "dn_dma": 0}

                class Ffn:

                    def __init__(self):
                        self.units = [(hf, ht) for hf in (0, 1) for ht in range(NHT)]
                        self.ui = 0
                        self.sj = 0
                        self.u = {}
                        self.done_all = False
                        self.tok = [0, 0]
                        self.tok_s = 0

                    def burst(self, budget):
                        for _ in range(budget):
                            self.step()

                    def dma_unit(self, i):
                        if i >= len(self.units):
                            return
                        hf, ht = self.units[i]
                        par = i % 2
                        prev = self.u.get(i - 2)
                        if prev is not None and "tr_last" in prev:
                            WVAL("gpsimd", "t", prev["tr_last"])
                        g2 = dma("gpsimd", wgn[par][:], wg_d[ht * 128:(ht + 1) * 128, :], "qw")
                        g2 = dma("gpsimd", wun[par][:], wu_d[ht * 128:(ht + 1) * 128, :], "qw")
                        if hf == 0:
                            g2 = dma("gpsimd", wdn[par][:],
                                     wd_d[:, ht * 128:(ht + 1) * 128].rearrange("(a p) c -> p a c", p=128), "qw")
                        self.u.setdefault(i, {})["dma"] = g2

                    def tr_half(self, src_t, par, hb, dst_t, dcol):
                        WVAL("tensor", "v", self.tok[hb])
                        WVAL("tensor", "s", self.tok_s)
                        lt = None
                        for m in range(4):
                            mm = hb * 4 + m
                            lt = op("tensor", lambda e, mm=mm, par=par, src_t=src_t: e.matmul(
                                psF[:, mm * 128:(mm + 1) * 128], src_t[par][:, mm * 128:(mm + 1) * 128],
                                ident_bf[:], start=True, stop=True),
                                inc="t" if m == 3 else None)
                        WVAL("vector", "t", lt)
                        ev = op("vector", lambda e, par=par, hb=hb, dst_t=dst_t, dcol=dcol: e.tensor_copy(
                            dst_t[par][:, dcol:dcol + 512], psF[:, hb * 512:(hb + 1) * 512]), inc="v")
                        self.tok[hb] = ev
                        return lt, ev

                    def step(self):
                        if self.done_all:
                            return
                        i = self.ui
                        if i >= len(self.units):
                            self.done_all = True
                            return
                        hf, ht = self.units[i]
                        par = i % 2
                        u = self.u.setdefault(i, {})
                        prev = self.u.get(i - 1, {})
                        steps = (["TRGa", "TRGb", "TRUa", "TRUb", "TRDa", "TRDb", "GATE", "UP"]
                                 if hf == 0 else ["TRGa", "TRGb", "TRUa", "TRUb", "GATE", "UP"])
                        name = steps[self.sj]
                        if name == "TRGa":
                            if i == 0:
                                self.dma_unit(0)
                                self.dma_unit(1)
                            else:
                                self.dma_unit(i + 1)
                            WVAL("tensor", "qw", u["dma"])
                            u["tr_last"], _ = self.tr_half(wgn, par, 0, wgT, 0)
                        elif name == "TRGb":
                            u["tr_last"], _ = self.tr_half(wgn, par, 1, wgT, 512)
                        elif name == "TRUa":
                            u["tr_last"], _ = self.tr_half(wun, par, 0, wuT, 0)
                        elif name == "TRUb":
                            u["tr_last"], _ = self.tr_half(wun, par, 1, wuT, 512)
                        elif name == "TRDa":
                            u["tr_last"], _ = self.tr_half(wdn, par, 0, wdstg, 0)
                        elif name == "TRDb":
                            u["tr_last"], ev = self.tr_half(wdn, par, 1, wdstg, 512)
                            pp2 = self.u.get(i - 2, {})
                            if pp2.get("wdma"):
                                WVAL("sync", "qs", pp2["wdma"])
                            WVAL("sync", "v", ev)
                            u["wdma"] = dma("sync", wdt_d[ht * 128:(ht + 1) * 128, :], wdstg[par][:], "qs")
                        elif name == "GATE":
                            WVAL("tensor", "v", self.tok[0])
                            WVAL("tensor", "v", self.tok[1])
                            WVAL("tensor", "s", self.tok_s)
                            lt = None
                            for mt in range(NM):
                                for c2 in range(2):
                                    lt = op("tensor", lambda e, mt=mt, c2=c2, par=par, hf=hf: e.matmul(
                                        psF[:, c2 * 512:(c2 + 1) * 512],
                                        wgT[par][:, mt * 128:(mt + 1) * 128],
                                        xnT[:, mt * T + hf * 1024 + c2 * 512: mt * T + hf * 1024 + (c2 + 1) * 512],
                                        start=(mt == 0), stop=(mt == NM - 1)),
                                        inc="t" if (mt == NM - 1 and c2 == 1) else None, reuse=(c2 == 1))
                            WVAL("scalar", "t", lt)
                            if prev.get("hct"):
                                WVAL("scalar", "v", prev["hct"])
                            u["silu"] = op("scalar", lambda e: e.activation(
                                tsil[:], psF[:], AF.Silu, scale=SCL), inc="s")
                            self.tok_s = u["silu"]
                        elif name == "UP":
                            WVAL("tensor", "s", u["silu"])
                            lt = None
                            for mt in range(NM):
                                for c2 in range(2):
                                    lt = op("tensor", lambda e, mt=mt, c2=c2, par=par, hf=hf: e.matmul(
                                        psF[:, c2 * 512:(c2 + 1) * 512],
                                        wuT[par][:, mt * 128:(mt + 1) * 128],
                                        xnT[:, mt * T + hf * 1024 + c2 * 512: mt * T + hf * 1024 + (c2 + 1) * 512],
                                        start=(mt == 0), stop=(mt == NM - 1)),
                                        inc="t" if (mt == NM - 1 and c2 == 1) else None, reuse=(c2 == 1))
                            WVAL("vector", "t", lt)
                            WVAL("vector", "s", u["silu"])
                            pp2 = self.u.get(i - 2, {})
                            if pp2.get("hdma"):
                                WVAL("vector", "qs", pp2["hdma"])
                            u["hct"] = op("vector", lambda e, par=par: e.tensor_mul(
                                hstg[par][:], psF[:], tsil[:]), inc="v")
                            self.tok[0] = u["hct"]
                            self.tok[1] = u["hct"]
                            WVAL("sync", "v", u["hct"])
                            u["hdma"] = dma("sync",
                                            hact_d[ht * 128:(ht + 1) * 128, hf * 1024:(hf + 1) * 1024],
                                            hstg[par][:], "qs")
                        self.sj += 1
                        if self.sj >= len(steps):
                            self.sj = 0
                            self.ui += 1

                    def step_force(self):
                        self.step()

                ffn = Ffn()
                WVAL("gpsimd", "t", p1_t)
                WVAL("gpsimd", "v", p1_v)
                WVAL("gpsimd", "s", s_negc)

                def panel_piece(panel, kt):
                    q0 = kt * 128
                    if panel == 0:
                        return (q0, 1024 - q0)
                    return (1024, 1024) if kt < 8 else (q0, T - q0)

                def panel_chunks(panel, kt):
                    pst, plen = panel_piece(panel, kt)
                    out, qa = [], pst
                    while qa < pst + plen:
                        nxt = min(pst + plen, (qa // 512 + 1) * 512)
                        out.append((qa, nxt - qa))
                        qa = nxt
                    return out

                def emit_head(hd, hi, Sp, ATp, a, ftq):
                    g_off = (hd % NH) * 65
                    aev_prev_v = 0
                    for panel in (0, 1):
                        base = 1024 * panel
                        kts = list(range(8)) if panel == 0 else list(range(NT))
                        exp_cnt = {}

                        def emit_scores(kt):
                            pst, plen = panel_piece(panel, kt)
                            has_mask = (panel == 0) or (kt >= 8)
                            pp = st8["piece"] % 2
                            st8["piece"] += 1
                            WVAL("tensor", "s", st8["exp_par"][pp])
                            lt = None
                            mm_i = 0
                            for lo in range(0, plen, 512):
                                n = min(512, plen - lo)
                                lt = op("tensor", lambda e, kt=kt, pst=pst, lo=lo, n=n, pp=pp: e.matmul(
                                    Sp[pp][:, lo:lo + n],
                                    kT[32 * a:32 * a + 32, ftq * T + kt * 128: ftq * T + (kt + 1) * 128],
                                    qT[32 * a:32 * a + 32, ftq * T + pst + lo: ftq * T + pst + lo + n],
                                    start=True, stop=True, tile_position=(32 * a, 0)),
                                    inc="t" if lo + n >= plen else None, reuse=(mm_i > 0))
                                mm_i += 1
                            if has_mask:
                                WVAL("vector", "t", lt)
                                vmk = op("vector", lambda e, pp=pp: e.tensor_add(
                                    Sp[pp][:, 0:128], Sp[pp][:, 0:128], mkb[:]), inc="v")
                                WVAL("scalar", "v", vmk)
                            WVAL("scalar", "t", lt)
                            st8["exp_par"][pp] = op("scalar", lambda e, pst=pst, plen=plen, kt=kt, pp=pp: e.activation(
                                Pbuf[kt % 3][:, pst:pst + plen], Sp[pp][:, 0:plen], AF.Exp,
                                bias=negC[:], scale=ISQ), inc="s")
                            exp_cnt[kt] = st8["exp_par"][pp]

                        def emit_pv(kt, is_last):
                            chunks = panel_chunks(panel, kt)
                            for ci, (qa, n) in enumerate(chunks):
                                WVAL("tensor", "s", exp_cnt[kt])
                                lb = (qa - base) // 512
                                op("tensor", lambda e, kt=kt, qa=qa, n=n, lb=lb, base=base: e.matmul(
                                    ATp[0:65, qa - base: qa - base + n],
                                    vaug[kt][:, g_off:g_off + 65],
                                    Pbuf[kt % 3][:, qa:qa + n],
                                    start=(kt == 0), stop=(kt == 4 * (lb + 2 * panel) + 3)),
                                    inc="t" if (is_last and ci == len(chunks) - 1) else None, reuse=(ci > 0))

                        emit_scores(kts[0])
                        emit_scores(kts[1])
                        WVAL("tensor", "v", aev_prev_v)
                        for kx in range(2, len(kts)):
                            emit_scores(kts[kx])
                            emit_pv(kts[kx - 2], False)
                        emit_pv(kts[-2], False)
                        emit_pv(kts[-1], True)
                        t_pv = now("t")
                        WVAL("vector", "t", t_pv)
                        WVAL("vector", "t", st8["dn_t"])
                        aev_prev_v = op("vector", lambda e, hi=hi, base=base: e.tensor_copy(
                            a_ev[hi][:, base:base + 1024], ATp[:]), inc="v")
                        ffn.burst(3)
                    if hi == 0:
                        WVAL("tensor", "v", aev_prev_v)
                    return aev_prev_v

                def emit_pair(pr):
                    h1, h2 = pr, pr + NH
                    a, ftq = pr % 4, pr // 4
                    with ExitStack() as pp_s:
                        Sp = [pp_s.enter_context(nc.psum_tensor(f"Sp{pr}{i}", [128, 1024], F32)) for i in range(2)]
                        ATp = pp_s.enter_context(nc.psum_tensor(f"AT{pr}", [65, 1024], F32))
                        WVAL("tensor", "s", st8["dn_ev_s"])
                        WVAL("tensor", "v", st8["aev2"])
                        WVAL("tensor", "v", st8["post_v"])
                        emit_head(h1, 0, Sp, ATp, a, ftq)
                        av2 = emit_head(h2, 1, Sp, ATp, a, ftq)
                        st8["aev2"] = av2

                    if "aev0" in dbg_d and pr == 0:
                        WVAL("vector", "v", av2)
                        tap([(a_ev[0][:], dbg_d["aev0"][0:65, :]), (a_ev[1][:], dbg_d["aev0"][65:130, :])], T, dstage2)

                    for hv in range(2):
                        with ExitStack() as po_s:
                            psA = po_s.enter_context(nc.psum_tensor(f"psA{pr}{hv}", [128, 1024], F32))
                            psB = po_s.enter_context(nc.psum_tensor(f"psB{pr}{hv}", [128, 1024], F32))
                            WVAL("tensor", "v", av2 if hv == 0 else st8["post_v"])
                            tt = None
                            for qt in range(8):
                                qg = hv * 8 + qt
                                op("tensor", lambda e, qt=qt, qg=qg, psA=psA: e.matmul(
                                    psA[:, qt * 128: qt * 128 + 65], a_ev[0][:, qg * 128:(qg + 1) * 128],
                                    ident_bf[0:65, 0:65], start=True, stop=True))
                                tt = op("tensor", lambda e, qt=qt, qg=qg, psB=psB: e.matmul(
                                    psB[:, qt * 128: qt * 128 + 65], a_ev[1][:, qg * 128:(qg + 1) * 128],
                                    ident_bf[0:65, 0:65], start=True, stop=True),
                                    inc="t" if qt == 7 else None)
                            WVAL("vector", "t", tt)
                            pav = psA[:].rearrange("p (a b) -> p a b", a=8)
                            pbv = psB[:].rearrange("p (a b) -> p a b", a=8)
                            vtiny(lambda e, pav=pav: e.reciprocal(rab[:, 0:8], pav[:, :, 64]))
                            vtiny(lambda e, pbv=pbv: e.reciprocal(rab[:, 8:16], pbv[:, :, 64]))
                            vtiny(lambda e: e.tensor_scalar_mul(rab[:, 8:16], rab[:, 8:16], lam_col[:, 0:1]))
                            op("vector", lambda e, pbv=pbv: e.tensor_tensor(
                                t2f[:].rearrange("p (a b) -> p a b", a=8), pbv[:, :, 0:64],
                                rab[:, 8:16].unsqueeze(2).broadcast_to([128, 8, 64]), ALU.mult))
                            op("vector", lambda e, pav=pav: e.tensor_tensor(
                                t1f[:].rearrange("p (a b) -> p a b", a=8), pav[:, :, 0:64],
                                rab[:, 0:8].unsqueeze(2).broadcast_to([128, 8, 64]), ALU.mult))
                            op("vector", lambda e: e.tensor_sub(dff[:], t1f[:], t2f[:]))
                            op("vector", lambda e: e.tensor_mul(sqf[:], dff[:], dff[:]))
                            v_ms = op("vector", lambda e: e.reduce_sum(
                                rab[:, 16:24], sqf[:].rearrange("p (a b) -> p a b", a=8), axis=AX), inc="v")
                            WVAL("scalar", "v", v_ms)
                            stiny(lambda e: e.activation(rab[:, 24:32], rab[:, 16:24], AF.Ln, bias=eps_col, scale=1.0 / VD))
                            s_rstd = op("scalar", lambda e: e.activation(rab[:, 24:32], rab[:, 24:32], AF.Exp, scale=-0.5, bias=ln08_col), inc="s")
                            WVAL("vector", "s", s_rstd)
                            WVAL("vector", "t", st8["dn_t"])
                            v_dn = op("vector", lambda e, pr=pr, hv=hv: e.tensor_tensor(
                                dn2[:].rearrange("p (a j b) -> p a j b", a=NT, j=2)[:, hv * 8:(hv + 1) * 8, pr % 2, :],
                                dff[:].rearrange("p (a b) -> p a b", a=8),
                                rab[:, 24:32].unsqueeze(2).broadcast_to([128, 8, 64]), ALU.mult), inc="v")
                            st8["post_v"] = v_dn
                        ffn.burst(6)

                    if pr % 2 == 1:
                        m = pr // 2
                        with ExitStack() as pd_s:
                            psD = pd_s.enter_context(nc.psum_tensor(f"psD{pr}", [128, T], F32))
                            WVAL("tensor", "v", st8["post_v"])
                            td = None
                            for qt in range(NT):
                                td = op("tensor", lambda e, qt=qt, psD=psD: e.matmul(
                                    psD[:, qt * 128:(qt + 1) * 128], dn2[:, qt * 128:(qt + 1) * 128],
                                    ident_bf[:], start=True, stop=True),
                                    inc="t" if qt == NT - 1 else None)
                            st8["dn_t"] = td
                            WVAL("scalar", "t", td)
                            WVAL("scalar", "qs", st8["dn_dma"])
                            st8["dn_ev_s"] = op("scalar", lambda e, psD=psD: e.copy(dnstg[:], psD[:]), inc="s")
                            WVAL("sync", "s", st8["dn_ev_s"])
                            st8["dn_dma"] = dma("sync", dnt_d[m * 128:(m + 1) * 128, :], dnstg[:], "qs")
                        ffn.burst(8)

                for pr in range(NH):
                    emit_pair(pr)
                    mark(f"pair{pr}")
                while not ffn.done_all:
                    ffn.step()
                dn_ev_s_final = st8["dn_ev_s"]
                dn_dma_final = st8["dn_dma"]
                last_hdma = ffn.u[len(ffn.units) - 1]["hdma"]
                last_wdma = ffn.u[NHT - 1]["wdma"]

                if "dnT" in dbg_d:
                    pass

        scope_x.close()
        p2_t, p2_v, p2_s = now("t"), now("v"), now("s")
        mark("p3pre")

        with ExitStack() as p345:
            e4 = p345.enter_context
            hsl = [e4(nc.sbuf_tensor(f"hsl{i}", [128, NHT * 128], BF16)) for i in range(2)]
            wdT = e4(nc.sbuf_tensor("wdT_all", [128, NHT * C], BF16))
            wp = [e4(nc.sbuf_tensor(f"wp{i}", [128, C], BF16)) for i in range(NM)]
            dsl = [e4(nc.sbuf_tensor(f"dsl{i}", [128, NM * 128], BF16)) for i in range(2)]
            xsp = [e4(nc.sbuf_tensor(f"xsp{i}", [128, C], F32)) for i in range(2)]
            opb = [e4(nc.sbuf_tensor(f"opb{i}", [128, C], F32)) for i in range(2)]
            tds = e4(nc.sbuf_tensor("tds", [128, C], F32))
            outb = [e4(nc.sbuf_tensor(f"outb{i}", [128, C], F32)) for i in range(2)]

            for engb in ("gpsimd", "sync"):
                WVAL(engb, "t", p2_t)
                WVAL(engb, "v", p2_v)
                WVAL(engb, "s", p2_s)
            wp_t = None
            for mt in range(NM):
                wp_t = dma("gpsimd", wp[mt][:], wproj_d[mt * 128:(mt + 1) * 128, :], "qw")
            WVAL("gpsimd", "qs", last_wdma)
            wdt_ld = dma("gpsimd", wdT[:].rearrange("p (a c) -> p a c", a=NHT), wdt_d.rearrange("(a p) c -> p a c", p=128), "qw")

            WVAL("tensor", "qw", wp_t)
            WVAL("tensor", "s", dn_ev_s_final)
            out_dma = {}
            comb_v = {}
            opb_v = {}
            dn_ld = {}
            with ExitStack() as pp3:
                ps_p = [pp3.enter_context(nc.psum_tensor(f"psp{i}", [128, 1024], F32)) for i in range(2)]
                ps_d = [pp3.enter_context(nc.psum_tensor(f"psd{i}", [128, 512], F32)) for i in range(4)]
                sevac = {}

                xld_t = {}

                hs_ld = {}

                def issue_loads(st):
                    par = st % 2
                    if st >= 2:
                        WVAL("sync", "v", opb_v[st - 2])
                        WVAL("sync", "s", sevac[st - 2])
                    WVAL("sync", "qs", dn_dma_final)
                    dn_ld[st] = dma("sync", dsl[par][:].rearrange("p (a c) -> p a c", a=NM),
                                    dnt_d[:, st * 128:(st + 1) * 128].rearrange("(a p) c -> p a c", p=128), "qx")
                    if st == 0:
                        WVAL("sync", "qs", last_hdma)
                    hs_ld[st] = dma("sync", hsl[par][:].rearrange("p (a c) -> p a c", a=NHT),
                                    hact_d[:, st * 128:(st + 1) * 128].rearrange("(a p) c -> p a c", p=128), "qx")
                    xld_t[st] = dma("sync", xsp[par][:], x_d[st * 128:(st + 1) * 128, :], "qx")

                def emit_tile(st):
                    par = st % 2
                    xld = xld_t[st]
                    if st >= 2:
                        WVAL("tensor", "v", opb_v[st - 2])
                    WVAL("tensor", "qx", xld)
                    lt = None
                    for mt in range(NM):
                        for c2 in range(2):
                            lt = op("tensor", lambda e, mt=mt, c2=c2, p=par: e.matmul(
                                ps_p[p][:, c2 * 512:(c2 + 1) * 512],
                                dsl[p][:, mt * 128:(mt + 1) * 128],
                                wp[mt][:, c2 * 512:(c2 + 1) * 512],
                                start=(mt == 0), stop=(mt == NM - 1)),
                                inc="t" if mt == NM - 1 and c2 == 1 else None, reuse=(c2 == 1))
                    WVAL("vector", "t", lt)
                    WVAL("vector", "qx", xld)
                    opb_v[st] = op("vector", lambda e, p=par: e.tensor_add(opb[p][:], ps_p[p][:], xsp[p][:]), inc="v")
                    if st >= 2:
                        WVAL("tensor", "s", sevac[st - 2])
                    if st == 0:
                        WVAL("tensor", "qw", wdt_ld)
                    WVAL("tensor", "qx", hs_ld[st])
                    lt = None
                    for ht in range(NHT):
                        for c2 in range(2):
                            lt = op("tensor", lambda e, ht=ht, st=st, c2=c2, dp=par: e.matmul(
                                ps_d[2 * dp + c2][:],
                                hsl[par][:, ht * 128:(ht + 1) * 128],
                                wdT[:, ht * C + c2 * 512: ht * C + (c2 + 1) * 512],
                                start=(ht == 0), stop=(ht == NHT - 1)),
                                inc="t" if (ht == NHT - 1 and c2 == 1) else None, reuse=(c2 == 1))
                    WVAL("scalar", "t", lt)
                    sv = None
                    for c2 in range(2):
                        sv = op("scalar", lambda e, c2=c2, dp=par: e.mul(
                            tds[:, c2 * 512:(c2 + 1) * 512], ps_d[2 * dp + c2][:], SCL * SCL),
                            inc="s" if c2 == 1 else None)
                    sevac[st] = sv
                    if st + 2 < NT:
                        issue_loads(st + 2)
                    WVAL("vector", "s", sevac[st])
                    if st >= 2:
                        WVAL("vector", "o", out_dma[st - 2])
                    comb_v[st] = op("vector", lambda e, p=par: e.tensor_add(outb[p][:], tds[:], opb[p][:]), inc="v")
                    WVAL("sync", "v", comb_v[st])
                    out_dma[st] = dma("sync", out_d[st * 128:(st + 1) * 128, :], outb[par][:], "o")

                issue_loads(0)
                issue_loads(1)
                for st in range(NT):
                    emit_tile(st)
                mark("half1")

            WVAL("sync", "o", now("o"))

        if stop is not None:
            cut = MARKS[stop]
            for k in Q:
                Q[k] = Q[k][:cut[k]]
        with nc.Block() as block:
            block.tensor(lambda e: [fn(e) for fn in Q["tensor"]])
            block.vector(lambda e: [fn(e) for fn in Q["vector"]])
            block.scalar(lambda e: [fn(e) for fn in Q["scalar"]])
            block.gpsimd(lambda e: [fn(e) for fn in Q["gpsimd"]])
            block.sync(lambda e: [fn(e) for fn in Q["sync"]])

    return nc


def _consts():
    ident = np.eye(128, dtype=np.float32)
    swap = np.zeros((128, 128), dtype=np.float32)
    for p in range(128):
        swap[p ^ 1, p] = 1.0
    tri = np.triu(np.ones((128, 128), dtype=np.float32))
    half = HD // 2
    j = (np.arange(128) % 32) // 2
    theta = (10000.0 ** (-(np.arange(half) * 2.0 / HD)))[j]
    ang = theta[:, None] * np.arange(T, dtype=np.float64)[None, :]
    cos = np.cos(ang).astype(np.float32)
    sin = np.sin(ang).astype(np.float32)
    sgn = np.where(np.arange(128) % 2 == 0, -1.0, 1.0).astype(np.float32)
    sinm = (sin * sgn[:, None]).astype(np.float32)
    onesr = np.ones((1, 128), dtype=np.float32)
    maskbig = (np.tril(np.ones((128, 128)), -1) * (-1e9)).astype(np.float32)
    ccols = np.zeros((128, 4), dtype=np.float32)
    ccols[:, 0] = EPS
    ccols[:, 1] = math.log(1.0 - LAM0)
    return dict(ident=ident, swapm=swap, tri=tri, cosr=cos, sinr=sinm, onesr=onesr,
                ccols=ccols, maskbig=maskbig)


def kernel(L0_attn_x, L0_attn_W_attn, L0_attn_W_proj, L0_attn_lambda_q1,
           L0_attn_lambda_k1, L0_attn_lambda_q2, L0_attn_lambda_k2,
           L0_ffn_w_gate, L0_ffn_w_up, L0_ffn_w_down, _debug=(), _nc=None):
    x = np.ascontiguousarray(np.asarray(L0_attn_x, dtype=np.float32))
    lam = np.concatenate([np.asarray(a, dtype=np.float32).reshape(-1) for a in
                          (L0_attn_lambda_q1, L0_attn_lambda_k1,
                           L0_attn_lambda_q2, L0_attn_lambda_k2)]).reshape(1, 4 * HD)
    shared = dict(
        wattn=np.ascontiguousarray(np.asarray(L0_attn_W_attn, dtype=np.float32)),
        wproj=np.ascontiguousarray(np.asarray(L0_attn_W_proj, dtype=np.float32)),
        lambdas=lam,
        wg=np.ascontiguousarray(np.asarray(L0_ffn_w_gate, dtype=np.int32)),
        wu=np.ascontiguousarray(np.asarray(L0_ffn_w_up, dtype=np.int32)),
        wd=np.ascontiguousarray(np.asarray(L0_ffn_w_down, dtype=np.int32)),
        **_consts(),
    )
    nc = _nc if _nc is not None else build_nc(debug=_debug)
    in_maps = []
    for c in range(8):
        m = dict(shared)
        m["x"] = np.ascontiguousarray(x[c].reshape(T, C))
        in_maps.append(m)
    res = run_bass_kernel_spmd(nc, in_maps, core_ids=list(range(8)))
    out = np.stack([np.asarray(r["out"], dtype=np.float32) for r in res.results])
    if _debug:
        return out.reshape(8, T, C), res
    return out.reshape(8, T, C)


# version 13
# speedup vs baseline: 1.1979x; 1.1979x over previous
import dataclasses
import math
import numpy as np
from contextlib import ExitStack

import concourse.bass as bass
import concourse.mybir as mybir
from concourse.bass_utils import run_bass_kernel_spmd

F32 = mybir.dt.float32
BF16 = mybir.dt.bfloat16
I32 = mybir.dt.int32
AF = mybir.ActivationFunctionType
ALU = mybir.AluOpType
AX = mybir.AxisListType.X

T = 2048
C = 1024
NH = 16
HD = 32
VD = 64
HID = 3072
SCL = 1.0 / 127.0
LAM0 = 0.2
EPS = 1e-5
ISQ = 1.0 / math.sqrt(32.0)
NT = T // 128
NM = C // 128
NHT = HID // 128


def build_nc(debug=(), stop=None):
    debug = set(debug)
    nc = bass.Bass("TRN2", target_bir_lowering=False)

    x_d = nc.dram_tensor("x", [T, C], F32, kind="ExternalInput").ap()
    wattn_d = nc.dram_tensor("wattn", [C, 3 * C], F32, kind="ExternalInput").ap()
    wproj_d = nc.dram_tensor("wproj", [C, C], F32, kind="ExternalInput").ap()
    lam_d = nc.dram_tensor("lambdas", [1, 4 * HD], F32, kind="ExternalInput").ap()
    wg_d = nc.dram_tensor("wg", [HID, C], I32, kind="ExternalInput").ap()
    wu_d = nc.dram_tensor("wu", [HID, C], I32, kind="ExternalInput").ap()
    wd_d = nc.dram_tensor("wd", [C, HID], I32, kind="ExternalInput").ap()
    ident_d = nc.dram_tensor("ident", [128, 128], F32, kind="ExternalInput").ap()
    swap_d = nc.dram_tensor("swapm", [128, 128], F32, kind="ExternalInput").ap()
    tri_d = nc.dram_tensor("tri", [128, 128], F32, kind="ExternalInput").ap()
    mkb_d = nc.dram_tensor("maskbig", [128, 128], F32, kind="ExternalInput").ap()
    cos_d = nc.dram_tensor("cosr", [128, T], F32, kind="ExternalInput").ap()
    sinm_d = nc.dram_tensor("sinr", [128, T], F32, kind="ExternalInput").ap()
    onesr_d = nc.dram_tensor("onesr", [1, 128], F32, kind="ExternalInput").ap()
    ccols_d = nc.dram_tensor("ccols", [128, 4], F32, kind="ExternalInput").ap()
    out_d = nc.dram_tensor("out", [T, C], F32, kind="ExternalOutput").ap()
    spill_d = nc.dram_tensor("spill", [T, C], BF16, kind="Internal").ap()
    hact_d = nc.dram_tensor("hacts", [HID, T], BF16, kind="Internal").ap()
    wdt_d = nc.dram_tensor("wdts", [HID, C], BF16, kind="Internal").ap()
    dnt_d = nc.dram_tensor("dnts", [C, T], BF16, kind="Internal").ap()
    dbg_d = {}
    for name, shape in (("xnT", [128, NM * T]), ("qT", [128, NM * T]), ("kT", [128, NM * T]),
                        ("v", [T, NH * 65]), ("aev0", [2 * 65, T]), ("dnT", [128, NM * T]),
                        ("hact", [128, NHT * 1024])):
        if name in debug:
            dbg_d[name] = nc.dram_tensor("dbg_" + name, shape, F32, kind="ExternalOutput").ap()

    Q = {k: [] for k in ("tensor", "vector", "scalar", "gpsimd", "sync")}
    CNT = {}
    MARKS = {}

    def mark(name):
        MARKS[name] = {k: len(v) for k, v in Q.items()}

    with ExitStack() as top:
        et = top.enter_context
        sm = {n: et(nc.semaphore(n)) for n in ("qx", "qw", "qs", "t", "v", "s", "g", "o")}

        def bump(sem, by=1):
            CNT[sem] = CNT.get(sem, 0) + by
            return CNT[sem]

        def now(sem):
            return CNT.get(sem, 0)

        def WVAL(eng, sem, val):
            if val and val > 0:
                Q[eng].append(lambda e, h=sm[sem], v=val: e.wait_ge(h, v))

        def dma(eng, out, in_, sem):
            tgt = bump(sem, 16)
            Q[eng].append(lambda e, o=out, i=in_, h=sm[sem]: e.dma_start(out=o, in_=i).then_inc(h, 16))
            return tgt

        _ldw_cache = {}

        def _swap_ldw(b):
            ins = b.ins
            blk = _ldw_cache.get("blk")
            if blk is not None:
                insts = blk.instructions
                k = len(insts) - 1
                if insts[k].name == ins.name:
                    insts[k] = dataclasses.replace(ins, ldweights=False)
                    return
            for fn_ in nc.m.functions:
                for bb in fn_.blocks:
                    insts = bb.instructions
                    k = len(insts) - 1
                    if k >= 0 and insts[k].name == ins.name:
                        _ldw_cache["blk"] = bb
                        insts[k] = dataclasses.replace(ins, ldweights=False)
                        return
            raise RuntimeError("ldw swap failed")

        def op(eng, fn, inc=None, reuse=False):
            if not reuse:
                if inc is None:
                    Q[eng].append(fn)
                    return None
                tgt = bump(inc)
                Q[eng].append(lambda e, f=fn, h=sm[inc]: f(e).then_inc(h, 1))
                return tgt
            tgt = bump(inc) if inc else None
            h = sm[inc] if inc else None

            def wrapped(e, f=fn, h=h):
                b = f(e)
                if h is not None:
                    b.then_inc(h, 1)
                _swap_ldw(b)
            Q[eng].append(wrapped)
            return tgt

        def vtiny(fn):
            t = op("vector", fn, inc="v")
            WVAL("vector", "v", t)
            return t

        def stiny(fn):
            t = op("scalar", fn, inc="s")
            WVAL("scalar", "s", t)
            return t

        ident_bf = et(nc.sbuf_tensor("ident_bf", [128, 128], BF16))
        swap_bf = et(nc.sbuf_tensor("swap_bf", [128, 128], BF16))
        tri_bf = et(nc.sbuf_tensor("tri_bf", [128, 128], BF16))
        mkb = et(nc.sbuf_tensor("mkb", [128, 128], F32))
        onesr_f = et(nc.sbuf_tensor("onesr_f", [1, 128], F32))
        ccols = et(nc.sbuf_tensor("ccols_sb", [128, 4], F32))
        lam_col = et(nc.sbuf_tensor("lam_col", [128, 1], F32))
        negC = et(nc.sbuf_tensor("negC", [128, 1], F32))
        small = et(nc.sbuf_tensor("small", [128, 8], F32))
        lamv = et(nc.sbuf_tensor("lamv", [1, 8], F32))
        lam_in = et(nc.sbuf_tensor("lam_in", [1, 4 * HD], F32))
        lprod = et(nc.sbuf_tensor("lprod", [1, 2 * HD], F32))
        eps_col = ccols[:, 0:1]
        ln08_col = ccols[:, 1:2]

        def tap(pairs, ncols, stage):
            for src_ap, dram_ap in pairs:
                np_ = src_ap.shape[0]
                vv = op("vector", lambda e, sp=src_ap, st=stage, n=ncols, np_=np_: e.tensor_copy(st[0:np_, 0:n], sp), inc="v")
                WVAL("sync", "v", vv)
                dq = dma("sync", dram_ap, stage[0:np_, 0:ncols], "qs")
                WVAL("vector", "qs", dq)

        dma("sync", onesr_f[:], onesr_d, "qx")
        dma("sync", ccols[:], ccols_d, "qx")
        qx_l = dma("sync", lam_in[:], lam_d, "qx")
        dma("gpsimd", ident_bf[:], ident_d, "qw")
        dma("gpsimd", swap_bf[:], swap_d, "qw")
        dma("gpsimd", tri_bf[:], tri_d, "qw")
        qw_consts = dma("gpsimd", mkb[:], mkb_d, "qw")

        WVAL("vector", "qx", qx_l)
        vtiny(lambda e: e.tensor_mul(lprod[:, 0:HD], lam_in[:, 0:HD], lam_in[:, HD:2 * HD]))
        vtiny(lambda e: e.tensor_mul(lprod[:, HD:2 * HD], lam_in[:, 2 * HD:3 * HD], lam_in[:, 3 * HD:4 * HD]))
        v1 = op("vector", lambda e: e.reduce_sum(lamv[0:1, 0:2], lprod[:].rearrange("p (a b) -> p a b", a=2), axis=AX), inc="v")
        WVAL("scalar", "v", v1)
        s1 = op("scalar", lambda e: e.activation(lamv[0:1, 2:4], lamv[0:1, 0:2], AF.Exp), inc="s")
        WVAL("vector", "s", s1)
        vtiny(lambda e: e.tensor_sub(lamv[0:1, 0:1], lamv[0:1, 2:3], lamv[0:1, 3:4]))
        v_lam = op("vector", lambda e: e.tensor_scalar_add(lamv[0:1, 0:1], lamv[0:1, 0:1], LAM0), inc="v")

        scope_x = ExitStack()
        xnT = scope_x.enter_context(nc.sbuf_tensor("xnT", [128, NM * T], BF16))
        with ExitStack() as p0:
            ep = p0.enter_context
            xst = [ep(nc.sbuf_tensor(f"xst{i}", [128, C], F32)) for i in range(2)]
            sqj = ep(nc.sbuf_tensor("sqj", [128, C], F32))
            xnbp = [ep(nc.sbuf_tensor(f"xnbp{i}", [128, C], BF16)) for i in range(2)]
            ps_tr = [ep(nc.psum_tensor(f"ps_tr{i}", [128, NM * 128], F32)) for i in range(2)]
            ps_misc = ep(nc.psum_tensor("ps_misc", [128, 1], F32))

            WVAL("tensor", "qx", qx_l)
            WVAL("tensor", "v", v_lam)
            t_lam = op("tensor", lambda e: e.matmul(ps_misc[:], onesr_f[:], lamv[0:1, 0:1], start=True, stop=True), inc="t")
            WVAL("vector", "t", t_lam)
            op("vector", lambda e: e.tensor_copy(lam_col[:], ps_misc[:]), inc="v")

            xn_done = [0] * NT
            ev_done = [0] * NT
            rs_done = [0] * NT

            def ln_stats(st):
                par = st % 2
                c0 = 4 * par
                if st >= 2:
                    WVAL("sync", "v", xn_done[st - 2])
                ld = dma("sync", xst[par][:], x_d[st * 128:(st + 1) * 128, :], "qx")
                WVAL("vector", "qx", ld)
                vtiny(lambda e, par=par, c0=c0: e.reduce_sum(small[:, c0:c0 + 1], xst[par][:], axis=AX))
                v_mn = op("vector", lambda e, c0=c0: e.tensor_scalar_mul(small[:, c0 + 1:c0 + 2], small[:, c0:c0 + 1], -1.0 / C), inc="v")
                WVAL("scalar", "v", v_mn)
                op("scalar", lambda e, par=par, c0=c0: e.activation(
                    sqj[:], xst[par][:], AF.Square, bias=small[:, c0 + 1:c0 + 2], accum_out=small[:, c0 + 2:c0 + 3]))
                stiny(lambda e, c0=c0: e.activation(small[:, c0 + 3:c0 + 4], small[:, c0 + 2:c0 + 3], AF.Ln, bias=eps_col, scale=1.0 / C))
                rs_done[st] = op("scalar", lambda e, c0=c0: e.activation(small[:, c0 + 3:c0 + 4], small[:, c0 + 3:c0 + 4], AF.Exp, scale=-0.5), inc="s")

            def ln_finish(st):
                par = st % 2
                c0 = 4 * par
                WVAL("vector", "s", rs_done[st])
                xn_done[st] = op("vector", lambda e, par=par, c0=c0: e.tensor_scalar(
                    xnbp[par][:], xst[par][:], small[:, c0 + 1:c0 + 2], small[:, c0 + 3:c0 + 4],
                    ALU.add, ALU.mult), inc="v")
                WVAL("tensor", "v", xn_done[st])
                WVAL("tensor", "qw", qw_consts)
                if st >= 2:
                    WVAL("tensor", "v", ev_done[st - 2])
                t_tr = None
                for m in range(NM):
                    t_tr = op("tensor", lambda e, m=m, par=par: e.matmul(
                        ps_tr[par][:, m * 128:(m + 1) * 128], xnbp[par][:, m * 128:(m + 1) * 128], ident_bf[:],
                        start=True, stop=True),
                        inc="t" if m == NM - 1 else None)
                WVAL("vector", "t", t_tr)
                ev_done[st] = op("vector", lambda e, st=st, par=par: e.tensor_copy(
                    xnT[:].rearrange("p (m t) -> p m t", m=NM)[:, :, st * 128:(st + 1) * 128],
                    ps_tr[par][:].rearrange("p (m t) -> p m t", m=NM)), inc="v")

            ln_stats(0)
            for st in range(1, NT):
                ln_stats(st)
                ln_finish(st - 1)
            ln_finish(NT - 1)
            v_xnT = ev_done[NT - 1]
            p0_t, p0_s = now("t"), now("s")
            mark("p0")

        with ExitStack() as pA:
            ea = pA.enter_context
            qT = ea(nc.sbuf_tensor("qT", [128, NM * T], BF16))
            kT = ea(nc.sbuf_tensor("kT", [128, NM * T], BF16))
            vaug = [ea(nc.sbuf_tensor(f"vaug{i}", [128, NH * 65], BF16)) for i in range(NT)]

            g_ones = 0
            for st in range(NT):
                g_ones = op("gpsimd", lambda e, st=st: e.memset(
                    vaug[st][:].rearrange("p (h c) -> p h c", h=NH)[:, :, 64:65], 1.0),
                    inc="g" if st == NT - 1 else None)

            with ExitStack() as p1:
                e1 = p1.enter_context
                cosr = e1(nc.sbuf_tensor("cosr_sb", [128, T], BF16))
                sinr = e1(nc.sbuf_tensor("sinr_sb", [128, T], BF16))
                wab = [e1(nc.sbuf_tensor(f"wab{i}", [128, C], BF16)) for i in range(NM)]
                wab2 = [e1(nc.sbuf_tensor(f"wab2{i}", [128, C], BF16)) for i in range(NM)]
                tmp1 = e1(nc.sbuf_tensor("tmp1", [128, 512], F32))
                tmp2 = e1(nc.sbuf_tensor("tmp2", [128, 512], F32))
                macc = e1(nc.sbuf_tensor("macc", [128, 2], F32))
                mbf = e1(nc.sbuf_tensor("mbf", [128, 2], BF16))

                WVAL("gpsimd", "v", v_xnT)
                WVAL("gpsimd", "t", p0_t)
                WVAL("gpsimd", "s", p0_s)
                qw_cs = dma("gpsimd", cosr[:], cos_d, "qw")
                qw_cs = dma("gpsimd", sinr[:], sinm_d, "qw")
                ps_w = [e1(nc.psum_tensor(f"psw{i}", [128, 512], F32)) for i in range(2)]

                class Rope:

                    def __init__(self):
                        self.items = []
                        for w in (0, 1):
                            for i in range(4 * NM):
                                self.items.append((w, i))
                                if i % 4 == 3:
                                    self.items.append((2 + w, i // 4))
                        self.i = 0
                        self.rev = {}
                        self.ready = [0, 0]
                        self.first = True

                    def step(self):
                        if self.i >= len(self.items):
                            return
                        which, idx = self.items[self.i]
                        if self.ready[which % 2] == 0:
                            return
                        if which >= 2:
                            ft, j = idx, which - 2
                            srcq = qT if j == 0 else kT
                            vtiny(lambda e, ft=ft, srcq=srcq: e.reduce_max(
                                small[:, 4:5], srcq[:, ft * T:(ft + 1) * T], axis=AX, apply_absolute_value=True))
                            if ft == 0:
                                vtiny(lambda e, j=j: e.tensor_copy(macc[:, j:j + 1], small[:, 4:5]))
                            else:
                                vtiny(lambda e, j=j: e.tensor_max(macc[:, j:j + 1], macc[:, j:j + 1], small[:, 4:5]))
                            self.i += 1
                            return
                        dst = qT if which == 0 else kT
                        gi = sum(1 for it in self.items[:self.i] if it[0] < 2)
                        self.i += 1
                        ft, c4, par = idx // 4, idx % 4, gi % 2
                        if self.first:
                            WVAL("tensor", "v", v_xnT)
                            self.first = False
                        if idx == 0:
                            WVAL("tensor", "v", self.ready[which])
                        if gi >= 2:
                            WVAL("tensor", "v", self.rev[gi - 2])
                        tsw = op("tensor", lambda e, ft=ft, c4=c4, p=par, dst=dst: e.matmul(
                            ps_w[p][:], swap_bf[:], dst[:, ft * T + c4 * 512: ft * T + (c4 + 1) * 512],
                            start=True, stop=True), inc="t")
                        WVAL("vector", "t", tsw)
                        if gi == 0:
                            WVAL("vector", "qw", qw_cs)
                        op("vector", lambda e, ft=ft, c4=c4, dst=dst: e.tensor_mul(
                            tmp1[:], dst[:, ft * T + c4 * 512: ft * T + (c4 + 1) * 512],
                            cosr[:, c4 * 512:(c4 + 1) * 512]))
                        op("vector", lambda e, c4=c4, p=par: e.tensor_mul(
                            tmp2[:], ps_w[p][:], sinr[:, c4 * 512:(c4 + 1) * 512]))
                        self.rev[gi] = op("vector", lambda e, ft=ft, c4=c4, dst=dst: e.tensor_add(
                            dst[:, ft * T + c4 * 512: ft * T + (c4 + 1) * 512], tmp1[:], tmp2[:]), inc="v")

                    def drain(self):
                        while self.i < len(self.items):
                            before = self.i
                            self.step()
                            if self.i == before:
                                break
                        return self.rev.get(8 * NM - 1, 0)

                rope = Rope()

                def qk_phase(base, dst, wab_free_t, prev_ev_v, which):
                    wb = wab if which % 2 == 0 else wab2
                    qw_t = None
                    for mt in range(NM):
                        WVAL("gpsimd", "t", wab_free_t)
                        qw_t = dma("gpsimd", wb[mt][:], wattn_d[mt * 128:(mt + 1) * 128, base:base + C], "qw")
                    WVAL("tensor", "qw", qw_t)
                    WVAL("tensor", "v", v_xnT)
                    WVAL("tensor", "v", prev_ev_v)
                    last_t = None
                    ev = {}
                    with ExitStack() as pq_s:
                        ps_q = [pq_s.enter_context(nc.psum_tensor(f"psq{base}{i}", [128, 1024], F32)) for i in range(3)]
                        for g in range(2 * NM):
                            ft, half, par = g // 2, g % 2, g % 3
                            if g >= 3:
                                WVAL("tensor", "v", ev[g - 3])
                            for mt in range(NM):
                                for c2 in range(2):
                                    last_t = op("tensor", lambda e, mt=mt, c2=c2, ft=ft, half=half, p=par, wb=wb: e.matmul(
                                        ps_q[p][:, c2 * 512:(c2 + 1) * 512],
                                        wb[mt][:, ft * 128:(ft + 1) * 128],
                                        xnT[:, mt * T + half * 1024 + c2 * 512: mt * T + half * 1024 + (c2 + 1) * 512],
                                        start=(mt == 0), stop=(mt == NM - 1)),
                                        inc="t" if (mt == NM - 1 and c2 == 1) else None, reuse=(c2 == 1))
                            WVAL("vector", "t", last_t)
                            ev[g] = op("vector", lambda e, ft=ft, half=half, p=par: e.tensor_copy(
                                dst[:, ft * T + half * 1024: ft * T + (half + 1) * 1024], ps_q[p][:]), inc="v")
                            if which >= 1:
                                rope.step()
                    return last_t, ev[2 * NM - 1]

                wq_t, q_ev = qk_phase(0, qT, 0, v_xnT, 0)
                rope.ready[0] = q_ev
                mark("q")
                wk_t, k_ev = qk_phase(C, kT, 0, q_ev, 1)
                rope.ready[1] = k_ev
                mark("k")

                def v_phase():
                    wv_t = None
                    for mt in range(NM):
                        WVAL("gpsimd", "t", wq_t)
                        wv_t = dma("gpsimd", wab[mt][:], wattn_d[mt * 128:(mt + 1) * 128, 2 * C:3 * C], "qw")
                    WVAL("tensor", "qw", wv_t)
                    WVAL("tensor", "v", k_ev)
                    vev = {}
                    with ExitStack() as pv_s:
                        ps_v = [pv_s.enter_context(nc.psum_tensor(f"psv{i}", [128, 1024], F32)) for i in range(3)]
                        for st in range(NT):
                            par = st % 3
                            if st >= 3:
                                WVAL("tensor", "v", vev[st - 3])
                            lt = None
                            for mt in range(NM):
                                for c2 in range(2):
                                    lt = op("tensor", lambda e, mt=mt, c2=c2, st=st, p=par: e.matmul(
                                        ps_v[p][:, c2 * 512:(c2 + 1) * 512],
                                        xnT[:, mt * T + st * 128:mt * T + (st + 1) * 128],
                                        wab[mt][:, c2 * 512:(c2 + 1) * 512],
                                        start=(mt == 0), stop=(mt == NM - 1)),
                                        inc="t" if (mt == NM - 1 and c2 == 1) else None, reuse=(c2 == 1))
                            WVAL("vector", "t", lt)
                            vev[st] = op("vector", lambda e, st=st, p=par: e.tensor_copy(
                                vaug[st][:].rearrange("p (h c) -> p h c", h=NH)[:, :, 0:64],
                                ps_v[p][:].rearrange("p (h c) -> p h c", h=NH)), inc="v")
                            rope.step()
                            rope.step()
                    return vev[NT - 1]

                v_v_done = v_phase()
                mark("v")
                q_rope = k_rope = rope.drain()
                mark("rope")

                def bound_phase():
                    v_mx = op("vector", lambda e: e.tensor_copy(mbf[:], macc[:]), inc="v")
                    if True:
                        ps_m = ps_w[0][0:1, 0:256]
                        ps_c = ps_w[1][:, 0:1]
                        WVAL("tensor", "v", v_mx)
                        op("tensor", lambda e: e.matmul(ps_m[0:1, 0:128], mbf[:, 0:1], ident_bf[:], start=True, stop=True))
                        t_mt = op("tensor", lambda e: e.matmul(ps_m[0:1, 128:256], mbf[:, 1:2], ident_bf[:], start=True, stop=True), inc="t")
                        bound_done = {}
                        WVAL("vector", "t", t_mt)
                        vtiny(lambda e: e.reduce_max(lamv[0:1, 2:4], ps_m[0:1, :].rearrange("p (a b) -> p a b", a=2), axis=AX))
                        vtiny(lambda e: e.tensor_mul(lamv[0:1, 1:2], lamv[0:1, 2:3], lamv[0:1, 3:4]))
                        v_cb = op("vector", lambda e: e.tensor_scalar_mul(lamv[0:1, 1:2], lamv[0:1, 1:2], 32.0 * ISQ), inc="v")
                        WVAL("tensor", "v", v_cb)
                        t_cb = op("tensor", lambda e: e.matmul(ps_c[:], onesr_f[:], lamv[0:1, 1:2], start=True, stop=True), inc="t")
                        WVAL("scalar", "t", t_cb)
                        return stiny(lambda e: e.mul(negC[:], ps_c[:], -1.0))

                s_negc = bound_phase()
                mark("bound")

                if dbg_d:
                    dstage = e1(nc.sbuf_tensor("dstage", [128, T], F32))
                    if "xnT" in dbg_d:
                        WVAL("vector", "v", v_xnT)
                        tap([(xnT[:, f * T:(f + 1) * T], dbg_d["xnT"][:, f * T:(f + 1) * T]) for f in range(NM)], T, dstage)
                    if "qT" in dbg_d:
                        WVAL("vector", "v", q_rope)
                        tap([(qT[:, f * T:(f + 1) * T], dbg_d["qT"][:, f * T:(f + 1) * T]) for f in range(NM)], T, dstage)
                    if "kT" in dbg_d:
                        WVAL("vector", "v", k_rope)
                        tap([(kT[:, f * T:(f + 1) * T], dbg_d["kT"][:, f * T:(f + 1) * T]) for f in range(NM)], T, dstage)
                    if "v" in dbg_d:
                        WVAL("vector", "v", v_v_done)
                        WVAL("vector", "g", g_ones)
                        tap([(vaug[st][:], dbg_d["v"][st * 128:(st + 1) * 128, :]) for st in range(NT)], NH * 65, dstage)

            p1_t, p1_v = now("t"), now("v")
            with ExitStack() as p2:
                e2 = p2.enter_context
                Pbuf = [[e2(nc.sbuf_tensor(f"Pbuf{h}{i}", [128, 512], BF16)) for i in range(3)] for h in range(2)]
                a_ev = [e2(nc.sbuf_tensor(f"aev{i}", [65, T], BF16)) for i in range(4)]
                t1f = e2(nc.sbuf_tensor("t1f", [128, 512], F32))
                t2f = e2(nc.sbuf_tensor("t2f", [128, 512], F32))
                dff = e2(nc.sbuf_tensor("dff", [128, 512], F32))
                sqf = e2(nc.sbuf_tensor("sqf", [128, 512], F32))
                dn2 = e2(nc.sbuf_tensor("dn2", [128, T], BF16))
                rab = e2(nc.sbuf_tensor("rab", [128, 32], F32))
                dnstg = e2(nc.sbuf_tensor("dnstg", [128, T], BF16))
                wgn = [e2(nc.sbuf_tensor(f"wgn{i}", [128, C], BF16)) for i in range(2)]
                wun = [e2(nc.sbuf_tensor(f"wun{i}", [128, C], BF16)) for i in range(2)]
                wdn = [e2(nc.sbuf_tensor(f"wdn{i}", [128, C], BF16)) for i in range(2)]
                wgT = [e2(nc.sbuf_tensor(f"wgT{i}", [128, C], BF16)) for i in range(2)]
                wuT = [e2(nc.sbuf_tensor(f"wuT{i}", [128, C], BF16)) for i in range(2)]
                tsil = e2(nc.sbuf_tensor("tsil", [128, C], BF16))
                hstg = [e2(nc.sbuf_tensor(f"hstg{i}", [128, C], BF16)) for i in range(2)]
                wdstg = [e2(nc.sbuf_tensor(f"wdstg{i}", [128, C], BF16)) for i in range(2)]
                psF = e2(nc.psum_tensor("psF", [128, 1024], F32))
                dstage2 = e2(nc.sbuf_tensor("dstage2", [128, T], F32)) if dbg_d else None

                WVAL("tensor", "v", v_v_done)
                WVAL("tensor", "g", g_ones)
                WVAL("tensor", "s", s_negc)

                st8 = {"exp_par": [0, 0], "piece": 0, "dn_t": 0, "dn_ev_s": 0, "aev2": 0,
                       "post_v": 0, "dn_dma": 0, "last_aev": 0}

                class Ffn:

                    def __init__(self):
                        self.units = [(hf, ht) for hf in (0, 1) for ht in range(NHT)]
                        self.ui = 0
                        self.sj = 0
                        self.u = {}
                        self.done_all = False
                        self.calls = 0
                        self.emitted = 0
                        self.rate = (5 * NHT + 4 * NHT) / 816.0

                    def dma_unit(self, i):
                        if i >= len(self.units):
                            return
                        hf, ht = self.units[i]
                        par = i % 2
                        prev = self.u.get(i - 2)
                        if prev is not None and "tr_last" in prev:
                            WVAL("gpsimd", "t", prev["tr_last"])
                        g2 = dma("gpsimd", wgn[par][:], wg_d[ht * 128:(ht + 1) * 128, :], "qw")
                        g2 = dma("gpsimd", wun[par][:], wu_d[ht * 128:(ht + 1) * 128, :], "qw")
                        if hf == 0:
                            g2 = dma("gpsimd", wdn[par][:],
                                     wd_d[:, ht * 128:(ht + 1) * 128].rearrange("(a p) c -> p a c", p=128), "qw")
                        self.u.setdefault(i, {})["dma"] = g2

                    def burst(self, budget):
                        for _ in range(budget):
                            self.step(force=True)

                    def step(self, force=False):
                        if self.done_all:
                            return
                        if not force:
                            self.calls += 1
                            if self.emitted >= self.calls * self.rate:
                                return
                        self.emitted += 1
                        i = self.ui
                        if i >= len(self.units):
                            self.done_all = True
                            return
                        hf, ht = self.units[i]
                        par = i % 2
                        u = self.u.setdefault(i, {})
                        prev = self.u.get(i - 1, {})
                        sj = self.sj
                        steps = ["TRG", "TRU", "TRD", "GATE", "UP"] if hf == 0 else ["TRG", "TRU", "GATE", "UP"]
                        name = steps[sj]
                        if name == "TRG":
                            if i == 0:
                                self.dma_unit(0)
                                self.dma_unit(1)
                            else:
                                self.dma_unit(i + 1)
                            WVAL("tensor", "qw", u["dma"])
                            if prev.get("hct"):
                                WVAL("tensor", "v", prev["hct"])
                            lt = None
                            for m in range(NM):
                                lt = op("tensor", lambda e, m=m, par=par: e.matmul(
                                    psF[:, m * 128:(m + 1) * 128], wgn[par][:, m * 128:(m + 1) * 128],
                                    ident_bf[:], start=True, stop=True),
                                    inc="t" if m == NM - 1 else None)
                            u["tr_last"] = lt
                            WVAL("vector", "t", lt)
                            u["evg"] = op("vector", lambda e, par=par: e.tensor_copy(wgT[par][:], psF[:]), inc="v")
                        elif name == "TRU":
                            WVAL("tensor", "v", u["evg"])
                            lt = None
                            for m in range(NM):
                                lt = op("tensor", lambda e, m=m, par=par: e.matmul(
                                    psF[:, m * 128:(m + 1) * 128], wun[par][:, m * 128:(m + 1) * 128],
                                    ident_bf[:], start=True, stop=True),
                                    inc="t" if m == NM - 1 else None)
                            u["tr_last"] = lt
                            WVAL("vector", "t", lt)
                            u["evu"] = op("vector", lambda e, par=par: e.tensor_copy(wuT[par][:], psF[:]), inc="v")
                        elif name == "TRD":
                            WVAL("tensor", "v", u["evu"])
                            lt = None
                            for m in range(NM):
                                lt = op("tensor", lambda e, m=m, par=par: e.matmul(
                                    psF[:, m * 128:(m + 1) * 128], wdn[par][:, m * 128:(m + 1) * 128],
                                    ident_bf[:], start=True, stop=True),
                                    inc="t" if m == NM - 1 else None)
                            u["tr_last"] = lt
                            WVAL("vector", "t", lt)
                            pp2 = self.u.get(i - 2, {})
                            if pp2.get("wdma"):
                                WVAL("vector", "qs", pp2["wdma"])
                            u["evd"] = op("vector", lambda e, par=par: e.tensor_copy(wdstg[par][:], psF[:]), inc="v")
                            WVAL("sync", "v", u["evd"])
                            u["wdma"] = dma("sync", wdt_d[ht * 128:(ht + 1) * 128, :], wdstg[par][:], "qs")
                        elif name == "GATE":
                            WVAL("tensor", "v", u["evd"] if hf == 0 else u["evu"])
                            lt = None
                            for mt in range(NM):
                                for c2 in range(2):
                                    lt = op("tensor", lambda e, mt=mt, c2=c2, par=par, hf=hf: e.matmul(
                                        psF[:, c2 * 512:(c2 + 1) * 512],
                                        wgT[par][:, mt * 128:(mt + 1) * 128],
                                        xnT[:, mt * T + hf * 1024 + c2 * 512: mt * T + hf * 1024 + (c2 + 1) * 512],
                                        start=(mt == 0), stop=(mt == NM - 1)),
                                        inc="t" if (mt == NM - 1 and c2 == 1) else None, reuse=(c2 == 1))
                            WVAL("scalar", "t", lt)
                            if prev.get("hct"):
                                WVAL("scalar", "v", prev["hct"])
                            u["silu"] = op("scalar", lambda e: e.activation(
                                tsil[:], psF[:], AF.Silu, scale=SCL), inc="s")
                        elif name == "UP":
                            WVAL("tensor", "s", u["silu"])
                            lt = None
                            for mt in range(NM):
                                for c2 in range(2):
                                    lt = op("tensor", lambda e, mt=mt, c2=c2, par=par, hf=hf: e.matmul(
                                        psF[:, c2 * 512:(c2 + 1) * 512],
                                        wuT[par][:, mt * 128:(mt + 1) * 128],
                                        xnT[:, mt * T + hf * 1024 + c2 * 512: mt * T + hf * 1024 + (c2 + 1) * 512],
                                        start=(mt == 0), stop=(mt == NM - 1)),
                                        inc="t" if (mt == NM - 1 and c2 == 1) else None, reuse=(c2 == 1))
                            WVAL("vector", "t", lt)
                            WVAL("vector", "s", u["silu"])
                            pp2 = self.u.get(i - 2, {})
                            if pp2.get("hdma"):
                                WVAL("vector", "qs", pp2["hdma"])
                            u["hct"] = op("vector", lambda e, par=par: e.tensor_mul(
                                hstg[par][:], psF[:], tsil[:]), inc="v")
                            WVAL("sync", "v", u["hct"])
                            u["hdma"] = dma("sync",
                                            hact_d[ht * 128:(ht + 1) * 128, hf * 1024:(hf + 1) * 1024],
                                            hstg[par][:], "qs")
                        self.sj += 1
                        if self.sj >= len(steps):
                            self.sj = 0
                            self.ui += 1

                ffn = Ffn()
                WVAL("gpsimd", "t", p1_t)
                WVAL("gpsimd", "v", p1_v)
                WVAL("gpsimd", "s", s_negc)

                def piece_of(panel, kt):
                    lo = max(512 * panel, 128 * kt)
                    return (lo, 512 * (panel + 1) - lo)

                def emit_heads2(hA, hB, slA, slB, Sp, ATp, prev_last_aev):
                    aA, aB = hA % 4, hB % 4
                    fA, fB = hA // 4, hB // 4
                    gA, gB = (hA % NH) * 65, (hB % NH) * 65
                    par_i = st8.setdefault("lane_pi", [0, 0])
                    exp_cnt = st8.setdefault("lane_exp", [{}, {}])
                    panel_av = [[0] * 4, [0] * 4]
                    aev_prev = prev_last_aev
                    for panel in range(4):
                        kts = list(range(4 * panel + 4))
                        for kx, kt in enumerate(kts):
                            pst, plen = piece_of(panel, kt)
                            has_mask = 128 * kt >= 512 * panel
                            for hx, (a_, f_, sl) in enumerate(((aA, fA, slA), (aB, fB, slB))):
                                pp = par_i[hx] % 2
                                par_i[hx] += 1
                                WVAL("tensor", "s", exp_cnt[hx].get(par_i[hx] - 3, 0))
                                lt = op("tensor", lambda e, a_=a_, f_=f_, kt=kt, pst=pst, plen=plen, hx=hx, pp=pp: e.matmul(
                                    Sp[2 * hx + pp][:, 0:plen],
                                    kT[32 * a_:32 * a_ + 32, f_ * T + kt * 128: f_ * T + (kt + 1) * 128],
                                    qT[32 * a_:32 * a_ + 32, f_ * T + pst: f_ * T + pst + plen],
                                    start=True, stop=True, tile_position=(32 * a_, 0)), inc="t")
                                if has_mask:
                                    WVAL("vector", "t", lt)
                                    vmk = op("vector", lambda e, hx=hx, pp=pp: e.tensor_add(
                                        Sp[2 * hx + pp][:, 0:128], Sp[2 * hx + pp][:, 0:128], mkb[:]), inc="v")
                                    WVAL("scalar", "v", vmk)
                                WVAL("scalar", "t", lt)
                                exp_cnt[hx][par_i[hx] - 1] = op("scalar", lambda e, plen=plen, hx=hx, pp=pp, ci=par_i[hx] - 1: e.activation(
                                    Pbuf[hx][ci % 3][:, 0:plen], Sp[2 * hx + pp][:, 0:plen], AF.Exp,
                                    bias=negC[:], scale=ISQ), inc="s")
                            if kx >= 2:
                                if kx == 2:
                                    WVAL("tensor", "v", aev_prev)
                                ktp = kts[kx - 2]
                                pstp, plenp = piece_of(panel, ktp)
                                for hx, (g_, sl) in enumerate(((gA, slA), (gB, slB))):
                                    ci = par_i[hx] - 1 - 2
                                    WVAL("tensor", "s", exp_cnt[hx][ci])
                                    op("tensor", lambda e, ktp=ktp, pstp=pstp, plenp=plenp, g_=g_, hx=hx, ci=ci, panel=panel: e.matmul(
                                        ATp[hx][0:65, pstp - 512 * panel: pstp - 512 * panel + plenp],
                                        vaug[ktp][:, g_:g_ + 65],
                                        Pbuf[hx][ci % 3][:, 0:plenp],
                                        start=(ktp == 0), stop=(ktp == 4 * panel + 3)), inc="t")
                                ffn.step()
                        for ktp in kts[-2:]:
                            pstp, plenp = piece_of(panel, ktp)
                            for hx, (g_, sl) in enumerate(((gA, slA), (gB, slB))):
                                ci = par_i[hx] - 1 - (len(kts) - 1 - kts.index(ktp))
                                WVAL("tensor", "s", exp_cnt[hx][ci])
                                op("tensor", lambda e, ktp=ktp, pstp=pstp, plenp=plenp, g_=g_, hx=hx, ci=ci, panel=panel: e.matmul(
                                    ATp[hx][0:65, pstp - 512 * panel: pstp - 512 * panel + plenp],
                                    vaug[ktp][:, g_:g_ + 65],
                                    Pbuf[hx][ci % 3][:, 0:plenp],
                                    start=(ktp == 0), stop=(ktp == 4 * panel + 3)), inc="t")
                        t_pv = now("t")
                        WVAL("vector", "t", t_pv)
                        WVAL("vector", "t", st8["dn_t"])
                        for hx, sl in enumerate((slA, slB)):
                            aev_prev = op("vector", lambda e, sl=sl, panel=panel, hx=hx: e.tensor_copy(
                                a_ev[sl][:, 512 * panel:512 * (panel + 1)], ATp[hx][:]), inc="v")
                            panel_av[hx][panel] = aev_prev
                        ffn.step()
                    st8["last_aev"] = aev_prev
                    return panel_av

                def emit_post_pair(pr, sl1, sl2):
                    for hv in range(2):
                        with ExitStack() as po_s:
                            psA = po_s.enter_context(nc.psum_tensor(f"psA{pr}{hv}", [128, 1024], F32))
                            psB = po_s.enter_context(nc.psum_tensor(f"psB{pr}{hv}", [128, 1024], F32))
                            tt = None
                            for qt in range(8):
                                qg = hv * 8 + qt
                                op("tensor", lambda e, qt=qt, qg=qg, psA=psA, sl1=sl1: e.matmul(
                                    psA[:, qt * 128: qt * 128 + 65], a_ev[sl1][:, qg * 128:(qg + 1) * 128],
                                    ident_bf[0:65, 0:65], start=True, stop=True))
                                tt = op("tensor", lambda e, qt=qt, qg=qg, psB=psB, sl2=sl2: e.matmul(
                                    psB[:, qt * 128: qt * 128 + 65], a_ev[sl2][:, qg * 128:(qg + 1) * 128],
                                    ident_bf[0:65, 0:65], start=True, stop=True),
                                    inc="t" if qt == 7 else None)
                            WVAL("vector", "t", tt)
                            pav = psA[:].rearrange("p (a b) -> p a b", a=8)
                            pbv = psB[:].rearrange("p (a b) -> p a b", a=8)
                            vtiny(lambda e, pav=pav: e.reciprocal(rab[:, 0:8], pav[:, :, 64]))
                            vtiny(lambda e, pbv=pbv: e.reciprocal(rab[:, 8:16], pbv[:, :, 64]))
                            vtiny(lambda e: e.tensor_scalar_mul(rab[:, 8:16], rab[:, 8:16], lam_col[:, 0:1]))
                            op("vector", lambda e, pbv=pbv: e.tensor_tensor(
                                t2f[:].rearrange("p (a b) -> p a b", a=8), pbv[:, :, 0:64],
                                rab[:, 8:16].unsqueeze(2).broadcast_to([128, 8, 64]), ALU.mult))
                            op("vector", lambda e, pav=pav: e.tensor_tensor(
                                t1f[:].rearrange("p (a b) -> p a b", a=8), pav[:, :, 0:64],
                                rab[:, 0:8].unsqueeze(2).broadcast_to([128, 8, 64]), ALU.mult))
                            op("vector", lambda e: e.tensor_sub(dff[:], t1f[:], t2f[:]))
                            op("vector", lambda e: e.tensor_mul(sqf[:], dff[:], dff[:]))
                            v_ms = op("vector", lambda e: e.reduce_sum(
                                rab[:, 16:24], sqf[:].rearrange("p (a b) -> p a b", a=8), axis=AX), inc="v")
                            WVAL("scalar", "v", v_ms)
                            stiny(lambda e: e.activation(rab[:, 24:32], rab[:, 16:24], AF.Ln, bias=eps_col, scale=1.0 / VD))
                            s_rstd = op("scalar", lambda e: e.activation(rab[:, 24:32], rab[:, 24:32], AF.Exp, scale=-0.5, bias=ln08_col), inc="s")
                            WVAL("vector", "s", s_rstd)
                            WVAL("vector", "t", st8["dn_t"])
                            v_dn = op("vector", lambda e, pr=pr, hv=hv: e.tensor_tensor(
                                dn2[:].rearrange("p (a j b) -> p a j b", a=NT, j=2)[:, hv * 8:(hv + 1) * 8, pr % 2, :],
                                dff[:].rearrange("p (a b) -> p a b", a=8),
                                rab[:, 24:32].unsqueeze(2).broadcast_to([128, 8, 64]), ALU.mult), inc="v")
                            st8["post_v"] = v_dn
                        ffn.burst(2)
                    return st8["post_v"]

                def emit_group(pp):
                    hA, hB = 2 * pp, 2 * pp + 1
                    with ExitStack() as pp_s:
                        Sp = [pp_s.enter_context(nc.psum_tensor(f"Sg{pp}{i}", [128, 512], F32)) for i in range(4)]
                        ATp = [pp_s.enter_context(nc.psum_tensor(f"ATg{pp}{i}", [65, 512], F32)) for i in range(2)]
                        WVAL("tensor", "s", st8["dn_ev_s"])
                        WVAL("tensor", "v", st8["post_v"])
                        emit_heads2(hA, hB, 0, 1, Sp, ATp, st8["last_aev"])
                        emit_heads2(hA + NH, hB + NH, 2, 3, Sp, ATp, st8["last_aev"])
                    v_dn = emit_post_pair(hA, 0, 2)
                    v_dn = emit_post_pair(hB, 1, 3)
                    m = pp
                    with ExitStack() as pd_s:
                        psD = pd_s.enter_context(nc.psum_tensor(f"psD{pp}", [128, T], F32))
                        WVAL("tensor", "v", v_dn)
                        td = None
                        for qt in range(NT):
                            td = op("tensor", lambda e, qt=qt, psD=psD: e.matmul(
                                psD[:, qt * 128:(qt + 1) * 128], dn2[:, qt * 128:(qt + 1) * 128],
                                ident_bf[:], start=True, stop=True),
                                inc="t" if qt == NT - 1 else None)
                        st8["dn_t"] = td
                        WVAL("scalar", "t", td)
                        WVAL("scalar", "qs", st8["dn_dma"])
                        st8["dn_ev_s"] = op("scalar", lambda e, psD=psD: e.copy(dnstg[:], psD[:]), inc="s")
                        WVAL("sync", "s", st8["dn_ev_s"])
                        st8["dn_dma"] = dma("sync", dnt_d[m * 128:(m + 1) * 128, :], dnstg[:], "qs")
                    ffn.burst(2)

                st8["last_aev"] = 0
                for pp_i in range(8):
                    emit_group(pp_i)
                    mark(f"pair{2 * pp_i + 1}")
                while not ffn.done_all:
                    ffn.step(force=True)
                dn_ev_s_final = st8["dn_ev_s"]
                dn_dma_final = st8["dn_dma"]
                last_hdma = ffn.u[len(ffn.units) - 1]["hdma"]
                last_wdma = ffn.u[NHT - 1]["wdma"]

                if "dnT" in dbg_d:
                    pass

        scope_x.close()
        p2_t, p2_v, p2_s = now("t"), now("v"), now("s")
        mark("p3pre")

        with ExitStack() as p345:
            e4 = p345.enter_context
            hsl = [e4(nc.sbuf_tensor(f"hsl{i}", [128, NHT * 128], BF16)) for i in range(2)]
            wdT = e4(nc.sbuf_tensor("wdT_all", [128, NHT * C], BF16))
            wp = [e4(nc.sbuf_tensor(f"wp{i}", [128, C], BF16)) for i in range(NM)]
            dsl = [e4(nc.sbuf_tensor(f"dsl{i}", [128, NM * 128], BF16)) for i in range(2)]
            xsp = [e4(nc.sbuf_tensor(f"xsp{i}", [128, C], F32)) for i in range(2)]
            opb = [e4(nc.sbuf_tensor(f"opb{i}", [128, C], F32)) for i in range(2)]
            tds = e4(nc.sbuf_tensor("tds", [128, C], F32))
            outb = [e4(nc.sbuf_tensor(f"outb{i}", [128, C], F32)) for i in range(2)]

            for engb in ("gpsimd", "sync"):
                WVAL(engb, "t", p2_t)
                WVAL(engb, "v", p2_v)
                WVAL(engb, "s", p2_s)
            wp_t = None
            for mt in range(NM):
                wp_t = dma("gpsimd", wp[mt][:], wproj_d[mt * 128:(mt + 1) * 128, :], "qw")
            WVAL("gpsimd", "qs", last_wdma)
            wdt_ld = dma("gpsimd", wdT[:].rearrange("p (a c) -> p a c", a=NHT), wdt_d.rearrange("(a p) c -> p a c", p=128), "qw")

            WVAL("tensor", "qw", wp_t)
            WVAL("tensor", "s", dn_ev_s_final)
            out_dma = {}
            comb_v = {}
            opb_v = {}
            dn_ld = {}
            with ExitStack() as pp3:
                ps_p = [pp3.enter_context(nc.psum_tensor(f"psp{i}", [128, 1024], F32)) for i in range(2)]
                ps_d = [pp3.enter_context(nc.psum_tensor(f"psd{i}", [128, 512], F32)) for i in range(4)]
                sevac = {}

                xld_t = {}

                hs_ld = {}

                def issue_loads(st):
                    par = st % 2
                    if st >= 2:
                        WVAL("sync", "v", opb_v[st - 2])
                        WVAL("sync", "s", sevac[st - 2])
                    WVAL("sync", "qs", dn_dma_final)
                    dn_ld[st] = dma("sync", dsl[par][:].rearrange("p (a c) -> p a c", a=NM),
                                    dnt_d[:, st * 128:(st + 1) * 128].rearrange("(a p) c -> p a c", p=128), "qx")
                    if st == 0:
                        WVAL("sync", "qs", last_hdma)
                    hs_ld[st] = dma("sync", hsl[par][:].rearrange("p (a c) -> p a c", a=NHT),
                                    hact_d[:, st * 128:(st + 1) * 128].rearrange("(a p) c -> p a c", p=128), "qx")
                    xld_t[st] = dma("sync", xsp[par][:], x_d[st * 128:(st + 1) * 128, :], "qx")

                def emit_tile(st):
                    par = st % 2
                    xld = xld_t[st]
                    if st >= 2:
                        WVAL("tensor", "v", opb_v[st - 2])
                    WVAL("tensor", "qx", xld)
                    lt = None
                    for mt in range(NM):
                        for c2 in range(2):
                            lt = op("tensor", lambda e, mt=mt, c2=c2, p=par: e.matmul(
                                ps_p[p][:, c2 * 512:(c2 + 1) * 512],
                                dsl[p][:, mt * 128:(mt + 1) * 128],
                                wp[mt][:, c2 * 512:(c2 + 1) * 512],
                                start=(mt == 0), stop=(mt == NM - 1)),
                                inc="t" if mt == NM - 1 and c2 == 1 else None, reuse=(c2 == 1))
                    WVAL("vector", "t", lt)
                    WVAL("vector", "qx", xld)
                    opb_v[st] = op("vector", lambda e, p=par: e.tensor_add(opb[p][:], ps_p[p][:], xsp[p][:]), inc="v")
                    if st >= 2:
                        WVAL("tensor", "s", sevac[st - 2])
                    if st == 0:
                        WVAL("tensor", "qw", wdt_ld)
                    WVAL("tensor", "qx", hs_ld[st])
                    lt = None
                    for ht in range(NHT):
                        for c2 in range(2):
                            lt = op("tensor", lambda e, ht=ht, st=st, c2=c2, dp=par: e.matmul(
                                ps_d[2 * dp + c2][:],
                                hsl[par][:, ht * 128:(ht + 1) * 128],
                                wdT[:, ht * C + c2 * 512: ht * C + (c2 + 1) * 512],
                                start=(ht == 0), stop=(ht == NHT - 1)),
                                inc="t" if (ht == NHT - 1 and c2 == 1) else None, reuse=(c2 == 1))
                    WVAL("scalar", "t", lt)
                    sv = None
                    for c2 in range(2):
                        sv = op("scalar", lambda e, c2=c2, dp=par: e.mul(
                            tds[:, c2 * 512:(c2 + 1) * 512], ps_d[2 * dp + c2][:], SCL * SCL),
                            inc="s" if c2 == 1 else None)
                    sevac[st] = sv
                    if st + 2 < NT:
                        issue_loads(st + 2)
                    WVAL("vector", "s", sevac[st])
                    if st >= 2:
                        WVAL("vector", "o", out_dma[st - 2])
                    comb_v[st] = op("vector", lambda e, p=par: e.tensor_add(outb[p][:], tds[:], opb[p][:]), inc="v")
                    WVAL("sync", "v", comb_v[st])
                    out_dma[st] = dma("sync", out_d[st * 128:(st + 1) * 128, :], outb[par][:], "o")

                issue_loads(0)
                issue_loads(1)
                for st in range(NT):
                    emit_tile(st)
                mark("half1")

            WVAL("sync", "o", now("o"))

        if stop is not None:
            cut = MARKS[stop]
            for k in Q:
                Q[k] = Q[k][:cut[k]]
        with nc.Block() as block:
            block.tensor(lambda e: [fn(e) for fn in Q["tensor"]])
            block.vector(lambda e: [fn(e) for fn in Q["vector"]])
            block.scalar(lambda e: [fn(e) for fn in Q["scalar"]])
            block.gpsimd(lambda e: [fn(e) for fn in Q["gpsimd"]])
            block.sync(lambda e: [fn(e) for fn in Q["sync"]])

    return nc


def _consts():
    ident = np.eye(128, dtype=np.float32)
    swap = np.zeros((128, 128), dtype=np.float32)
    for p in range(128):
        swap[p ^ 1, p] = 1.0
    tri = np.triu(np.ones((128, 128), dtype=np.float32))
    half = HD // 2
    j = (np.arange(128) % 32) // 2
    theta = (10000.0 ** (-(np.arange(half) * 2.0 / HD)))[j]
    ang = theta[:, None] * np.arange(T, dtype=np.float64)[None, :]
    cos = np.cos(ang).astype(np.float32)
    sin = np.sin(ang).astype(np.float32)
    sgn = np.where(np.arange(128) % 2 == 0, -1.0, 1.0).astype(np.float32)
    sinm = (sin * sgn[:, None]).astype(np.float32)
    onesr = np.ones((1, 128), dtype=np.float32)
    maskbig = (np.tril(np.ones((128, 128)), -1) * (-1e9)).astype(np.float32)
    ccols = np.zeros((128, 4), dtype=np.float32)
    ccols[:, 0] = EPS
    ccols[:, 1] = math.log(1.0 - LAM0)
    return dict(ident=ident, swapm=swap, tri=tri, cosr=cos, sinr=sinm, onesr=onesr,
                ccols=ccols, maskbig=maskbig)


def kernel(L0_attn_x, L0_attn_W_attn, L0_attn_W_proj, L0_attn_lambda_q1,
           L0_attn_lambda_k1, L0_attn_lambda_q2, L0_attn_lambda_k2,
           L0_ffn_w_gate, L0_ffn_w_up, L0_ffn_w_down, _debug=(), _nc=None):
    x = np.ascontiguousarray(np.asarray(L0_attn_x, dtype=np.float32))
    lam = np.concatenate([np.asarray(a, dtype=np.float32).reshape(-1) for a in
                          (L0_attn_lambda_q1, L0_attn_lambda_k1,
                           L0_attn_lambda_q2, L0_attn_lambda_k2)]).reshape(1, 4 * HD)
    shared = dict(
        wattn=np.ascontiguousarray(np.asarray(L0_attn_W_attn, dtype=np.float32)),
        wproj=np.ascontiguousarray(np.asarray(L0_attn_W_proj, dtype=np.float32)),
        lambdas=lam,
        wg=np.ascontiguousarray(np.asarray(L0_ffn_w_gate, dtype=np.int32)),
        wu=np.ascontiguousarray(np.asarray(L0_ffn_w_up, dtype=np.int32)),
        wd=np.ascontiguousarray(np.asarray(L0_ffn_w_down, dtype=np.int32)),
        **_consts(),
    )
    nc = _nc if _nc is not None else build_nc(debug=_debug)
    in_maps = []
    for c in range(8):
        m = dict(shared)
        m["x"] = np.ascontiguousarray(x[c].reshape(T, C))
        in_maps.append(m)
    res = run_bass_kernel_spmd(nc, in_maps, core_ids=list(range(8)))
    out = np.stack([np.asarray(r["out"], dtype=np.float32) for r in res.results])
    if _debug:
        return out.reshape(8, T, C), res
    return out.reshape(8, T, C)


# version 15
# speedup vs baseline: 1.2654x; 1.0563x over previous
import dataclasses
import math
import numpy as np
from contextlib import ExitStack

import concourse.bass as bass
import concourse.mybir as mybir
from concourse.bass_utils import run_bass_kernel_spmd

F32 = mybir.dt.float32
BF16 = mybir.dt.bfloat16
I32 = mybir.dt.int32
AF = mybir.ActivationFunctionType
ALU = mybir.AluOpType
AX = mybir.AxisListType.X

T = 2048
C = 1024
NH = 16
HD = 32
VD = 64
HID = 3072
SCL = 1.0 / 127.0
LAM0 = 0.2
EPS = 1e-5
ISQ = 1.0 / math.sqrt(32.0)
NT = T // 128
NM = C // 128
NHT = HID // 128


def build_nc(debug=(), stop=None):
    debug = set(debug)
    nc = bass.Bass("TRN2", target_bir_lowering=False)

    x_d = nc.dram_tensor("x", [T, C], F32, kind="ExternalInput").ap()
    wattn_d = nc.dram_tensor("wattn", [C, 3 * C], F32, kind="ExternalInput").ap()
    wproj_d = nc.dram_tensor("wproj", [C, C], F32, kind="ExternalInput").ap()
    lam_d = nc.dram_tensor("lambdas", [1, 4 * HD], F32, kind="ExternalInput").ap()
    wg_d = nc.dram_tensor("wg", [HID, C], I32, kind="ExternalInput").ap()
    wu_d = nc.dram_tensor("wu", [HID, C], I32, kind="ExternalInput").ap()
    wd_d = nc.dram_tensor("wd", [C, HID], I32, kind="ExternalInput").ap()
    ident_d = nc.dram_tensor("ident", [128, 128], F32, kind="ExternalInput").ap()
    swap_d = nc.dram_tensor("swapm", [128, 128], F32, kind="ExternalInput").ap()
    tri_d = nc.dram_tensor("tri", [128, 128], F32, kind="ExternalInput").ap()
    mkb_d = nc.dram_tensor("maskbig", [128, 128], F32, kind="ExternalInput").ap()
    cos_d = nc.dram_tensor("cosr", [128, T], F32, kind="ExternalInput").ap()
    sinm_d = nc.dram_tensor("sinr", [128, T], F32, kind="ExternalInput").ap()
    onesr_d = nc.dram_tensor("onesr", [1, 128], F32, kind="ExternalInput").ap()
    ccols_d = nc.dram_tensor("ccols", [128, 4], F32, kind="ExternalInput").ap()
    out_d = nc.dram_tensor("out", [T, C], F32, kind="ExternalOutput").ap()
    spill_d = nc.dram_tensor("spill", [T, C], BF16, kind="Internal").ap()
    hact_d = nc.dram_tensor("hacts", [HID, T], BF16, kind="Internal").ap()
    wdt_d = nc.dram_tensor("wdts", [HID, C], BF16, kind="Internal").ap()
    dnt_d = nc.dram_tensor("dnts", [C, T], BF16, kind="Internal").ap()
    dbg_d = {}
    for name, shape in (("xnT", [128, NM * T]), ("qT", [128, NM * T]), ("kT", [128, NM * T]),
                        ("v", [T, NH * 65]), ("aev0", [2 * 65, T]), ("dnT", [128, NM * T]),
                        ("hact", [128, NHT * 1024])):
        if name in debug:
            dbg_d[name] = nc.dram_tensor("dbg_" + name, shape, F32, kind="ExternalOutput").ap()

    Q = {k: [] for k in ("tensor", "vector", "scalar", "gpsimd", "sync")}
    CNT = {}
    MARKS = {}

    def mark(name):
        MARKS[name] = {k: len(v) for k, v in Q.items()}

    with ExitStack() as top:
        et = top.enter_context
        sm = {n: et(nc.semaphore(n)) for n in ("qx", "qw", "qs", "t", "v", "s", "g", "o")}

        def bump(sem, by=1):
            CNT[sem] = CNT.get(sem, 0) + by
            return CNT[sem]

        def now(sem):
            return CNT.get(sem, 0)

        def WVAL(eng, sem, val):
            if val and val > 0:
                Q[eng].append(lambda e, h=sm[sem], v=val: e.wait_ge(h, v))

        def dma(eng, out, in_, sem):
            tgt = bump(sem, 16)
            Q[eng].append(lambda e, o=out, i=in_, h=sm[sem]: e.dma_start(out=o, in_=i).then_inc(h, 16))
            return tgt

        _ldw_cache = {}

        def _swap_ldw(b):
            ins = b.ins
            blk = _ldw_cache.get("blk")
            if blk is not None:
                insts = blk.instructions
                k = len(insts) - 1
                if insts[k].name == ins.name:
                    insts[k] = dataclasses.replace(ins, ldweights=False)
                    return
            for fn_ in nc.m.functions:
                for bb in fn_.blocks:
                    insts = bb.instructions
                    k = len(insts) - 1
                    if k >= 0 and insts[k].name == ins.name:
                        _ldw_cache["blk"] = bb
                        insts[k] = dataclasses.replace(ins, ldweights=False)
                        return
            raise RuntimeError("ldw swap failed")

        def op(eng, fn, inc=None, reuse=False):
            if not reuse:
                if inc is None:
                    Q[eng].append(fn)
                    return None
                tgt = bump(inc)
                Q[eng].append(lambda e, f=fn, h=sm[inc]: f(e).then_inc(h, 1))
                return tgt
            tgt = bump(inc) if inc else None
            h = sm[inc] if inc else None

            def wrapped(e, f=fn, h=h):
                b = f(e)
                if h is not None:
                    b.then_inc(h, 1)
                _swap_ldw(b)
            Q[eng].append(wrapped)
            return tgt

        def vtiny(fn):
            t = op("vector", fn, inc="v")
            WVAL("vector", "v", t)
            return t

        def stiny(fn):
            t = op("scalar", fn, inc="s")
            WVAL("scalar", "s", t)
            return t

        ident_bf = et(nc.sbuf_tensor("ident_bf", [128, 128], BF16))
        swap_bf = et(nc.sbuf_tensor("swap_bf", [128, 128], BF16))
        tri_bf = et(nc.sbuf_tensor("tri_bf", [128, 128], BF16))
        mkb = et(nc.sbuf_tensor("mkb", [128, 128], F32))
        onesr_f = et(nc.sbuf_tensor("onesr_f", [1, 128], F32))
        ccols = et(nc.sbuf_tensor("ccols_sb", [128, 4], F32))
        lam_col = et(nc.sbuf_tensor("lam_col", [128, 1], F32))
        negC = et(nc.sbuf_tensor("negC", [128, 1], F32))
        small = et(nc.sbuf_tensor("small", [128, 8], F32))
        lamv = et(nc.sbuf_tensor("lamv", [1, 8], F32))
        lam_in = et(nc.sbuf_tensor("lam_in", [1, 4 * HD], F32))
        lprod = et(nc.sbuf_tensor("lprod", [1, 2 * HD], F32))
        eps_col = ccols[:, 0:1]
        ln08_col = ccols[:, 1:2]

        def tap(pairs, ncols, stage):
            for src_ap, dram_ap in pairs:
                np_ = src_ap.shape[0]
                vv = op("vector", lambda e, sp=src_ap, st=stage, n=ncols, np_=np_: e.tensor_copy(st[0:np_, 0:n], sp), inc="v")
                WVAL("sync", "v", vv)
                dq = dma("sync", dram_ap, stage[0:np_, 0:ncols], "qs")
                WVAL("vector", "qs", dq)

        dma("sync", onesr_f[:], onesr_d, "qx")
        dma("sync", ccols[:], ccols_d, "qx")
        qx_l = dma("sync", lam_in[:], lam_d, "qx")
        dma("gpsimd", ident_bf[:], ident_d, "qw")
        dma("gpsimd", swap_bf[:], swap_d, "qw")
        dma("gpsimd", tri_bf[:], tri_d, "qw")
        qw_consts = dma("gpsimd", mkb[:], mkb_d, "qw")

        WVAL("vector", "qx", qx_l)
        vtiny(lambda e: e.tensor_mul(lprod[:, 0:HD], lam_in[:, 0:HD], lam_in[:, HD:2 * HD]))
        vtiny(lambda e: e.tensor_mul(lprod[:, HD:2 * HD], lam_in[:, 2 * HD:3 * HD], lam_in[:, 3 * HD:4 * HD]))
        v1 = op("vector", lambda e: e.reduce_sum(lamv[0:1, 0:2], lprod[:].rearrange("p (a b) -> p a b", a=2), axis=AX), inc="v")
        WVAL("scalar", "v", v1)
        s1 = op("scalar", lambda e: e.activation(lamv[0:1, 2:4], lamv[0:1, 0:2], AF.Exp), inc="s")
        WVAL("vector", "s", s1)
        vtiny(lambda e: e.tensor_sub(lamv[0:1, 0:1], lamv[0:1, 2:3], lamv[0:1, 3:4]))
        v_lam = op("vector", lambda e: e.tensor_scalar_add(lamv[0:1, 0:1], lamv[0:1, 0:1], LAM0), inc="v")

        scope_x = ExitStack()
        xnT = scope_x.enter_context(nc.sbuf_tensor("xnT", [128, NM * T], BF16))
        with ExitStack() as p0:
            ep = p0.enter_context
            xst = [ep(nc.sbuf_tensor(f"xst{i}", [128, C], F32)) for i in range(2)]
            sqj = ep(nc.sbuf_tensor("sqj", [128, C], F32))
            xnbp = [ep(nc.sbuf_tensor(f"xnbp{i}", [128, C], BF16)) for i in range(2)]
            ps_tr = [ep(nc.psum_tensor(f"ps_tr{i}", [128, NM * 128], F32)) for i in range(2)]
            ps_misc = ep(nc.psum_tensor("ps_misc", [128, 1], F32))

            WVAL("tensor", "qx", qx_l)
            WVAL("tensor", "v", v_lam)
            t_lam = op("tensor", lambda e: e.matmul(ps_misc[:], onesr_f[:], lamv[0:1, 0:1], start=True, stop=True), inc="t")
            WVAL("vector", "t", t_lam)
            op("vector", lambda e: e.tensor_copy(lam_col[:], ps_misc[:]), inc="v")

            xn_done = [0] * NT
            ev_done = [0] * NT
            rs_done = [0] * NT

            def ln_stats(st):
                par = st % 2
                c0 = 4 * par
                if st >= 2:
                    WVAL("sync", "v", xn_done[st - 2])
                ld = dma("sync", xst[par][:], x_d[st * 128:(st + 1) * 128, :], "qx")
                WVAL("vector", "qx", ld)
                vtiny(lambda e, par=par, c0=c0: e.reduce_sum(small[:, c0:c0 + 1], xst[par][:], axis=AX))
                v_mn = op("vector", lambda e, c0=c0: e.tensor_scalar_mul(small[:, c0 + 1:c0 + 2], small[:, c0:c0 + 1], -1.0 / C), inc="v")
                WVAL("scalar", "v", v_mn)
                op("scalar", lambda e, par=par, c0=c0: e.activation(
                    sqj[:], xst[par][:], AF.Square, bias=small[:, c0 + 1:c0 + 2], accum_out=small[:, c0 + 2:c0 + 3]))
                stiny(lambda e, c0=c0: e.activation(small[:, c0 + 3:c0 + 4], small[:, c0 + 2:c0 + 3], AF.Ln, bias=eps_col, scale=1.0 / C))
                rs_done[st] = op("scalar", lambda e, c0=c0: e.activation(small[:, c0 + 3:c0 + 4], small[:, c0 + 3:c0 + 4], AF.Exp, scale=-0.5), inc="s")

            def ln_finish(st):
                par = st % 2
                c0 = 4 * par
                WVAL("vector", "s", rs_done[st])
                xn_done[st] = op("vector", lambda e, par=par, c0=c0: e.tensor_scalar(
                    xnbp[par][:], xst[par][:], small[:, c0 + 1:c0 + 2], small[:, c0 + 3:c0 + 4],
                    ALU.add, ALU.mult), inc="v")
                WVAL("tensor", "v", xn_done[st])
                WVAL("tensor", "qw", qw_consts)
                if st >= 2:
                    WVAL("tensor", "v", ev_done[st - 2])
                t_tr = None
                for m in range(NM):
                    t_tr = op("tensor", lambda e, m=m, par=par: e.matmul(
                        ps_tr[par][:, m * 128:(m + 1) * 128], xnbp[par][:, m * 128:(m + 1) * 128], ident_bf[:],
                        start=True, stop=True),
                        inc="t" if m == NM - 1 else None)
                WVAL("vector", "t", t_tr)
                ev_done[st] = op("vector", lambda e, st=st, par=par: e.tensor_copy(
                    xnT[:].rearrange("p (m t) -> p m t", m=NM)[:, :, st * 128:(st + 1) * 128],
                    ps_tr[par][:].rearrange("p (m t) -> p m t", m=NM)), inc="v")

            ln_stats(0)
            for st in range(1, NT):
                ln_stats(st)
                ln_finish(st - 1)
            ln_finish(NT - 1)
            v_xnT = ev_done[NT - 1]
            p0_t, p0_s = now("t"), now("s")
            mark("p0")

        with ExitStack() as pA:
            ea = pA.enter_context
            qT = ea(nc.sbuf_tensor("qT", [128, NM * T], BF16))
            kT = ea(nc.sbuf_tensor("kT", [128, NM * T], BF16))
            vaug = [ea(nc.sbuf_tensor(f"vaug{i}", [128, NH * 65], BF16)) for i in range(NT)]

            g_ones = 0
            for st in range(NT):
                g_ones = op("gpsimd", lambda e, st=st: e.memset(
                    vaug[st][:].rearrange("p (h c) -> p h c", h=NH)[:, :, 64:65], 1.0),
                    inc="g" if st == NT - 1 else None)

            with ExitStack() as p1:
                e1 = p1.enter_context
                cosr = e1(nc.sbuf_tensor("cosr_sb", [128, T], BF16))
                sinr = e1(nc.sbuf_tensor("sinr_sb", [128, T], BF16))
                wab = [e1(nc.sbuf_tensor(f"wab{i}", [128, C], BF16)) for i in range(NM)]
                wab2 = [e1(nc.sbuf_tensor(f"wab2{i}", [128, C], BF16)) for i in range(NM)]
                tmp1 = e1(nc.sbuf_tensor("tmp1", [128, 512], F32))
                tmp2 = e1(nc.sbuf_tensor("tmp2", [128, 512], F32))
                macc = e1(nc.sbuf_tensor("macc", [128, 2], F32))
                mbf = e1(nc.sbuf_tensor("mbf", [128, 2], BF16))

                WVAL("gpsimd", "v", v_xnT)
                WVAL("gpsimd", "t", p0_t)
                WVAL("gpsimd", "s", p0_s)
                qw_cs = dma("gpsimd", cosr[:], cos_d, "qw")
                qw_cs = dma("gpsimd", sinr[:], sinm_d, "qw")
                ps_w = [e1(nc.psum_tensor(f"psw{i}", [128, 512], F32)) for i in range(2)]

                class Rope:

                    def __init__(self):
                        self.items = []
                        for w in (0, 1):
                            for i in range(4 * NM):
                                self.items.append((w, i))
                                if i % 4 == 3:
                                    self.items.append((2 + w, i // 4))
                        self.i = 0
                        self.rev = {}
                        self.ready = [0, 0]
                        self.first = True

                    def step(self):
                        if self.i >= len(self.items):
                            return
                        which, idx = self.items[self.i]
                        if self.ready[which % 2] == 0:
                            return
                        if which >= 2:
                            ft, j = idx, which - 2
                            srcq = qT if j == 0 else kT
                            vtiny(lambda e, ft=ft, srcq=srcq: e.reduce_max(
                                small[:, 4:5], srcq[:, ft * T:(ft + 1) * T], axis=AX, apply_absolute_value=True))
                            if ft == 0:
                                vtiny(lambda e, j=j: e.tensor_copy(macc[:, j:j + 1], small[:, 4:5]))
                            else:
                                vtiny(lambda e, j=j: e.tensor_max(macc[:, j:j + 1], macc[:, j:j + 1], small[:, 4:5]))
                            self.i += 1
                            return
                        dst = qT if which == 0 else kT
                        gi = sum(1 for it in self.items[:self.i] if it[0] < 2)
                        self.i += 1
                        ft, c4, par = idx // 4, idx % 4, gi % 2
                        if self.first:
                            WVAL("tensor", "v", v_xnT)
                            self.first = False
                        if idx == 0:
                            WVAL("tensor", "v", self.ready[which])
                        if gi >= 2:
                            WVAL("tensor", "v", self.rev[gi - 2])
                        tsw = op("tensor", lambda e, ft=ft, c4=c4, p=par, dst=dst: e.matmul(
                            ps_w[p][:], swap_bf[:], dst[:, ft * T + c4 * 512: ft * T + (c4 + 1) * 512],
                            start=True, stop=True), inc="t")
                        WVAL("vector", "t", tsw)
                        if gi == 0:
                            WVAL("vector", "qw", qw_cs)
                        op("vector", lambda e, ft=ft, c4=c4, dst=dst: e.tensor_mul(
                            tmp1[:], dst[:, ft * T + c4 * 512: ft * T + (c4 + 1) * 512],
                            cosr[:, c4 * 512:(c4 + 1) * 512]))
                        op("vector", lambda e, c4=c4, p=par: e.tensor_mul(
                            tmp2[:], ps_w[p][:], sinr[:, c4 * 512:(c4 + 1) * 512]))
                        self.rev[gi] = op("vector", lambda e, ft=ft, c4=c4, dst=dst: e.tensor_add(
                            dst[:, ft * T + c4 * 512: ft * T + (c4 + 1) * 512], tmp1[:], tmp2[:]), inc="v")

                    def drain(self):
                        while self.i < len(self.items):
                            before = self.i
                            self.step()
                            if self.i == before:
                                break
                        return self.rev.get(8 * NM - 1, 0)

                rope = Rope()

                def qk_phase(base, dst, wab_free_t, prev_ev_v, which):
                    wb = wab if which % 2 == 0 else wab2
                    qw_t = None
                    for mt in range(NM):
                        WVAL("gpsimd", "t", wab_free_t)
                        qw_t = dma("gpsimd", wb[mt][:], wattn_d[mt * 128:(mt + 1) * 128, base:base + C], "qw")
                    WVAL("tensor", "qw", qw_t)
                    WVAL("tensor", "v", v_xnT)
                    WVAL("tensor", "v", prev_ev_v)
                    last_t = None
                    ev = {}
                    with ExitStack() as pq_s:
                        ps_q = [pq_s.enter_context(nc.psum_tensor(f"psq{base}{i}", [128, 1024], F32)) for i in range(3)]
                        for g in range(2 * NM):
                            ft, half, par = g // 2, g % 2, g % 3
                            if g >= 3:
                                WVAL("tensor", "v", ev[g - 3])
                            for mt in range(NM):
                                for c2 in range(2):
                                    last_t = op("tensor", lambda e, mt=mt, c2=c2, ft=ft, half=half, p=par, wb=wb: e.matmul(
                                        ps_q[p][:, c2 * 512:(c2 + 1) * 512],
                                        wb[mt][:, ft * 128:(ft + 1) * 128],
                                        xnT[:, mt * T + half * 1024 + c2 * 512: mt * T + half * 1024 + (c2 + 1) * 512],
                                        start=(mt == 0), stop=(mt == NM - 1)),
                                        inc="t" if (mt == NM - 1 and c2 == 1) else None, reuse=(c2 == 1))
                            WVAL("vector", "t", last_t)
                            ev[g] = op("vector", lambda e, ft=ft, half=half, p=par: e.tensor_copy(
                                dst[:, ft * T + half * 1024: ft * T + (half + 1) * 1024], ps_q[p][:]), inc="v")
                            if which >= 1:
                                rope.step()
                    return last_t, ev[2 * NM - 1]

                wq_t, q_ev = qk_phase(0, qT, 0, v_xnT, 0)
                rope.ready[0] = q_ev
                mark("q")
                wk_t, k_ev = qk_phase(C, kT, 0, q_ev, 1)
                rope.ready[1] = k_ev
                mark("k")

                def v_phase():
                    wv_t = None
                    for mt in range(NM):
                        WVAL("gpsimd", "t", wq_t)
                        wv_t = dma("gpsimd", wab[mt][:], wattn_d[mt * 128:(mt + 1) * 128, 2 * C:3 * C], "qw")
                    WVAL("tensor", "qw", wv_t)
                    WVAL("tensor", "v", k_ev)
                    vev = {}
                    with ExitStack() as pv_s:
                        ps_v = [pv_s.enter_context(nc.psum_tensor(f"psv{i}", [128, 1024], F32)) for i in range(3)]
                        for st in range(NT):
                            par = st % 3
                            if st >= 3:
                                WVAL("tensor", "v", vev[st - 3])
                            lt = None
                            for mt in range(NM):
                                for c2 in range(2):
                                    lt = op("tensor", lambda e, mt=mt, c2=c2, st=st, p=par: e.matmul(
                                        ps_v[p][:, c2 * 512:(c2 + 1) * 512],
                                        xnT[:, mt * T + st * 128:mt * T + (st + 1) * 128],
                                        wab[mt][:, c2 * 512:(c2 + 1) * 512],
                                        start=(mt == 0), stop=(mt == NM - 1)),
                                        inc="t" if (mt == NM - 1 and c2 == 1) else None, reuse=(c2 == 1))
                            WVAL("vector", "t", lt)
                            vev[st] = op("vector", lambda e, st=st, p=par: e.tensor_copy(
                                vaug[st][:].rearrange("p (h c) -> p h c", h=NH)[:, :, 0:64],
                                ps_v[p][:].rearrange("p (h c) -> p h c", h=NH)), inc="v")
                            rope.step()
                            rope.step()
                    return vev[NT - 1]

                v_v_done = v_phase()
                mark("v")
                q_rope = k_rope = rope.drain()
                mark("rope")

                def bound_phase():
                    v_mx = op("vector", lambda e: e.tensor_copy(mbf[:], macc[:]), inc="v")
                    if True:
                        ps_m = ps_w[0][0:1, 0:256]
                        ps_c = ps_w[1][:, 0:1]
                        WVAL("tensor", "v", v_mx)
                        op("tensor", lambda e: e.matmul(ps_m[0:1, 0:128], mbf[:, 0:1], ident_bf[:], start=True, stop=True))
                        t_mt = op("tensor", lambda e: e.matmul(ps_m[0:1, 128:256], mbf[:, 1:2], ident_bf[:], start=True, stop=True), inc="t")
                        bound_done = {}
                        WVAL("vector", "t", t_mt)
                        vtiny(lambda e: e.reduce_max(lamv[0:1, 2:4], ps_m[0:1, :].rearrange("p (a b) -> p a b", a=2), axis=AX))
                        vtiny(lambda e: e.tensor_mul(lamv[0:1, 1:2], lamv[0:1, 2:3], lamv[0:1, 3:4]))
                        v_cb = op("vector", lambda e: e.tensor_scalar_mul(lamv[0:1, 1:2], lamv[0:1, 1:2], 32.0 * ISQ), inc="v")
                        WVAL("tensor", "v", v_cb)
                        t_cb = op("tensor", lambda e: e.matmul(ps_c[:], onesr_f[:], lamv[0:1, 1:2], start=True, stop=True), inc="t")
                        WVAL("scalar", "t", t_cb)
                        return stiny(lambda e: e.mul(negC[:], ps_c[:], -1.0))

                s_negc = bound_phase()
                mark("bound")

                if dbg_d:
                    dstage = e1(nc.sbuf_tensor("dstage", [128, T], F32))
                    if "xnT" in dbg_d:
                        WVAL("vector", "v", v_xnT)
                        tap([(xnT[:, f * T:(f + 1) * T], dbg_d["xnT"][:, f * T:(f + 1) * T]) for f in range(NM)], T, dstage)
                    if "qT" in dbg_d:
                        WVAL("vector", "v", q_rope)
                        tap([(qT[:, f * T:(f + 1) * T], dbg_d["qT"][:, f * T:(f + 1) * T]) for f in range(NM)], T, dstage)
                    if "kT" in dbg_d:
                        WVAL("vector", "v", k_rope)
                        tap([(kT[:, f * T:(f + 1) * T], dbg_d["kT"][:, f * T:(f + 1) * T]) for f in range(NM)], T, dstage)
                    if "v" in dbg_d:
                        WVAL("vector", "v", v_v_done)
                        WVAL("vector", "g", g_ones)
                        tap([(vaug[st][:], dbg_d["v"][st * 128:(st + 1) * 128, :]) for st in range(NT)], NH * 65, dstage)

            p1_t, p1_v = now("t"), now("v")
            with ExitStack() as p2:
                e2 = p2.enter_context
                Pbuf = [[e2(nc.sbuf_tensor(f"Pbuf{h}{i}", [128, 512], BF16)) for i in range(3)] for h in range(2)]
                a_ev = [e2(nc.sbuf_tensor(f"aev{i}", [65, T], BF16)) for i in range(4)]
                t1f = e2(nc.sbuf_tensor("t1f", [128, 512], F32))
                t2f = e2(nc.sbuf_tensor("t2f", [128, 512], F32))
                dff = e2(nc.sbuf_tensor("dff", [128, 512], F32))
                sqf = e2(nc.sbuf_tensor("sqf", [128, 512], F32))
                dn2 = e2(nc.sbuf_tensor("dn2", [128, T], BF16))
                rab = e2(nc.sbuf_tensor("rab", [128, 32], F32))
                dnstg = e2(nc.sbuf_tensor("dnstg", [128, T], BF16))
                wgn = [e2(nc.sbuf_tensor(f"wgn{i}", [128, C], BF16)) for i in range(2)]
                wun = [e2(nc.sbuf_tensor(f"wun{i}", [128, C], BF16)) for i in range(2)]
                wdn = [e2(nc.sbuf_tensor(f"wdn{i}", [128, C], BF16)) for i in range(2)]
                wgT = [e2(nc.sbuf_tensor(f"wgT{i}", [128, C], BF16)) for i in range(2)]
                wuT = [e2(nc.sbuf_tensor(f"wuT{i}", [128, C], BF16)) for i in range(2)]
                tsil = e2(nc.sbuf_tensor("tsil", [128, C], BF16))
                hstg = [e2(nc.sbuf_tensor(f"hstg{i}", [128, C], BF16)) for i in range(2)]
                wdstg = [e2(nc.sbuf_tensor(f"wdstg{i}", [128, C], BF16)) for i in range(2)]
                psF = e2(nc.psum_tensor("psF", [128, 1024], F32))
                dstage2 = e2(nc.sbuf_tensor("dstage2", [128, T], F32)) if dbg_d else None

                WVAL("tensor", "v", v_v_done)
                WVAL("tensor", "g", g_ones)
                WVAL("tensor", "s", s_negc)

                st8 = {"exp_par": [0, 0], "piece": 0, "dn_t": 0, "dn_ev_s": 0, "aev2": 0,
                       "post_v": 0, "dn_dma": 0, "last_aev": 0}

                class Ffn:

                    def __init__(self):
                        self.units = [(hf, ht) for hf in (0, 1) for ht in range(NHT)]
                        self.ui = 0
                        self.sj = 0
                        self.u = {}
                        self.done_all = False
                        self.calls = 0
                        self.emitted = 0
                        self.rate = (5 * NHT + 4 * NHT) / 580.0

                    def dma_unit(self, i):
                        if i >= len(self.units):
                            return
                        hf, ht = self.units[i]
                        par = i % 2
                        prev = self.u.get(i - 2)
                        if prev is not None and "tr_last" in prev:
                            WVAL("gpsimd", "t", prev["tr_last"])
                        g2 = dma("gpsimd", wgn[par][:], wg_d[ht * 128:(ht + 1) * 128, :], "qw")
                        g2 = dma("gpsimd", wun[par][:], wu_d[ht * 128:(ht + 1) * 128, :], "qw")
                        if hf == 0:
                            g2 = dma("gpsimd", wdn[par][:],
                                     wd_d[:, ht * 128:(ht + 1) * 128].rearrange("(a p) c -> p a c", p=128), "qw")
                        self.u.setdefault(i, {})["dma"] = g2

                    def burst(self, budget):
                        for _ in range(budget):
                            self.step(force=True)

                    def step(self, force=False):
                        if self.done_all:
                            return
                        if not force:
                            self.calls += 1
                            if self.emitted >= self.calls * self.rate:
                                return
                        self.emitted += 1
                        i = self.ui
                        if i >= len(self.units):
                            self.done_all = True
                            return
                        hf, ht = self.units[i]
                        par = i % 2
                        u = self.u.setdefault(i, {})
                        prev = self.u.get(i - 1, {})
                        sj = self.sj
                        steps = ["TRG", "TRU", "TRD", "GATE", "UP"] if hf == 0 else ["TRG", "TRU", "GATE", "UP"]
                        name = steps[sj]
                        if name == "TRG":
                            if i == 0:
                                self.dma_unit(0)
                                self.dma_unit(1)
                            else:
                                self.dma_unit(i + 1)
                            WVAL("tensor", "qw", u["dma"])
                            if prev.get("hct"):
                                WVAL("tensor", "v", prev["hct"])
                            lt = None
                            for m in range(NM):
                                lt = op("tensor", lambda e, m=m, par=par: e.matmul(
                                    psF[:, m * 128:(m + 1) * 128], wgn[par][:, m * 128:(m + 1) * 128],
                                    ident_bf[:], start=True, stop=True),
                                    inc="t" if m == NM - 1 else None)
                            u["tr_last"] = lt
                            WVAL("vector", "t", lt)
                            u["evg"] = op("vector", lambda e, par=par: e.tensor_copy(wgT[par][:], psF[:]), inc="v")
                        elif name == "TRU":
                            WVAL("tensor", "v", u["evg"])
                            lt = None
                            for m in range(NM):
                                lt = op("tensor", lambda e, m=m, par=par: e.matmul(
                                    psF[:, m * 128:(m + 1) * 128], wun[par][:, m * 128:(m + 1) * 128],
                                    ident_bf[:], start=True, stop=True),
                                    inc="t" if m == NM - 1 else None)
                            u["tr_last"] = lt
                            WVAL("vector", "t", lt)
                            u["evu"] = op("vector", lambda e, par=par: e.tensor_copy(wuT[par][:], psF[:]), inc="v")
                        elif name == "TRD":
                            WVAL("tensor", "v", u["evu"])
                            lt = None
                            for m in range(NM):
                                lt = op("tensor", lambda e, m=m, par=par: e.matmul(
                                    psF[:, m * 128:(m + 1) * 128], wdn[par][:, m * 128:(m + 1) * 128],
                                    ident_bf[:], start=True, stop=True),
                                    inc="t" if m == NM - 1 else None)
                            u["tr_last"] = lt
                            WVAL("vector", "t", lt)
                            pp2 = self.u.get(i - 2, {})
                            if pp2.get("wdma"):
                                WVAL("vector", "qs", pp2["wdma"])
                            u["evd"] = op("vector", lambda e, par=par: e.tensor_copy(wdstg[par][:], psF[:]), inc="v")
                            WVAL("sync", "v", u["evd"])
                            u["wdma"] = dma("sync", wdt_d[ht * 128:(ht + 1) * 128, :], wdstg[par][:], "qs")
                        elif name == "GATE":
                            WVAL("tensor", "v", u["evd"] if hf == 0 else u["evu"])
                            lt = None
                            for mt in range(NM):
                                for c2 in range(2):
                                    lt = op("tensor", lambda e, mt=mt, c2=c2, par=par, hf=hf: e.matmul(
                                        psF[:, c2 * 512:(c2 + 1) * 512],
                                        wgT[par][:, mt * 128:(mt + 1) * 128],
                                        xnT[:, mt * T + hf * 1024 + c2 * 512: mt * T + hf * 1024 + (c2 + 1) * 512],
                                        start=(mt == 0), stop=(mt == NM - 1)),
                                        inc="t" if (mt == NM - 1 and c2 == 1) else None, reuse=(c2 == 1))
                            WVAL("scalar", "t", lt)
                            if prev.get("hct"):
                                WVAL("scalar", "v", prev["hct"])
                            u["silu"] = op("scalar", lambda e: e.activation(
                                tsil[:], psF[:], AF.Silu, scale=SCL), inc="s")
                        elif name == "UP":
                            WVAL("tensor", "s", u["silu"])
                            lt = None
                            for mt in range(NM):
                                for c2 in range(2):
                                    lt = op("tensor", lambda e, mt=mt, c2=c2, par=par, hf=hf: e.matmul(
                                        psF[:, c2 * 512:(c2 + 1) * 512],
                                        wuT[par][:, mt * 128:(mt + 1) * 128],
                                        xnT[:, mt * T + hf * 1024 + c2 * 512: mt * T + hf * 1024 + (c2 + 1) * 512],
                                        start=(mt == 0), stop=(mt == NM - 1)),
                                        inc="t" if (mt == NM - 1 and c2 == 1) else None, reuse=(c2 == 1))
                            WVAL("vector", "t", lt)
                            WVAL("vector", "s", u["silu"])
                            pp2 = self.u.get(i - 2, {})
                            if pp2.get("hdma"):
                                WVAL("vector", "qs", pp2["hdma"])
                            u["hct"] = op("vector", lambda e, par=par: e.tensor_mul(
                                hstg[par][:], psF[:], tsil[:]), inc="v")
                            WVAL("sync", "v", u["hct"])
                            u["hdma"] = dma("sync",
                                            hact_d[ht * 128:(ht + 1) * 128, hf * 1024:(hf + 1) * 1024],
                                            hstg[par][:], "qs")
                        self.sj += 1
                        if self.sj >= len(steps):
                            self.sj = 0
                            self.ui += 1

                ffn = Ffn()
                WVAL("gpsimd", "t", p1_t)
                WVAL("gpsimd", "v", p1_v)
                WVAL("gpsimd", "s", s_negc)

                def piece_of(panel, kt):
                    lo = max(512 * panel, 128 * kt)
                    return (lo, 512 * (panel + 1) - lo)

                def emit_heads2(hA, hB, slA, slB, Sp, ATp, prev_last_aev):
                    aA, aB = hA % 4, hB % 4
                    fA, fB = hA // 4, hB // 4
                    gA, gB = (hA % NH) * 65, (hB % NH) * 65
                    par_i = st8.setdefault("lane_pi", [0, 0])
                    exp_cnt = st8.setdefault("lane_exp", [{}, {}])
                    panel_av = [[0] * 4, [0] * 4]
                    aev_prev = prev_last_aev
                    for panel in range(4):
                        kts = list(range(4 * panel + 4))
                        for kx, kt in enumerate(kts):
                            pst, plen = piece_of(panel, kt)
                            has_mask = 128 * kt >= 512 * panel
                            for hx, (a_, f_, sl) in enumerate(((aA, fA, slA), (aB, fB, slB))):
                                pp = par_i[hx] % 2
                                par_i[hx] += 1
                                WVAL("tensor", "s", exp_cnt[hx].get(par_i[hx] - 3, 0))
                                lt = op("tensor", lambda e, a_=a_, f_=f_, kt=kt, pst=pst, plen=plen, hx=hx, pp=pp: e.matmul(
                                    Sp[2 * hx + pp][:, 0:plen],
                                    kT[32 * a_:32 * a_ + 32, f_ * T + kt * 128: f_ * T + (kt + 1) * 128],
                                    qT[32 * a_:32 * a_ + 32, f_ * T + pst: f_ * T + pst + plen],
                                    start=True, stop=True, tile_position=(32 * a_, 0)), inc="t")
                                if has_mask:
                                    WVAL("vector", "t", lt)
                                    vmk = op("vector", lambda e, hx=hx, pp=pp: e.tensor_add(
                                        Sp[2 * hx + pp][:, 0:128], Sp[2 * hx + pp][:, 0:128], mkb[:]), inc="v")
                                    WVAL("scalar", "v", vmk)
                                WVAL("scalar", "t", lt)
                                exp_cnt[hx][par_i[hx] - 1] = op("scalar", lambda e, plen=plen, hx=hx, pp=pp, ci=par_i[hx] - 1: e.activation(
                                    Pbuf[hx][ci % 3][:, 0:plen], Sp[2 * hx + pp][:, 0:plen], AF.Exp,
                                    bias=negC[:], scale=ISQ), inc="s")
                            if kx >= 2:
                                if kx == 2:
                                    WVAL("tensor", "v", aev_prev)
                                ktp = kts[kx - 2]
                                pstp, plenp = piece_of(panel, ktp)
                                for hx, (g_, sl) in enumerate(((gA, slA), (gB, slB))):
                                    ci = par_i[hx] - 1 - 2
                                    WVAL("tensor", "s", exp_cnt[hx][ci])
                                    op("tensor", lambda e, ktp=ktp, pstp=pstp, plenp=plenp, g_=g_, hx=hx, ci=ci, panel=panel: e.matmul(
                                        ATp[hx][0:65, pstp - 512 * panel: pstp - 512 * panel + plenp],
                                        vaug[ktp][:, g_:g_ + 65],
                                        Pbuf[hx][ci % 3][:, 0:plenp],
                                        start=(ktp == 0), stop=(ktp == 4 * panel + 3)), inc="t")
                                ffn.step()
                        for ktp in kts[-2:]:
                            pstp, plenp = piece_of(panel, ktp)
                            for hx, (g_, sl) in enumerate(((gA, slA), (gB, slB))):
                                ci = par_i[hx] - 1 - (len(kts) - 1 - kts.index(ktp))
                                WVAL("tensor", "s", exp_cnt[hx][ci])
                                op("tensor", lambda e, ktp=ktp, pstp=pstp, plenp=plenp, g_=g_, hx=hx, ci=ci, panel=panel: e.matmul(
                                    ATp[hx][0:65, pstp - 512 * panel: pstp - 512 * panel + plenp],
                                    vaug[ktp][:, g_:g_ + 65],
                                    Pbuf[hx][ci % 3][:, 0:plenp],
                                    start=(ktp == 0), stop=(ktp == 4 * panel + 3)), inc="t")
                        t_pv = now("t")
                        WVAL("vector", "t", t_pv)
                        WVAL("vector", "t", st8["dn_t"])
                        for hx, sl in enumerate((slA, slB)):
                            aev_prev = op("vector", lambda e, sl=sl, panel=panel, hx=hx: e.tensor_copy(
                                a_ev[sl][:, 512 * panel:512 * (panel + 1)], ATp[hx][:]), inc="v")
                            panel_av[hx][panel] = aev_prev
                        ffn.step()
                    st8["last_aev"] = aev_prev
                    return panel_av

                def emit_post_pair(pr, sl1, sl2):
                    for hv in range(2):
                        with ExitStack() as po_s:
                            psA = po_s.enter_context(nc.psum_tensor(f"psA{pr}{hv}", [128, 1024], F32))
                            psB = po_s.enter_context(nc.psum_tensor(f"psB{pr}{hv}", [128, 1024], F32))
                            tt = None
                            for qt in range(8):
                                qg = hv * 8 + qt
                                op("tensor", lambda e, qt=qt, qg=qg, psA=psA, sl1=sl1: e.matmul(
                                    psA[:, qt * 128: qt * 128 + 65], a_ev[sl1][:, qg * 128:(qg + 1) * 128],
                                    ident_bf[0:65, 0:65], start=True, stop=True))
                                tt = op("tensor", lambda e, qt=qt, qg=qg, psB=psB, sl2=sl2: e.matmul(
                                    psB[:, qt * 128: qt * 128 + 65], a_ev[sl2][:, qg * 128:(qg + 1) * 128],
                                    ident_bf[0:65, 0:65], start=True, stop=True),
                                    inc="t" if qt == 7 else None)
                            WVAL("vector", "t", tt)
                            pav = psA[:].rearrange("p (a b) -> p a b", a=8)
                            pbv = psB[:].rearrange("p (a b) -> p a b", a=8)
                            vtiny(lambda e, pav=pav: e.reciprocal(rab[:, 0:8], pav[:, :, 64]))
                            vtiny(lambda e, pbv=pbv: e.reciprocal(rab[:, 8:16], pbv[:, :, 64]))
                            vtiny(lambda e: e.tensor_scalar_mul(rab[:, 8:16], rab[:, 8:16], lam_col[:, 0:1]))
                            op("vector", lambda e, pbv=pbv: e.tensor_tensor(
                                t2f[:].rearrange("p (a b) -> p a b", a=8), pbv[:, :, 0:64],
                                rab[:, 8:16].unsqueeze(2).broadcast_to([128, 8, 64]), ALU.mult))
                            op("vector", lambda e, pav=pav: e.tensor_tensor(
                                t1f[:].rearrange("p (a b) -> p a b", a=8), pav[:, :, 0:64],
                                rab[:, 0:8].unsqueeze(2).broadcast_to([128, 8, 64]), ALU.mult))
                            op("vector", lambda e: e.tensor_sub(dff[:], t1f[:], t2f[:]))
                            op("vector", lambda e: e.tensor_mul(sqf[:], dff[:], dff[:]))
                            v_ms = op("vector", lambda e: e.reduce_sum(
                                rab[:, 16:24], sqf[:].rearrange("p (a b) -> p a b", a=8), axis=AX), inc="v")
                            WVAL("scalar", "v", v_ms)
                            stiny(lambda e: e.activation(rab[:, 24:32], rab[:, 16:24], AF.Ln, bias=eps_col, scale=1.0 / VD))
                            s_rstd = op("scalar", lambda e: e.activation(rab[:, 24:32], rab[:, 24:32], AF.Exp, scale=-0.5, bias=ln08_col), inc="s")
                            WVAL("vector", "s", s_rstd)
                            WVAL("vector", "t", st8["dn_t"])
                            v_dn = op("vector", lambda e, pr=pr, hv=hv: e.tensor_tensor(
                                dn2[:].rearrange("p (a j b) -> p a j b", a=NT, j=2)[:, hv * 8:(hv + 1) * 8, pr % 2, :],
                                dff[:].rearrange("p (a b) -> p a b", a=8),
                                rab[:, 24:32].unsqueeze(2).broadcast_to([128, 8, 64]), ALU.mult), inc="v")
                            st8["post_v"] = v_dn
                        ffn.burst(2)
                    return st8["post_v"]

                def emit_group(pp):
                    hA, hB = 2 * pp, 2 * pp + 1
                    with ExitStack() as pp_s:
                        Sp = [pp_s.enter_context(nc.psum_tensor(f"Sg{pp}{i}", [128, 512], F32)) for i in range(4)]
                        ATp = [pp_s.enter_context(nc.psum_tensor(f"ATg{pp}{i}", [65, 512], F32)) for i in range(2)]
                        WVAL("tensor", "s", st8["dn_ev_s"])
                        WVAL("tensor", "v", st8["post_v"])
                        emit_heads2(hA, hB, 0, 1, Sp, ATp, st8["last_aev"])
                        emit_heads2(hA + NH, hB + NH, 2, 3, Sp, ATp, st8["last_aev"])
                    m = pp
                    v_dn = emit_post_pair(hA, 0, 2)
                    v_dn = emit_post_pair(hB, 1, 3)
                    with ExitStack() as pd_s:
                        psD = pd_s.enter_context(nc.psum_tensor(f"psD{pp}", [128, T], F32))
                        WVAL("tensor", "v", v_dn)
                        td = None
                        for qt in range(NT):
                            td = op("tensor", lambda e, qt=qt, psD=psD: e.matmul(
                                psD[:, qt * 128:(qt + 1) * 128], dn2[:, qt * 128:(qt + 1) * 128],
                                ident_bf[:], start=True, stop=True),
                                inc="t" if qt == 7 or qt == NT - 1 else None)
                            if qt == 7:
                                WVAL("scalar", "t", now("t"))
                                WVAL("scalar", "qs", st8["dn_dma"])
                                op("scalar", lambda e, psD=psD: e.copy(dnstg[:, 0:1024], psD[:, 0:1024]))
                        st8["dn_t"] = td
                        WVAL("scalar", "t", td)
                        st8["dn_ev_s"] = op("scalar", lambda e, psD=psD: e.copy(dnstg[:, 1024:2048], psD[:, 1024:2048]), inc="s")
                        WVAL("sync", "s", st8["dn_ev_s"])
                        st8["dn_dma"] = dma("sync", dnt_d[m * 128:(m + 1) * 128, :], dnstg[:], "qs")
                    ffn.burst(2)

                st8["last_aev"] = 0
                for pp_i in range(8):
                    emit_group(pp_i)
                    mark(f"pair{2 * pp_i + 1}")
                while not ffn.done_all:
                    ffn.step(force=True)
                dn_ev_s_final = st8["dn_ev_s"]
                dn_dma_final = st8["dn_dma"]
                last_hdma = ffn.u[len(ffn.units) - 1]["hdma"]
                last_wdma = ffn.u[NHT - 1]["wdma"]

                if "dnT" in dbg_d:
                    pass

        scope_x.close()
        p2_t, p2_v, p2_s = now("t"), now("v"), now("s")
        mark("p3pre")

        with ExitStack() as p345:
            e4 = p345.enter_context
            hsl = [e4(nc.sbuf_tensor(f"hsl{i}", [128, NHT * 128], BF16)) for i in range(2)]
            wdT = e4(nc.sbuf_tensor("wdT_all", [128, NHT * C], BF16))
            wp = [e4(nc.sbuf_tensor(f"wp{i}", [128, C], BF16)) for i in range(NM)]
            dsl = [e4(nc.sbuf_tensor(f"dsl{i}", [128, NM * 128], BF16)) for i in range(2)]
            xsp = [e4(nc.sbuf_tensor(f"xsp{i}", [128, C], F32)) for i in range(2)]
            opb = [e4(nc.sbuf_tensor(f"opb{i}", [128, C], F32)) for i in range(2)]
            tds = e4(nc.sbuf_tensor("tds", [128, C], F32))
            outb = [e4(nc.sbuf_tensor(f"outb{i}", [128, C], F32)) for i in range(2)]

            for engb in ("gpsimd", "sync"):
                WVAL(engb, "t", p2_t)
                WVAL(engb, "v", p2_v)
                WVAL(engb, "s", p2_s)
            wp_t = None
            for mt in range(NM):
                wp_t = dma("gpsimd", wp[mt][:], wproj_d[mt * 128:(mt + 1) * 128, :], "qw")
            WVAL("gpsimd", "qs", last_wdma)
            wdt_ld = dma("gpsimd", wdT[:].rearrange("p (a c) -> p a c", a=NHT), wdt_d.rearrange("(a p) c -> p a c", p=128), "qw")

            WVAL("tensor", "qw", wp_t)
            WVAL("tensor", "s", dn_ev_s_final)
            out_dma = {}
            comb_v = {}
            opb_v = {}
            dn_ld = {}
            with ExitStack() as pp3:
                ps_p = [pp3.enter_context(nc.psum_tensor(f"psp{i}", [128, 1024], F32)) for i in range(2)]
                ps_d = [pp3.enter_context(nc.psum_tensor(f"psd{i}", [128, 512], F32)) for i in range(4)]
                sevac = {}

                xld_t = {}

                hs_ld = {}

                def issue_loads(st):
                    par = st % 2
                    if st >= 2:
                        WVAL("sync", "v", opb_v[st - 2])
                        WVAL("sync", "s", sevac[st - 2])
                    WVAL("sync", "qs", dn_dma_final)
                    dn_ld[st] = dma("sync", dsl[par][:].rearrange("p (a c) -> p a c", a=NM),
                                    dnt_d[:, st * 128:(st + 1) * 128].rearrange("(a p) c -> p a c", p=128), "qx")
                    if st == 0:
                        WVAL("sync", "qs", last_hdma)
                    hs_ld[st] = dma("sync", hsl[par][:].rearrange("p (a c) -> p a c", a=NHT),
                                    hact_d[:, st * 128:(st + 1) * 128].rearrange("(a p) c -> p a c", p=128), "qx")
                    xld_t[st] = dma("sync", xsp[par][:], x_d[st * 128:(st + 1) * 128, :], "qx")

                def emit_tile(st):
                    par = st % 2
                    xld = xld_t[st]
                    if st >= 2:
                        WVAL("tensor", "v", opb_v[st - 2])
                    WVAL("tensor", "qx", xld)
                    lt = None
                    for mt in range(NM):
                        for c2 in range(2):
                            lt = op("tensor", lambda e, mt=mt, c2=c2, p=par: e.matmul(
                                ps_p[p][:, c2 * 512:(c2 + 1) * 512],
                                dsl[p][:, mt * 128:(mt + 1) * 128],
                                wp[mt][:, c2 * 512:(c2 + 1) * 512],
                                start=(mt == 0), stop=(mt == NM - 1)),
                                inc="t" if mt == NM - 1 and c2 == 1 else None, reuse=(c2 == 1))
                    WVAL("vector", "t", lt)
                    WVAL("vector", "qx", xld)
                    opb_v[st] = op("vector", lambda e, p=par: e.tensor_add(opb[p][:], ps_p[p][:], xsp[p][:]), inc="v")
                    if st >= 2:
                        WVAL("tensor", "s", sevac[st - 2])
                    if st == 0:
                        WVAL("tensor", "qw", wdt_ld)
                    WVAL("tensor", "qx", hs_ld[st])
                    lt = None
                    for ht in range(NHT):
                        for c2 in range(2):
                            lt = op("tensor", lambda e, ht=ht, st=st, c2=c2, dp=par: e.matmul(
                                ps_d[2 * dp + c2][:],
                                hsl[par][:, ht * 128:(ht + 1) * 128],
                                wdT[:, ht * C + c2 * 512: ht * C + (c2 + 1) * 512],
                                start=(ht == 0), stop=(ht == NHT - 1)),
                                inc="t" if (ht == NHT - 1 and c2 == 1) else None, reuse=(c2 == 1))
                    WVAL("scalar", "t", lt)
                    sv = None
                    for c2 in range(2):
                        sv = op("scalar", lambda e, c2=c2, dp=par: e.mul(
                            tds[:, c2 * 512:(c2 + 1) * 512], ps_d[2 * dp + c2][:], SCL * SCL),
                            inc="s" if c2 == 1 else None)
                    sevac[st] = sv
                    if st + 2 < NT:
                        issue_loads(st + 2)
                    WVAL("vector", "s", sevac[st])
                    if st >= 2:
                        WVAL("vector", "o", out_dma[st - 2])
                    comb_v[st] = op("vector", lambda e, p=par: e.tensor_add(outb[p][:], tds[:], opb[p][:]), inc="v")
                    WVAL("sync", "v", comb_v[st])
                    out_dma[st] = dma("sync", out_d[st * 128:(st + 1) * 128, :], outb[par][:], "o")

                issue_loads(0)
                issue_loads(1)
                for st in range(NT):
                    emit_tile(st)
                mark("half1")

            WVAL("sync", "o", now("o"))

        if stop is not None:
            cut = MARKS[stop]
            for k in Q:
                Q[k] = Q[k][:cut[k]]
        with nc.Block() as block:
            block.tensor(lambda e: [fn(e) for fn in Q["tensor"]])
            block.vector(lambda e: [fn(e) for fn in Q["vector"]])
            block.scalar(lambda e: [fn(e) for fn in Q["scalar"]])
            block.gpsimd(lambda e: [fn(e) for fn in Q["gpsimd"]])
            block.sync(lambda e: [fn(e) for fn in Q["sync"]])

    return nc


def _consts():
    ident = np.eye(128, dtype=np.float32)
    swap = np.zeros((128, 128), dtype=np.float32)
    for p in range(128):
        swap[p ^ 1, p] = 1.0
    tri = np.triu(np.ones((128, 128), dtype=np.float32))
    half = HD // 2
    j = (np.arange(128) % 32) // 2
    theta = (10000.0 ** (-(np.arange(half) * 2.0 / HD)))[j]
    ang = theta[:, None] * np.arange(T, dtype=np.float64)[None, :]
    cos = np.cos(ang).astype(np.float32)
    sin = np.sin(ang).astype(np.float32)
    sgn = np.where(np.arange(128) % 2 == 0, -1.0, 1.0).astype(np.float32)
    sinm = (sin * sgn[:, None]).astype(np.float32)
    onesr = np.ones((1, 128), dtype=np.float32)
    maskbig = (np.tril(np.ones((128, 128)), -1) * (-1e9)).astype(np.float32)
    ccols = np.zeros((128, 4), dtype=np.float32)
    ccols[:, 0] = EPS
    ccols[:, 1] = math.log(1.0 - LAM0)
    return dict(ident=ident, swapm=swap, tri=tri, cosr=cos, sinr=sinm, onesr=onesr,
                ccols=ccols, maskbig=maskbig)


def kernel(L0_attn_x, L0_attn_W_attn, L0_attn_W_proj, L0_attn_lambda_q1,
           L0_attn_lambda_k1, L0_attn_lambda_q2, L0_attn_lambda_k2,
           L0_ffn_w_gate, L0_ffn_w_up, L0_ffn_w_down, _debug=(), _nc=None):
    x = np.ascontiguousarray(np.asarray(L0_attn_x, dtype=np.float32))
    lam = np.concatenate([np.asarray(a, dtype=np.float32).reshape(-1) for a in
                          (L0_attn_lambda_q1, L0_attn_lambda_k1,
                           L0_attn_lambda_q2, L0_attn_lambda_k2)]).reshape(1, 4 * HD)
    shared = dict(
        wattn=np.ascontiguousarray(np.asarray(L0_attn_W_attn, dtype=np.float32)),
        wproj=np.ascontiguousarray(np.asarray(L0_attn_W_proj, dtype=np.float32)),
        lambdas=lam,
        wg=np.ascontiguousarray(np.asarray(L0_ffn_w_gate, dtype=np.int32)),
        wu=np.ascontiguousarray(np.asarray(L0_ffn_w_up, dtype=np.int32)),
        wd=np.ascontiguousarray(np.asarray(L0_ffn_w_down, dtype=np.int32)),
        **_consts(),
    )
    nc = _nc if _nc is not None else build_nc(debug=_debug)
    in_maps = []
    for c in range(8):
        m = dict(shared)
        m["x"] = np.ascontiguousarray(x[c].reshape(T, C))
        in_maps.append(m)
    res = run_bass_kernel_spmd(nc, in_maps, core_ids=list(range(8)))
    out = np.stack([np.asarray(r["out"], dtype=np.float32) for r in res.results])
    if _debug:
        return out.reshape(8, T, C), res
    return out.reshape(8, T, C)


# version 16
# speedup vs baseline: 1.2763x; 1.0087x over previous
import dataclasses
import math
import numpy as np
from contextlib import ExitStack

import concourse.bass as bass
import concourse.mybir as mybir
from concourse.bass_utils import run_bass_kernel_spmd

F32 = mybir.dt.float32
BF16 = mybir.dt.bfloat16
I32 = mybir.dt.int32
AF = mybir.ActivationFunctionType
ALU = mybir.AluOpType
AX = mybir.AxisListType.X

T = 2048
C = 1024
NH = 16
HD = 32
VD = 64
HID = 3072
SCL = 1.0 / 127.0
LAM0 = 0.2
EPS = 1e-5
ISQ = 1.0 / math.sqrt(32.0)
NT = T // 128
NM = C // 128
NHT = HID // 128


def build_nc(debug=(), stop=None):
    debug = set(debug)
    nc = bass.Bass("TRN2", target_bir_lowering=False)

    x_d = nc.dram_tensor("x", [T, C], F32, kind="ExternalInput").ap()
    wattn_d = nc.dram_tensor("wattn", [C, 3 * C], F32, kind="ExternalInput").ap()
    wproj_d = nc.dram_tensor("wproj", [C, C], F32, kind="ExternalInput").ap()
    lam_d = nc.dram_tensor("lambdas", [1, 4 * HD], F32, kind="ExternalInput").ap()
    wg_d = nc.dram_tensor("wg", [HID, C], I32, kind="ExternalInput").ap()
    wu_d = nc.dram_tensor("wu", [HID, C], I32, kind="ExternalInput").ap()
    wd_d = nc.dram_tensor("wd", [C, HID], I32, kind="ExternalInput").ap()
    ident_d = nc.dram_tensor("ident", [128, 128], F32, kind="ExternalInput").ap()
    swap_d = nc.dram_tensor("swapm", [128, 128], F32, kind="ExternalInput").ap()
    tri_d = nc.dram_tensor("tri", [128, 128], F32, kind="ExternalInput").ap()
    mkb_d = nc.dram_tensor("maskbig", [128, 128], F32, kind="ExternalInput").ap()
    cos_d = nc.dram_tensor("cosr", [128, T], F32, kind="ExternalInput").ap()
    sinm_d = nc.dram_tensor("sinr", [128, T], F32, kind="ExternalInput").ap()
    onesr_d = nc.dram_tensor("onesr", [1, 128], F32, kind="ExternalInput").ap()
    ccols_d = nc.dram_tensor("ccols", [128, 4], F32, kind="ExternalInput").ap()
    out_d = nc.dram_tensor("out", [T, C], F32, kind="ExternalOutput").ap()
    spill_d = nc.dram_tensor("spill", [T, C], BF16, kind="Internal").ap()
    hact_d = nc.dram_tensor("hacts", [HID, T], BF16, kind="Internal").ap()
    wdt_d = nc.dram_tensor("wdts", [HID, C], BF16, kind="Internal").ap()
    dnt_d = nc.dram_tensor("dnts", [C, T], BF16, kind="Internal").ap()
    dbg_d = {}
    for name, shape in (("xnT", [128, NM * T]), ("qT", [128, NM * T]), ("kT", [128, NM * T]),
                        ("v", [T, NH * 65]), ("aev0", [2 * 65, T]), ("dnT", [128, NM * T]),
                        ("hact", [128, NHT * 1024])):
        if name in debug:
            dbg_d[name] = nc.dram_tensor("dbg_" + name, shape, F32, kind="ExternalOutput").ap()

    Q = {k: [] for k in ("tensor", "vector", "scalar", "gpsimd", "sync")}
    CNT = {}
    MARKS = {}

    def mark(name):
        MARKS[name] = {k: len(v) for k, v in Q.items()}

    with ExitStack() as top:
        et = top.enter_context
        sm = {n: et(nc.semaphore(n)) for n in ("qx", "qw", "qs", "t", "v", "s", "g", "o")}

        def bump(sem, by=1):
            CNT[sem] = CNT.get(sem, 0) + by
            return CNT[sem]

        def now(sem):
            return CNT.get(sem, 0)

        def WVAL(eng, sem, val):
            if val and val > 0:
                Q[eng].append(lambda e, h=sm[sem], v=val: e.wait_ge(h, v))

        def dma(eng, out, in_, sem):
            tgt = bump(sem, 16)
            Q[eng].append(lambda e, o=out, i=in_, h=sm[sem]: e.dma_start(out=o, in_=i).then_inc(h, 16))
            return tgt

        _ldw_cache = {}

        def _swap_ldw(b):
            ins = b.ins
            blk = _ldw_cache.get("blk")
            if blk is not None:
                insts = blk.instructions
                k = len(insts) - 1
                if insts[k].name == ins.name:
                    insts[k] = dataclasses.replace(ins, ldweights=False)
                    return
            for fn_ in nc.m.functions:
                for bb in fn_.blocks:
                    insts = bb.instructions
                    k = len(insts) - 1
                    if k >= 0 and insts[k].name == ins.name:
                        _ldw_cache["blk"] = bb
                        insts[k] = dataclasses.replace(ins, ldweights=False)
                        return
            raise RuntimeError("ldw swap failed")

        def op(eng, fn, inc=None, reuse=False):
            if not reuse:
                if inc is None:
                    Q[eng].append(fn)
                    return None
                tgt = bump(inc)
                Q[eng].append(lambda e, f=fn, h=sm[inc]: f(e).then_inc(h, 1))
                return tgt
            tgt = bump(inc) if inc else None
            h = sm[inc] if inc else None

            def wrapped(e, f=fn, h=h):
                b = f(e)
                if h is not None:
                    b.then_inc(h, 1)
                _swap_ldw(b)
            Q[eng].append(wrapped)
            return tgt

        def vtiny(fn):
            t = op("vector", fn, inc="v")
            WVAL("vector", "v", t)
            return t

        def stiny(fn):
            t = op("scalar", fn, inc="s")
            WVAL("scalar", "s", t)
            return t

        ident_bf = et(nc.sbuf_tensor("ident_bf", [128, 128], BF16))
        swap_bf = et(nc.sbuf_tensor("swap_bf", [128, 128], BF16))
        tri_bf = et(nc.sbuf_tensor("tri_bf", [128, 128], BF16))
        mkb = et(nc.sbuf_tensor("mkb", [128, 128], F32))
        onesr_f = et(nc.sbuf_tensor("onesr_f", [1, 128], F32))
        ccols = et(nc.sbuf_tensor("ccols_sb", [128, 4], F32))
        lam_col = et(nc.sbuf_tensor("lam_col", [128, 1], F32))
        negC = et(nc.sbuf_tensor("negC", [128, 1], F32))
        small = et(nc.sbuf_tensor("small", [128, 8], F32))
        lamv = et(nc.sbuf_tensor("lamv", [1, 8], F32))
        lam_in = et(nc.sbuf_tensor("lam_in", [1, 4 * HD], F32))
        lprod = et(nc.sbuf_tensor("lprod", [1, 2 * HD], F32))
        eps_col = ccols[:, 0:1]
        ln08_col = ccols[:, 1:2]

        def tap(pairs, ncols, stage):
            for src_ap, dram_ap in pairs:
                np_ = src_ap.shape[0]
                vv = op("vector", lambda e, sp=src_ap, st=stage, n=ncols, np_=np_: e.tensor_copy(st[0:np_, 0:n], sp), inc="v")
                WVAL("sync", "v", vv)
                dq = dma("sync", dram_ap, stage[0:np_, 0:ncols], "qs")
                WVAL("vector", "qs", dq)

        dma("sync", onesr_f[:], onesr_d, "qx")
        dma("sync", ccols[:], ccols_d, "qx")
        qx_l = dma("sync", lam_in[:], lam_d, "qx")
        dma("gpsimd", ident_bf[:], ident_d, "qw")
        dma("gpsimd", swap_bf[:], swap_d, "qw")
        dma("gpsimd", tri_bf[:], tri_d, "qw")
        qw_consts = dma("gpsimd", mkb[:], mkb_d, "qw")

        WVAL("vector", "qx", qx_l)
        vtiny(lambda e: e.tensor_mul(lprod[:, 0:HD], lam_in[:, 0:HD], lam_in[:, HD:2 * HD]))
        vtiny(lambda e: e.tensor_mul(lprod[:, HD:2 * HD], lam_in[:, 2 * HD:3 * HD], lam_in[:, 3 * HD:4 * HD]))
        v1 = op("vector", lambda e: e.reduce_sum(lamv[0:1, 0:2], lprod[:].rearrange("p (a b) -> p a b", a=2), axis=AX), inc="v")
        WVAL("scalar", "v", v1)
        s1 = op("scalar", lambda e: e.activation(lamv[0:1, 2:4], lamv[0:1, 0:2], AF.Exp), inc="s")
        WVAL("vector", "s", s1)
        vtiny(lambda e: e.tensor_sub(lamv[0:1, 0:1], lamv[0:1, 2:3], lamv[0:1, 3:4]))
        v_lam = op("vector", lambda e: e.tensor_scalar_add(lamv[0:1, 0:1], lamv[0:1, 0:1], LAM0), inc="v")

        scope_x = ExitStack()
        xnT = scope_x.enter_context(nc.sbuf_tensor("xnT", [128, NM * T], BF16))
        with ExitStack() as p0:
            ep = p0.enter_context
            xst = [ep(nc.sbuf_tensor(f"xst{i}", [128, C], F32)) for i in range(2)]
            sqj = ep(nc.sbuf_tensor("sqj", [128, C], F32))
            xnbp = [ep(nc.sbuf_tensor(f"xnbp{i}", [128, C], BF16)) for i in range(2)]
            ps_tr = [ep(nc.psum_tensor(f"ps_tr{i}", [128, NM * 128], F32)) for i in range(2)]
            ps_misc = ep(nc.psum_tensor("ps_misc", [128, 1], F32))

            WVAL("tensor", "qx", qx_l)
            WVAL("tensor", "v", v_lam)
            t_lam = op("tensor", lambda e: e.matmul(ps_misc[:], onesr_f[:], lamv[0:1, 0:1], start=True, stop=True), inc="t")
            WVAL("vector", "t", t_lam)
            op("vector", lambda e: e.tensor_copy(lam_col[:], ps_misc[:]), inc="v")

            xn_done = [0] * NT
            ev_done = [0] * NT
            rs_done = [0] * NT

            def ln_stats(st):
                par = st % 2
                c0 = 4 * par
                if st >= 2:
                    WVAL("sync", "v", xn_done[st - 2])
                ld = dma("sync", xst[par][:], x_d[st * 128:(st + 1) * 128, :], "qx")
                WVAL("vector", "qx", ld)
                vtiny(lambda e, par=par, c0=c0: e.reduce_sum(small[:, c0:c0 + 1], xst[par][:], axis=AX))
                v_mn = op("vector", lambda e, c0=c0: e.tensor_scalar_mul(small[:, c0 + 1:c0 + 2], small[:, c0:c0 + 1], -1.0 / C), inc="v")
                WVAL("scalar", "v", v_mn)
                op("scalar", lambda e, par=par, c0=c0: e.activation(
                    sqj[:], xst[par][:], AF.Square, bias=small[:, c0 + 1:c0 + 2], accum_out=small[:, c0 + 2:c0 + 3]))
                stiny(lambda e, c0=c0: e.activation(small[:, c0 + 3:c0 + 4], small[:, c0 + 2:c0 + 3], AF.Ln, bias=eps_col, scale=1.0 / C))
                rs_done[st] = op("scalar", lambda e, c0=c0: e.activation(small[:, c0 + 3:c0 + 4], small[:, c0 + 3:c0 + 4], AF.Exp, scale=-0.5), inc="s")

            def ln_finish(st):
                par = st % 2
                c0 = 4 * par
                WVAL("vector", "s", rs_done[st])
                xn_done[st] = op("vector", lambda e, par=par, c0=c0: e.tensor_scalar(
                    xnbp[par][:], xst[par][:], small[:, c0 + 1:c0 + 2], small[:, c0 + 3:c0 + 4],
                    ALU.add, ALU.mult), inc="v")
                WVAL("tensor", "v", xn_done[st])
                WVAL("tensor", "qw", qw_consts)
                if st >= 2:
                    WVAL("tensor", "v", ev_done[st - 2])
                t_tr = None
                for m in range(NM):
                    t_tr = op("tensor", lambda e, m=m, par=par: e.matmul(
                        ps_tr[par][:, m * 128:(m + 1) * 128], xnbp[par][:, m * 128:(m + 1) * 128], ident_bf[:],
                        start=True, stop=True),
                        inc="t" if m == NM - 1 else None)
                WVAL("vector", "t", t_tr)
                ev_done[st] = op("vector", lambda e, st=st, par=par: e.tensor_copy(
                    xnT[:].rearrange("p (m t) -> p m t", m=NM)[:, :, st * 128:(st + 1) * 128],
                    ps_tr[par][:].rearrange("p (m t) -> p m t", m=NM)), inc="v")

            ln_stats(0)
            for st in range(1, NT):
                ln_stats(st)
                ln_finish(st - 1)
            ln_finish(NT - 1)
            v_xnT = ev_done[NT - 1]
            p0_t, p0_s = now("t"), now("s")
            mark("p0")

        with ExitStack() as pA:
            ea = pA.enter_context
            qT = ea(nc.sbuf_tensor("qT", [128, NM * T], BF16))
            kT = ea(nc.sbuf_tensor("kT", [128, NM * T], BF16))
            vaug = [ea(nc.sbuf_tensor(f"vaug{i}", [128, NH * 65], BF16)) for i in range(NT)]

            g_ones = 0
            for st in range(NT):
                g_ones = op("gpsimd", lambda e, st=st: e.memset(
                    vaug[st][:].rearrange("p (h c) -> p h c", h=NH)[:, :, 64:65], 1.0),
                    inc="g" if st == NT - 1 else None)

            with ExitStack() as p1:
                e1 = p1.enter_context
                cosr = e1(nc.sbuf_tensor("cosr_sb", [128, T], BF16))
                sinr = e1(nc.sbuf_tensor("sinr_sb", [128, T], BF16))
                wab = [e1(nc.sbuf_tensor(f"wab{i}", [128, C], BF16)) for i in range(NM)]
                wab2 = [e1(nc.sbuf_tensor(f"wab2{i}", [128, C], BF16)) for i in range(NM)]
                tmp1 = e1(nc.sbuf_tensor("tmp1", [128, 512], F32))
                tmp2 = e1(nc.sbuf_tensor("tmp2", [128, 512], F32))
                macc = e1(nc.sbuf_tensor("macc", [128, 2], F32))
                mbf = e1(nc.sbuf_tensor("mbf", [128, 2], BF16))

                WVAL("gpsimd", "v", v_xnT)
                WVAL("gpsimd", "t", p0_t)
                WVAL("gpsimd", "s", p0_s)
                qw_cs = dma("gpsimd", cosr[:], cos_d, "qw")
                qw_cs = dma("gpsimd", sinr[:], sinm_d, "qw")
                ps_w = [e1(nc.psum_tensor(f"psw{i}", [128, 512], F32)) for i in range(2)]

                class Rope:

                    def __init__(self):
                        self.items = []
                        for w in (0, 1):
                            for i in range(4 * NM):
                                self.items.append((w, i))
                                if i % 4 == 3:
                                    self.items.append((2 + w, i // 4))
                        self.i = 0
                        self.rev = {}
                        self.ready = [0, 0]
                        self.first = True

                    def step(self):
                        if self.i >= len(self.items):
                            return
                        which, idx = self.items[self.i]
                        if self.ready[which % 2] == 0:
                            return
                        if which >= 2:
                            ft, j = idx, which - 2
                            srcq = qT if j == 0 else kT
                            vtiny(lambda e, ft=ft, srcq=srcq: e.reduce_max(
                                small[:, 4:5], srcq[:, ft * T:(ft + 1) * T], axis=AX, apply_absolute_value=True))
                            if ft == 0:
                                vtiny(lambda e, j=j: e.tensor_copy(macc[:, j:j + 1], small[:, 4:5]))
                            else:
                                vtiny(lambda e, j=j: e.tensor_max(macc[:, j:j + 1], macc[:, j:j + 1], small[:, 4:5]))
                            self.i += 1
                            return
                        dst = qT if which == 0 else kT
                        gi = sum(1 for it in self.items[:self.i] if it[0] < 2)
                        self.i += 1
                        ft, c4, par = idx // 4, idx % 4, gi % 2
                        if self.first:
                            WVAL("tensor", "v", v_xnT)
                            self.first = False
                        if idx == 0:
                            WVAL("tensor", "v", self.ready[which])
                        if gi >= 2:
                            WVAL("tensor", "v", self.rev[gi - 2])
                        tsw = op("tensor", lambda e, ft=ft, c4=c4, p=par, dst=dst: e.matmul(
                            ps_w[p][:], swap_bf[:], dst[:, ft * T + c4 * 512: ft * T + (c4 + 1) * 512],
                            start=True, stop=True), inc="t")
                        WVAL("vector", "t", tsw)
                        if gi == 0:
                            WVAL("vector", "qw", qw_cs)
                        op("vector", lambda e, ft=ft, c4=c4, dst=dst: e.tensor_mul(
                            tmp1[:], dst[:, ft * T + c4 * 512: ft * T + (c4 + 1) * 512],
                            cosr[:, c4 * 512:(c4 + 1) * 512]))
                        op("vector", lambda e, c4=c4, p=par: e.tensor_mul(
                            tmp2[:], ps_w[p][:], sinr[:, c4 * 512:(c4 + 1) * 512]))
                        self.rev[gi] = op("vector", lambda e, ft=ft, c4=c4, dst=dst: e.tensor_add(
                            dst[:, ft * T + c4 * 512: ft * T + (c4 + 1) * 512], tmp1[:], tmp2[:]), inc="v")

                    def drain(self):
                        while self.i < len(self.items):
                            before = self.i
                            self.step()
                            if self.i == before:
                                break
                        return self.rev.get(8 * NM - 1, 0)

                rope = Rope()

                def qk_phase(base, dst, wab_free_t, prev_ev_v, which):
                    wb = wab if which % 2 == 0 else wab2
                    qw_t = None
                    for mt in range(NM):
                        WVAL("gpsimd", "t", wab_free_t)
                        qw_t = dma("gpsimd", wb[mt][:], wattn_d[mt * 128:(mt + 1) * 128, base:base + C], "qw")
                    WVAL("tensor", "qw", qw_t)
                    WVAL("tensor", "v", v_xnT)
                    WVAL("tensor", "v", prev_ev_v)
                    last_t = None
                    ev = {}
                    with ExitStack() as pq_s:
                        ps_q = [pq_s.enter_context(nc.psum_tensor(f"psq{base}{i}", [128, 1024], F32)) for i in range(3)]
                        for g in range(2 * NM):
                            ft, half, par = g // 2, g % 2, g % 3
                            if g >= 3:
                                WVAL("tensor", "v", ev[g - 3])
                            for mt in range(NM):
                                for c2 in range(2):
                                    last_t = op("tensor", lambda e, mt=mt, c2=c2, ft=ft, half=half, p=par, wb=wb: e.matmul(
                                        ps_q[p][:, c2 * 512:(c2 + 1) * 512],
                                        wb[mt][:, ft * 128:(ft + 1) * 128],
                                        xnT[:, mt * T + half * 1024 + c2 * 512: mt * T + half * 1024 + (c2 + 1) * 512],
                                        start=(mt == 0), stop=(mt == NM - 1)),
                                        inc="t" if (mt == NM - 1 and c2 == 1) else None, reuse=(c2 == 1))
                            WVAL("vector", "t", last_t)
                            ev[g] = op("vector", lambda e, ft=ft, half=half, p=par: e.tensor_copy(
                                dst[:, ft * T + half * 1024: ft * T + (half + 1) * 1024], ps_q[p][:]), inc="v")
                            if which >= 1:
                                rope.step()
                    return last_t, ev[2 * NM - 1]

                wq_t, q_ev = qk_phase(0, qT, 0, v_xnT, 0)
                rope.ready[0] = q_ev
                mark("q")
                wk_t, k_ev = qk_phase(C, kT, 0, q_ev, 1)
                rope.ready[1] = k_ev
                mark("k")

                def v_phase():
                    wv_t = None
                    for mt in range(NM):
                        WVAL("gpsimd", "t", wq_t)
                        wv_t = dma("gpsimd", wab[mt][:], wattn_d[mt * 128:(mt + 1) * 128, 2 * C:3 * C], "qw")
                    WVAL("tensor", "qw", wv_t)
                    WVAL("tensor", "v", k_ev)
                    vev = {}
                    with ExitStack() as pv_s:
                        ps_v = [pv_s.enter_context(nc.psum_tensor(f"psv{i}", [128, 1024], F32)) for i in range(3)]
                        for st in range(NT):
                            par = st % 3
                            if st >= 3:
                                WVAL("tensor", "v", vev[st - 3])
                            lt = None
                            for mt in range(NM):
                                for c2 in range(2):
                                    lt = op("tensor", lambda e, mt=mt, c2=c2, st=st, p=par: e.matmul(
                                        ps_v[p][:, c2 * 512:(c2 + 1) * 512],
                                        xnT[:, mt * T + st * 128:mt * T + (st + 1) * 128],
                                        wab[mt][:, c2 * 512:(c2 + 1) * 512],
                                        start=(mt == 0), stop=(mt == NM - 1)),
                                        inc="t" if (mt == NM - 1 and c2 == 1) else None, reuse=(c2 == 1))
                            WVAL("vector", "t", lt)
                            vev[st] = op("vector", lambda e, st=st, p=par: e.tensor_copy(
                                vaug[st][:].rearrange("p (h c) -> p h c", h=NH)[:, :, 0:64],
                                ps_v[p][:].rearrange("p (h c) -> p h c", h=NH)), inc="v")
                            rope.step()
                            rope.step()
                    return vev[NT - 1]

                v_v_done = v_phase()
                mark("v")
                q_rope = k_rope = rope.drain()
                mark("rope")

                def bound_phase():
                    v_mx = op("vector", lambda e: e.tensor_copy(mbf[:], macc[:]), inc="v")
                    if True:
                        ps_m = ps_w[0][0:1, 0:256]
                        ps_c = ps_w[1][:, 0:1]
                        WVAL("tensor", "v", v_mx)
                        op("tensor", lambda e: e.matmul(ps_m[0:1, 0:128], mbf[:, 0:1], ident_bf[:], start=True, stop=True))
                        t_mt = op("tensor", lambda e: e.matmul(ps_m[0:1, 128:256], mbf[:, 1:2], ident_bf[:], start=True, stop=True), inc="t")
                        bound_done = {}
                        WVAL("vector", "t", t_mt)
                        vtiny(lambda e: e.reduce_max(lamv[0:1, 2:4], ps_m[0:1, :].rearrange("p (a b) -> p a b", a=2), axis=AX))
                        vtiny(lambda e: e.tensor_mul(lamv[0:1, 1:2], lamv[0:1, 2:3], lamv[0:1, 3:4]))
                        v_cb = op("vector", lambda e: e.tensor_scalar_mul(lamv[0:1, 1:2], lamv[0:1, 1:2], 32.0 * ISQ), inc="v")
                        WVAL("tensor", "v", v_cb)
                        t_cb = op("tensor", lambda e: e.matmul(ps_c[:], onesr_f[:], lamv[0:1, 1:2], start=True, stop=True), inc="t")
                        WVAL("scalar", "t", t_cb)
                        return stiny(lambda e: e.mul(negC[:], ps_c[:], -1.0))

                s_negc = bound_phase()
                mark("bound")

                if dbg_d:
                    dstage = e1(nc.sbuf_tensor("dstage", [128, T], F32))
                    if "xnT" in dbg_d:
                        WVAL("vector", "v", v_xnT)
                        tap([(xnT[:, f * T:(f + 1) * T], dbg_d["xnT"][:, f * T:(f + 1) * T]) for f in range(NM)], T, dstage)
                    if "qT" in dbg_d:
                        WVAL("vector", "v", q_rope)
                        tap([(qT[:, f * T:(f + 1) * T], dbg_d["qT"][:, f * T:(f + 1) * T]) for f in range(NM)], T, dstage)
                    if "kT" in dbg_d:
                        WVAL("vector", "v", k_rope)
                        tap([(kT[:, f * T:(f + 1) * T], dbg_d["kT"][:, f * T:(f + 1) * T]) for f in range(NM)], T, dstage)
                    if "v" in dbg_d:
                        WVAL("vector", "v", v_v_done)
                        WVAL("vector", "g", g_ones)
                        tap([(vaug[st][:], dbg_d["v"][st * 128:(st + 1) * 128, :]) for st in range(NT)], NH * 65, dstage)

            p1_t, p1_v = now("t"), now("v")
            with ExitStack() as p2:
                e2 = p2.enter_context
                Pbuf = [e2(nc.sbuf_tensor(f"Pbuf{i}", [128, 1024], BF16)) for i in range(3)]
                a_ev = [e2(nc.sbuf_tensor(f"aev{i}", [65, T], BF16)) for i in range(4)]
                t1f = e2(nc.sbuf_tensor("t1f", [128, 512], F32))
                t2f = e2(nc.sbuf_tensor("t2f", [128, 512], F32))
                dff = e2(nc.sbuf_tensor("dff", [128, 512], F32))
                sqf = e2(nc.sbuf_tensor("sqf", [128, 512], F32))
                dn2 = e2(nc.sbuf_tensor("dn2", [128, T], BF16))
                rab = e2(nc.sbuf_tensor("rab", [128, 32], F32))
                dnstg = e2(nc.sbuf_tensor("dnstg", [128, T], BF16))
                wgn = [e2(nc.sbuf_tensor(f"wgn{i}", [128, C], BF16)) for i in range(2)]
                wun = [e2(nc.sbuf_tensor(f"wun{i}", [128, C], BF16)) for i in range(2)]
                wdn = [e2(nc.sbuf_tensor(f"wdn{i}", [128, C], BF16)) for i in range(2)]
                wgT = [e2(nc.sbuf_tensor(f"wgT{i}", [128, C], BF16)) for i in range(2)]
                wuT = [e2(nc.sbuf_tensor(f"wuT{i}", [128, C], BF16)) for i in range(2)]
                tsil = e2(nc.sbuf_tensor("tsil", [128, C], BF16))
                hstg = [e2(nc.sbuf_tensor(f"hstg{i}", [128, C], BF16)) for i in range(2)]
                wdstg = [e2(nc.sbuf_tensor(f"wdstg{i}", [128, C], BF16)) for i in range(2)]
                psF = e2(nc.psum_tensor("psF", [128, 1024], F32))
                dstage2 = e2(nc.sbuf_tensor("dstage2", [128, T], F32)) if dbg_d else None

                WVAL("tensor", "v", v_v_done)
                WVAL("tensor", "g", g_ones)
                WVAL("tensor", "s", s_negc)

                st8 = {"exp_par": [0, 0], "piece": 0, "dn_t": 0, "dn_ev_s": 0, "aev2": 0,
                       "post_v": 0, "dn_dma": 0, "last_aev": 0}

                class Ffn:

                    def __init__(self):
                        self.units = [(hf, ht) for hf in (0, 1) for ht in range(NHT)]
                        self.ui = 0
                        self.sj = 0
                        self.u = {}
                        self.done_all = False
                        self.calls = 0
                        self.emitted = 0
                        self.rate = (5 * NHT + 4 * NHT) / 580.0

                    def dma_unit(self, i):
                        if i >= len(self.units):
                            return
                        hf, ht = self.units[i]
                        par = i % 2
                        prev = self.u.get(i - 2)
                        if prev is not None and "tr_last" in prev:
                            WVAL("gpsimd", "t", prev["tr_last"])
                        g2 = dma("gpsimd", wgn[par][:], wg_d[ht * 128:(ht + 1) * 128, :], "qw")
                        g2 = dma("gpsimd", wun[par][:], wu_d[ht * 128:(ht + 1) * 128, :], "qw")
                        if hf == 0:
                            g2 = dma("gpsimd", wdn[par][:],
                                     wd_d[:, ht * 128:(ht + 1) * 128].rearrange("(a p) c -> p a c", p=128), "qw")
                        self.u.setdefault(i, {})["dma"] = g2

                    def burst(self, budget):
                        for _ in range(budget):
                            self.step(force=True)

                    def step(self, force=False):
                        if self.done_all:
                            return
                        if not force:
                            self.calls += 1
                            if self.emitted >= self.calls * self.rate:
                                return
                        self.emitted += 1
                        i = self.ui
                        if i >= len(self.units):
                            self.done_all = True
                            return
                        hf, ht = self.units[i]
                        par = i % 2
                        u = self.u.setdefault(i, {})
                        prev = self.u.get(i - 1, {})
                        sj = self.sj
                        steps = ["TRG", "TRU", "TRD", "GATE", "UP"] if hf == 0 else ["TRG", "TRU", "GATE", "UP"]
                        name = steps[sj]
                        if name == "TRG":
                            if i == 0:
                                self.dma_unit(0)
                                self.dma_unit(1)
                            else:
                                self.dma_unit(i + 1)
                            WVAL("tensor", "qw", u["dma"])
                            if prev.get("hct"):
                                WVAL("tensor", "v", prev["hct"])
                            lt = None
                            for m in range(NM):
                                lt = op("tensor", lambda e, m=m, par=par: e.matmul(
                                    psF[:, m * 128:(m + 1) * 128], wgn[par][:, m * 128:(m + 1) * 128],
                                    ident_bf[:], start=True, stop=True),
                                    inc="t" if m == NM - 1 else None)
                            u["tr_last"] = lt
                            WVAL("vector", "t", lt)
                            u["evg"] = op("vector", lambda e, par=par: e.tensor_copy(wgT[par][:], psF[:]), inc="v")
                        elif name == "TRU":
                            WVAL("tensor", "v", u["evg"])
                            lt = None
                            for m in range(NM):
                                lt = op("tensor", lambda e, m=m, par=par: e.matmul(
                                    psF[:, m * 128:(m + 1) * 128], wun[par][:, m * 128:(m + 1) * 128],
                                    ident_bf[:], start=True, stop=True),
                                    inc="t" if m == NM - 1 else None)
                            u["tr_last"] = lt
                            WVAL("vector", "t", lt)
                            u["evu"] = op("vector", lambda e, par=par: e.tensor_copy(wuT[par][:], psF[:]), inc="v")
                        elif name == "TRD":
                            WVAL("tensor", "v", u["evu"])
                            lt = None
                            for m in range(NM):
                                lt = op("tensor", lambda e, m=m, par=par: e.matmul(
                                    psF[:, m * 128:(m + 1) * 128], wdn[par][:, m * 128:(m + 1) * 128],
                                    ident_bf[:], start=True, stop=True),
                                    inc="t" if m == NM - 1 else None)
                            u["tr_last"] = lt
                            WVAL("vector", "t", lt)
                            pp2 = self.u.get(i - 2, {})
                            if pp2.get("wdma"):
                                WVAL("vector", "qs", pp2["wdma"])
                            u["evd"] = op("vector", lambda e, par=par: e.tensor_copy(wdstg[par][:], psF[:]), inc="v")
                            WVAL("sync", "v", u["evd"])
                            u["wdma"] = dma("sync", wdt_d[ht * 128:(ht + 1) * 128, :], wdstg[par][:], "qs")
                        elif name == "GATE":
                            WVAL("tensor", "v", u["evd"] if hf == 0 else u["evu"])
                            lt = None
                            for mt in range(NM):
                                for c2 in range(2):
                                    lt = op("tensor", lambda e, mt=mt, c2=c2, par=par, hf=hf: e.matmul(
                                        psF[:, c2 * 512:(c2 + 1) * 512],
                                        wgT[par][:, mt * 128:(mt + 1) * 128],
                                        xnT[:, mt * T + hf * 1024 + c2 * 512: mt * T + hf * 1024 + (c2 + 1) * 512],
                                        start=(mt == 0), stop=(mt == NM - 1)),
                                        inc="t" if (mt == NM - 1 and c2 == 1) else None, reuse=(c2 == 1))
                            WVAL("scalar", "t", lt)
                            if prev.get("hct"):
                                WVAL("scalar", "v", prev["hct"])
                            u["silu"] = op("scalar", lambda e: e.activation(
                                tsil[:], psF[:], AF.Silu, scale=SCL), inc="s")
                        elif name == "UP":
                            WVAL("tensor", "s", u["silu"])
                            lt = None
                            for mt in range(NM):
                                for c2 in range(2):
                                    lt = op("tensor", lambda e, mt=mt, c2=c2, par=par, hf=hf: e.matmul(
                                        psF[:, c2 * 512:(c2 + 1) * 512],
                                        wuT[par][:, mt * 128:(mt + 1) * 128],
                                        xnT[:, mt * T + hf * 1024 + c2 * 512: mt * T + hf * 1024 + (c2 + 1) * 512],
                                        start=(mt == 0), stop=(mt == NM - 1)),
                                        inc="t" if (mt == NM - 1 and c2 == 1) else None, reuse=(c2 == 1))
                            WVAL("vector", "t", lt)
                            WVAL("vector", "s", u["silu"])
                            pp2 = self.u.get(i - 2, {})
                            if pp2.get("hdma"):
                                WVAL("vector", "qs", pp2["hdma"])
                            u["hct"] = op("vector", lambda e, par=par: e.tensor_mul(
                                hstg[par][:], psF[:], tsil[:]), inc="v")
                            WVAL("sync", "v", u["hct"])
                            u["hdma"] = dma("sync",
                                            hact_d[ht * 128:(ht + 1) * 128, hf * 1024:(hf + 1) * 1024],
                                            hstg[par][:], "qs")
                        self.sj += 1
                        if self.sj >= len(steps):
                            self.sj = 0
                            self.ui += 1

                ffn = Ffn()
                WVAL("gpsimd", "t", p1_t)
                WVAL("gpsimd", "v", p1_v)
                WVAL("gpsimd", "s", s_negc)

                def piece_of(panel, kt):
                    lo = max(512 * panel, 128 * kt)
                    return (lo, 512 * (panel + 1) - lo)

                def emit_heads2(hA, hB, slA, slB, Sp, ATp, prev_last_aev):
                    aA, aB = hA % 4, hB % 4
                    fA, fB = hA // 4, hB // 4
                    gA, gB = (hA % NH) * 65, (hB % NH) * 65
                    par_i = st8.setdefault("lane_pi", [0, 0])
                    exp_cnt = st8.setdefault("lane_exp", [{}, {}])
                    panel_av = [[0] * 4, [0] * 4]
                    aev_prev = prev_last_aev
                    for panel in range(4):
                        kts = list(range(4 * panel + 4))
                        for kx, kt in enumerate(kts):
                            pst, plen = piece_of(panel, kt)
                            has_mask = 128 * kt >= 512 * panel
                            pp = par_i[0] % 2
                            par_i[0] += 1
                            par_i[1] += 1
                            ci = par_i[0] - 1
                            WVAL("tensor", "s", exp_cnt[0].get(ci - 2, 0))
                            lt = None
                            for hx, (a_, f_) in enumerate(((aA, fA), (aB, fB))):
                                lt = op("tensor", lambda e, a_=a_, f_=f_, kt=kt, pst=pst, plen=plen, hx=hx, pp=pp: e.matmul(
                                    Sp[0][:, pp * 1024 + hx * 512: pp * 1024 + hx * 512 + plen],
                                    kT[32 * a_:32 * a_ + 32, f_ * T + kt * 128: f_ * T + (kt + 1) * 128],
                                    qT[32 * a_:32 * a_ + 32, f_ * T + pst: f_ * T + pst + plen],
                                    start=True, stop=True, tile_position=(32 * a_, 0)), inc="t")
                            if has_mask:
                                WVAL("vector", "t", lt)
                                vmk = op("vector", lambda e, pp=pp: e.tensor_tensor(
                                    Sp[0][:].rearrange("p (a b) -> p a b", a=4)[:, 2 * pp:2 * pp + 2, 0:128],
                                    Sp[0][:].rearrange("p (a b) -> p a b", a=4)[:, 2 * pp:2 * pp + 2, 0:128],
                                    mkb[:].unsqueeze(1).broadcast_to([128, 2, 128]), ALU.add), inc="v")
                                WVAL("scalar", "v", vmk)
                            WVAL("scalar", "t", lt)
                            ecount = op("scalar", lambda e, plen=plen, pp=pp, ci=ci: e.activation(
                                Pbuf[ci % 3][:].rearrange("p (a b) -> p a b", a=2)[:, :, 0:plen],
                                Sp[0][:].rearrange("p (a b) -> p a b", a=4)[:, 2 * pp:2 * pp + 2, 0:plen],
                                AF.Exp, bias=negC[:], scale=ISQ), inc="s")
                            exp_cnt[0][ci] = ecount
                            exp_cnt[1][ci] = ecount
                            if kx >= 2:
                                if kx == 2:
                                    WVAL("tensor", "v", aev_prev)
                                ktp = kts[kx - 2]
                                pstp, plenp = piece_of(panel, ktp)
                                for hx, (g_, sl) in enumerate(((gA, slA), (gB, slB))):
                                    ci = par_i[hx] - 1 - 2
                                    WVAL("tensor", "s", exp_cnt[hx][ci])
                                    op("tensor", lambda e, ktp=ktp, pstp=pstp, plenp=plenp, g_=g_, hx=hx, ci=ci, panel=panel: e.matmul(
                                        ATp[hx][0:65, pstp - 512 * panel: pstp - 512 * panel + plenp],
                                        vaug[ktp][:, g_:g_ + 65],
                                        Pbuf[ci % 3][:, hx * 512: hx * 512 + plenp],
                                        start=(ktp == 0), stop=(ktp == 4 * panel + 3)), inc="t")
                                ffn.step()
                        for ktp in kts[-2:]:
                            pstp, plenp = piece_of(panel, ktp)
                            for hx, (g_, sl) in enumerate(((gA, slA), (gB, slB))):
                                ci = par_i[hx] - 1 - (len(kts) - 1 - kts.index(ktp))
                                WVAL("tensor", "s", exp_cnt[hx][ci])
                                op("tensor", lambda e, ktp=ktp, pstp=pstp, plenp=plenp, g_=g_, hx=hx, ci=ci, panel=panel: e.matmul(
                                    ATp[hx][0:65, pstp - 512 * panel: pstp - 512 * panel + plenp],
                                    vaug[ktp][:, g_:g_ + 65],
                                    Pbuf[ci % 3][:, hx * 512: hx * 512 + plenp],
                                    start=(ktp == 0), stop=(ktp == 4 * panel + 3)), inc="t")
                        t_pv = now("t")
                        WVAL("vector", "t", t_pv)
                        WVAL("vector", "t", st8["dn_t"])
                        for hx, sl in enumerate((slA, slB)):
                            aev_prev = op("vector", lambda e, sl=sl, panel=panel, hx=hx: e.tensor_copy(
                                a_ev[sl][:, 512 * panel:512 * (panel + 1)], ATp[hx][:]), inc="v")
                            panel_av[hx][panel] = aev_prev
                        ffn.step()
                    st8["last_aev"] = aev_prev
                    return panel_av

                def emit_post_pair(pr, sl1, sl2):
                    for hv in range(2):
                        with ExitStack() as po_s:
                            psA = po_s.enter_context(nc.psum_tensor(f"psA{pr}{hv}", [128, 1024], F32))
                            psB = po_s.enter_context(nc.psum_tensor(f"psB{pr}{hv}", [128, 1024], F32))
                            tt = None
                            for qt in range(8):
                                qg = hv * 8 + qt
                                op("tensor", lambda e, qt=qt, qg=qg, psA=psA, sl1=sl1: e.matmul(
                                    psA[:, qt * 128: qt * 128 + 65], a_ev[sl1][:, qg * 128:(qg + 1) * 128],
                                    ident_bf[0:65, 0:65], start=True, stop=True))
                                tt = op("tensor", lambda e, qt=qt, qg=qg, psB=psB, sl2=sl2: e.matmul(
                                    psB[:, qt * 128: qt * 128 + 65], a_ev[sl2][:, qg * 128:(qg + 1) * 128],
                                    ident_bf[0:65, 0:65], start=True, stop=True),
                                    inc="t" if qt == 7 else None)
                            WVAL("vector", "t", tt)
                            pav = psA[:].rearrange("p (a b) -> p a b", a=8)
                            pbv = psB[:].rearrange("p (a b) -> p a b", a=8)
                            vtiny(lambda e, pav=pav: e.reciprocal(rab[:, 0:8], pav[:, :, 64]))
                            vtiny(lambda e, pbv=pbv: e.reciprocal(rab[:, 8:16], pbv[:, :, 64]))
                            vtiny(lambda e: e.tensor_scalar_mul(rab[:, 8:16], rab[:, 8:16], lam_col[:, 0:1]))
                            op("vector", lambda e, pbv=pbv: e.tensor_tensor(
                                t2f[:].rearrange("p (a b) -> p a b", a=8), pbv[:, :, 0:64],
                                rab[:, 8:16].unsqueeze(2).broadcast_to([128, 8, 64]), ALU.mult))
                            op("vector", lambda e, pav=pav: e.tensor_tensor(
                                t1f[:].rearrange("p (a b) -> p a b", a=8), pav[:, :, 0:64],
                                rab[:, 0:8].unsqueeze(2).broadcast_to([128, 8, 64]), ALU.mult))
                            op("vector", lambda e: e.tensor_sub(dff[:], t1f[:], t2f[:]))
                            op("vector", lambda e: e.tensor_mul(sqf[:], dff[:], dff[:]))
                            v_ms = op("vector", lambda e: e.reduce_sum(
                                rab[:, 16:24], sqf[:].rearrange("p (a b) -> p a b", a=8), axis=AX), inc="v")
                            WVAL("scalar", "v", v_ms)
                            stiny(lambda e: e.activation(rab[:, 24:32], rab[:, 16:24], AF.Ln, bias=eps_col, scale=1.0 / VD))
                            s_rstd = op("scalar", lambda e: e.activation(rab[:, 24:32], rab[:, 24:32], AF.Exp, scale=-0.5, bias=ln08_col), inc="s")
                            WVAL("vector", "s", s_rstd)
                            WVAL("vector", "t", st8["dn_t"])
                            v_dn = op("vector", lambda e, pr=pr, hv=hv: e.tensor_tensor(
                                dn2[:].rearrange("p (a j b) -> p a j b", a=NT, j=2)[:, hv * 8:(hv + 1) * 8, pr % 2, :],
                                dff[:].rearrange("p (a b) -> p a b", a=8),
                                rab[:, 24:32].unsqueeze(2).broadcast_to([128, 8, 64]), ALU.mult), inc="v")
                            st8["post_v"] = v_dn
                        ffn.burst(2)
                    return st8["post_v"]

                def emit_group(pp):
                    hA, hB = 2 * pp, 2 * pp + 1
                    with ExitStack() as pp_s:
                        Sp = [pp_s.enter_context(nc.psum_tensor(f"Sg{pp}0", [128, 2048], F32))]
                        ATp = [pp_s.enter_context(nc.psum_tensor(f"ATg{pp}{i}", [65, 512], F32)) for i in range(2)]
                        WVAL("tensor", "s", st8["dn_ev_s"])
                        WVAL("tensor", "v", st8["post_v"])
                        emit_heads2(hA, hB, 0, 1, Sp, ATp, st8["last_aev"])
                        emit_heads2(hA + NH, hB + NH, 2, 3, Sp, ATp, st8["last_aev"])
                    m = pp
                    v_dn = emit_post_pair(hA, 0, 2)
                    v_dn = emit_post_pair(hB, 1, 3)
                    with ExitStack() as pd_s:
                        psD = pd_s.enter_context(nc.psum_tensor(f"psD{pp}", [128, T], F32))
                        WVAL("tensor", "v", v_dn)
                        td = None
                        for qt in range(NT):
                            td = op("tensor", lambda e, qt=qt, psD=psD: e.matmul(
                                psD[:, qt * 128:(qt + 1) * 128], dn2[:, qt * 128:(qt + 1) * 128],
                                ident_bf[:], start=True, stop=True),
                                inc="t" if qt == 7 or qt == NT - 1 else None)
                            if qt == 7:
                                WVAL("scalar", "t", now("t"))
                                WVAL("scalar", "qs", st8["dn_dma"])
                                op("scalar", lambda e, psD=psD: e.copy(dnstg[:, 0:1024], psD[:, 0:1024]))
                        st8["dn_t"] = td
                        WVAL("scalar", "t", td)
                        st8["dn_ev_s"] = op("scalar", lambda e, psD=psD: e.copy(dnstg[:, 1024:2048], psD[:, 1024:2048]), inc="s")
                        WVAL("sync", "s", st8["dn_ev_s"])
                        st8["dn_dma"] = dma("sync", dnt_d[m * 128:(m + 1) * 128, :], dnstg[:], "qs")
                    ffn.burst(2)

                st8["last_aev"] = 0
                for pp_i in range(8):
                    emit_group(pp_i)
                    mark(f"pair{2 * pp_i + 1}")
                while not ffn.done_all:
                    ffn.step(force=True)
                dn_ev_s_final = st8["dn_ev_s"]
                dn_dma_final = st8["dn_dma"]
                last_hdma = ffn.u[len(ffn.units) - 1]["hdma"]
                last_wdma = ffn.u[NHT - 1]["wdma"]

                if "dnT" in dbg_d:
                    pass

        scope_x.close()
        p2_t, p2_v, p2_s = now("t"), now("v"), now("s")
        mark("p3pre")

        with ExitStack() as p345:
            e4 = p345.enter_context
            hsl = [e4(nc.sbuf_tensor(f"hsl{i}", [128, NHT * 128], BF16)) for i in range(2)]
            wdT = e4(nc.sbuf_tensor("wdT_all", [128, NHT * C], BF16))
            wp = [e4(nc.sbuf_tensor(f"wp{i}", [128, C], BF16)) for i in range(NM)]
            dsl = [e4(nc.sbuf_tensor(f"dsl{i}", [128, NM * 128], BF16)) for i in range(2)]
            xsp = [e4(nc.sbuf_tensor(f"xsp{i}", [128, C], F32)) for i in range(2)]
            opb = [e4(nc.sbuf_tensor(f"opb{i}", [128, C], F32)) for i in range(2)]
            tds = e4(nc.sbuf_tensor("tds", [128, C], F32))
            outb = [e4(nc.sbuf_tensor(f"outb{i}", [128, C], F32)) for i in range(2)]

            for engb in ("gpsimd", "sync"):
                WVAL(engb, "t", p2_t)
                WVAL(engb, "v", p2_v)
                WVAL(engb, "s", p2_s)
            wp_t = None
            for mt in range(NM):
                wp_t = dma("gpsimd", wp[mt][:], wproj_d[mt * 128:(mt + 1) * 128, :], "qw")
            WVAL("gpsimd", "qs", last_wdma)
            wdt_ld = dma("gpsimd", wdT[:].rearrange("p (a c) -> p a c", a=NHT), wdt_d.rearrange("(a p) c -> p a c", p=128), "qw")

            WVAL("tensor", "qw", wp_t)
            WVAL("tensor", "s", dn_ev_s_final)
            out_dma = {}
            comb_v = {}
            opb_v = {}
            dn_ld = {}
            with ExitStack() as pp3:
                ps_p = [pp3.enter_context(nc.psum_tensor(f"psp{i}", [128, 1024], F32)) for i in range(2)]
                ps_d = [pp3.enter_context(nc.psum_tensor(f"psd{i}", [128, 512], F32)) for i in range(4)]
                sevac = {}

                xld_t = {}

                hs_ld = {}

                def issue_loads(st):
                    par = st % 2
                    if st >= 2:
                        WVAL("sync", "v", opb_v[st - 2])
                        WVAL("sync", "s", sevac[st - 2])
                    WVAL("sync", "qs", dn_dma_final)
                    dn_ld[st] = dma("sync", dsl[par][:].rearrange("p (a c) -> p a c", a=NM),
                                    dnt_d[:, st * 128:(st + 1) * 128].rearrange("(a p) c -> p a c", p=128), "qx")
                    if st == 0:
                        WVAL("sync", "qs", last_hdma)
                    hs_ld[st] = dma("sync", hsl[par][:].rearrange("p (a c) -> p a c", a=NHT),
                                    hact_d[:, st * 128:(st + 1) * 128].rearrange("(a p) c -> p a c", p=128), "qx")
                    xld_t[st] = dma("sync", xsp[par][:], x_d[st * 128:(st + 1) * 128, :], "qx")

                def emit_tile(st):
                    par = st % 2
                    xld = xld_t[st]
                    if st >= 2:
                        WVAL("tensor", "v", opb_v[st - 2])
                    WVAL("tensor", "qx", xld)
                    lt = None
                    for mt in range(NM):
                        for c2 in range(2):
                            lt = op("tensor", lambda e, mt=mt, c2=c2, p=par: e.matmul(
                                ps_p[p][:, c2 * 512:(c2 + 1) * 512],
                                dsl[p][:, mt * 128:(mt + 1) * 128],
                                wp[mt][:, c2 * 512:(c2 + 1) * 512],
                                start=(mt == 0), stop=(mt == NM - 1)),
                                inc="t" if mt == NM - 1 and c2 == 1 else None, reuse=(c2 == 1))
                    WVAL("vector", "t", lt)
                    WVAL("vector", "qx", xld)
                    opb_v[st] = op("vector", lambda e, p=par: e.tensor_add(opb[p][:], ps_p[p][:], xsp[p][:]), inc="v")
                    if st >= 2:
                        WVAL("tensor", "s", sevac[st - 2])
                    if st == 0:
                        WVAL("tensor", "qw", wdt_ld)
                    WVAL("tensor", "qx", hs_ld[st])
                    lt = None
                    for ht in range(NHT):
                        for c2 in range(2):
                            lt = op("tensor", lambda e, ht=ht, st=st, c2=c2, dp=par: e.matmul(
                                ps_d[2 * dp + c2][:],
                                hsl[par][:, ht * 128:(ht + 1) * 128],
                                wdT[:, ht * C + c2 * 512: ht * C + (c2 + 1) * 512],
                                start=(ht == 0), stop=(ht == NHT - 1)),
                                inc="t" if (ht == NHT - 1 and c2 == 1) else None, reuse=(c2 == 1))
                    WVAL("scalar", "t", lt)
                    sv = None
                    for c2 in range(2):
                        sv = op("scalar", lambda e, c2=c2, dp=par: e.mul(
                            tds[:, c2 * 512:(c2 + 1) * 512], ps_d[2 * dp + c2][:], SCL * SCL),
                            inc="s" if c2 == 1 else None)
                    sevac[st] = sv
                    if st + 2 < NT:
                        issue_loads(st + 2)
                    WVAL("vector", "s", sevac[st])
                    if st >= 2:
                        WVAL("vector", "o", out_dma[st - 2])
                    comb_v[st] = op("vector", lambda e, p=par: e.tensor_add(outb[p][:], tds[:], opb[p][:]), inc="v")
                    WVAL("sync", "v", comb_v[st])
                    out_dma[st] = dma("sync", out_d[st * 128:(st + 1) * 128, :], outb[par][:], "o")

                issue_loads(0)
                issue_loads(1)
                for st in range(NT):
                    emit_tile(st)
                mark("half1")

            WVAL("sync", "o", now("o"))

        if stop is not None:
            cut = MARKS[stop]
            for k in Q:
                Q[k] = Q[k][:cut[k]]
        with nc.Block() as block:
            block.tensor(lambda e: [fn(e) for fn in Q["tensor"]])
            block.vector(lambda e: [fn(e) for fn in Q["vector"]])
            block.scalar(lambda e: [fn(e) for fn in Q["scalar"]])
            block.gpsimd(lambda e: [fn(e) for fn in Q["gpsimd"]])
            block.sync(lambda e: [fn(e) for fn in Q["sync"]])

    return nc


def _consts():
    ident = np.eye(128, dtype=np.float32)
    swap = np.zeros((128, 128), dtype=np.float32)
    for p in range(128):
        swap[p ^ 1, p] = 1.0
    tri = np.triu(np.ones((128, 128), dtype=np.float32))
    half = HD // 2
    j = (np.arange(128) % 32) // 2
    theta = (10000.0 ** (-(np.arange(half) * 2.0 / HD)))[j]
    ang = theta[:, None] * np.arange(T, dtype=np.float64)[None, :]
    cos = np.cos(ang).astype(np.float32)
    sin = np.sin(ang).astype(np.float32)
    sgn = np.where(np.arange(128) % 2 == 0, -1.0, 1.0).astype(np.float32)
    sinm = (sin * sgn[:, None]).astype(np.float32)
    onesr = np.ones((1, 128), dtype=np.float32)
    maskbig = (np.tril(np.ones((128, 128)), -1) * (-1e9)).astype(np.float32)
    ccols = np.zeros((128, 4), dtype=np.float32)
    ccols[:, 0] = EPS
    ccols[:, 1] = math.log(1.0 - LAM0)
    return dict(ident=ident, swapm=swap, tri=tri, cosr=cos, sinr=sinm, onesr=onesr,
                ccols=ccols, maskbig=maskbig)


def kernel(L0_attn_x, L0_attn_W_attn, L0_attn_W_proj, L0_attn_lambda_q1,
           L0_attn_lambda_k1, L0_attn_lambda_q2, L0_attn_lambda_k2,
           L0_ffn_w_gate, L0_ffn_w_up, L0_ffn_w_down, _debug=(), _nc=None):
    x = np.ascontiguousarray(np.asarray(L0_attn_x, dtype=np.float32))
    lam = np.concatenate([np.asarray(a, dtype=np.float32).reshape(-1) for a in
                          (L0_attn_lambda_q1, L0_attn_lambda_k1,
                           L0_attn_lambda_q2, L0_attn_lambda_k2)]).reshape(1, 4 * HD)
    shared = dict(
        wattn=np.ascontiguousarray(np.asarray(L0_attn_W_attn, dtype=np.float32)),
        wproj=np.ascontiguousarray(np.asarray(L0_attn_W_proj, dtype=np.float32)),
        lambdas=lam,
        wg=np.ascontiguousarray(np.asarray(L0_ffn_w_gate, dtype=np.int32)),
        wu=np.ascontiguousarray(np.asarray(L0_ffn_w_up, dtype=np.int32)),
        wd=np.ascontiguousarray(np.asarray(L0_ffn_w_down, dtype=np.int32)),
        **_consts(),
    )
    nc = _nc if _nc is not None else build_nc(debug=_debug)
    in_maps = []
    for c in range(8):
        m = dict(shared)
        m["x"] = np.ascontiguousarray(x[c].reshape(T, C))
        in_maps.append(m)
    res = run_bass_kernel_spmd(nc, in_maps, core_ids=list(range(8)))
    out = np.stack([np.asarray(r["out"], dtype=np.float32) for r in res.results])
    if _debug:
        return out.reshape(8, T, C), res
    return out.reshape(8, T, C)
